# version 12
# baseline (speedup 1.0000x reference)
import contextlib

import numpy as np
import ml_dtypes

import concourse.bass as bass
import concourse.mybir as mybir
from concourse.bass_utils import run_bass_kernel_spmd

F32 = mybir.dt.float32
BF16 = mybir.dt.bfloat16
AF = mybir.ActivationFunctionType
ALU = mybir.AluOpType
AX = mybir.AxisListType
BF = ml_dtypes.bfloat16

N_CORES = 8
T_SEQ = 2048
NT = T_SEQ // 128
D = 1024
NC_CH = D // 128
NE = 8
LR = 0.01
EPS = 1e-5

PHASES = "ABCD"
DEBUG_DUMP = None

COMPUTE_ENG = ("tensor", "vector", "scalar")
QUEUE_ENG = ("syncq", "gpq")


class Prog:

    def __init__(self):
        self.ops = []
        self.eng_ops = {e: [] for e in COMPUTE_ENG + QUEUE_ENG}
        self.last_w = {}
        self.readers = {}
        self.pad_fn = {}

    def _pad(self, eng):
        return self.emit(eng, self.pad_fn[eng], tag="hazpad")

    def emit(self, eng, fn, reads=(), writes=(), tag=""):
        if eng in self.pad_fn and self.eng_ops.get(eng):
            pw = self.ops[self.eng_ops[eng][-1]]["writes"]
            if any(b in pw for b in reads):
                self._pad(eng)
        rdeps = set()
        wdeps = set()
        for b in reads:
            w = self.last_w.get(b)
            if w is not None:
                rdeps.add(w)
        for b in writes:
            for r in self.readers.get(b, ()):
                wdeps.add(r)
            w = self.last_w.get(b)
            if w is not None:
                rdeps.add(w)
        rdeps = {d for d in rdeps if self.ops[d]["eng"] != eng}
        wdeps = {d for d in wdeps if self.ops[d]["eng"] != eng}
        deps = set(wdeps)
        for d in rdeps:
            te = self.ops[d]["eng"]
            if te in self.pad_fn and not self.ops[d]["dma"]:
                lst = self.eng_ops[te]
                k = self.ops[d]["idx_on_eng"]
                if k + 1 >= len(lst):
                    self._pad(te)
                    lst = self.eng_ops[te]
                deps.add(lst[k + 1])
            else:
                deps.add(d)
        opid = len(self.ops)
        self.ops.append(dict(eng=eng, fn=fn, deps=deps, dma=eng in QUEUE_ENG,
                             tag=tag, writes=tuple(writes),
                             idx_on_eng=len(self.eng_ops[eng])))
        self.eng_ops[eng].append(opid)
        for b in reads:
            self.readers.setdefault(b, []).append(opid)
        for b in writes:
            self.readers[b] = [opid]
            self.last_w[b] = opid
        return opid

    def finalize(self, nc):
        marked = [op["dma"] for op in self.ops]
        for op in self.ops:
            for d in op["deps"]:
                marked[d] = True
        val = [0] * len(self.ops)
        for eng in COMPUTE_ENG + QUEUE_ENG:
            c = 0
            for i in self.eng_ops[eng]:
                if marked[i]:
                    c += 16 if self.ops[i]["dma"] else 1
                val[i] = c
        totals = {eng: (val[self.eng_ops[eng][-1]] if self.eng_ops[eng] else 0)
                  for eng in COMPUTE_ENG + QUEUE_ENG}

        with contextlib.ExitStack() as es:
            sems = {e: es.enter_context(nc.semaphore("sem_" + e))
                    for e in COMPUTE_ENG + QUEUE_ENG}
            block = es.enter_context(nc.Block())

            def replay(eng_name, eng):
                last_wait = {}
                for i in self.eng_ops[eng_name]:
                    op = self.ops[i]
                    for d in sorted(op["deps"]):
                        te = self.ops[d]["eng"]
                        v = val[d]
                        if last_wait.get(te, 0) < v:
                            eng.wait_ge(sems[te], v)
                            last_wait[te] = v
                    inst = op["fn"](eng)
                    if marked[i]:
                        inst.then_inc(sems[eng_name], 16 if op["dma"] else 1)

            @block.tensor
            def _(eng):
                replay("tensor", eng)

            @block.vector
            def _(eng):
                replay("vector", eng)

            @block.scalar
            def _(eng):
                replay("scalar", eng)

            @block.sync
            def _(eng):
                replay("syncq", eng)
                if totals["syncq"]:
                    eng.wait_ge(sems["syncq"], totals["syncq"])

            @block.gpsimd
            def _(eng):
                replay("gpq", eng)
                if totals["gpq"]:
                    eng.wait_ge(sems["gpq"], totals["gpq"])


def build_program():
    nc = bass.Bass("TRN2", target_bir_lowering=False)
    p = Prog()

    def din(name, shape, dt=F32):
        return nc.dram_tensor(name, list(shape), dt, kind="ExternalInput").ap()

    x_d = din("x", (2 * T_SEQ, D))
    y_d = nc.dram_tensor("y", [2 * T_SEQ, D], F32, kind="ExternalOutput").ap()
    lntd_d = nc.dram_tensor("lntd_hbm", [1, D], F32, kind="Internal").ap()
    wk0_d = din("wk0t", (D, D))
    wv0_d = din("wv0t", (D, D))
    wr0_d = din("wr0t", (D, D))
    wo0_d = din("wo0t", (D, D))
    rt0_d = din("rt0", (D, NE))
    et0_d = din("et0", (NE * D, D))
    wkq1_d = din("wkq1t", (D, 128))
    wv1_d = din("wv1t", (D, D))
    wr2_d = din("wr2t", (D, D))
    wk2_d = din("wk2t", (D, D))
    wv2_d = din("wv2t", (D, D))
    vecs_d = din("vecs", (19, D))
    idbf_d = din("idbf", (128, 128), BF16)
    usbf_d = din("usbf", (128, 128), BF16)
    onescol_d = din("onescol", (128, 1), BF16)
    onesrow_d = din("onesrow", (65, 128), BF16)
    onesrow_f_d = din("onesrow_f", (1, 128))
    iota1_d = din("iota1", (1, 128))
    mubd_d = din("mubd", (128, 128), BF16)
    mlbd_d = din("mlbd", (128, 128), BF16)
    mlbs_d = din("mlbs", (128, 128), BF16)
    mus_d = din("mus_f", (128, 128), BF16)
    if32_d = din("if32", (128, 128))
    one11_d = din("one11_f", (1, 1))

    es = contextlib.ExitStack()
    sb = {}

    def S(name, shape, dt):
        sb[name] = es.enter_context(nc.sbuf_tensor(name, list(shape), dt))
        return sb[name]

    with es:
        hzv0 = S("hzv0", (128, 1), F32)
        hzv1 = S("hzv1", (128, 1), F32)
        hzs0 = S("hzs0", (128, 1), F32)
        hzs1 = S("hzs1", (128, 1), F32)
        p.pad_fn["vector"] = lambda e: e.tensor_copy(hzv1[:, :], hzv0[:, :])
        p.pad_fn["scalar"] = lambda e: e.copy(hzs1[:, :], hzs0[:, :])
        res = [S(f"res{t}", (128, D), F32) for t in range(NT)]
        wA = [S(f"wA{i}", (128, NC_CH * D), BF16) for i in range(4)]
        wB = [S(f"wB{i}", (128, 4 * D), BF16) for i in range(3)]
        wkq_sb = S("wkq_sb", (128, NC_CH * 128), BF16)
        rt_sb = S("rt_sb", (128, NC_CH * NE), BF16)
        idbf = S("idbf_s", (128, 128), BF16)
        usbf = S("usbf_s", (128, 128), BF16)
        onescol = S("onescol_s", (128, 1), BF16)
        onesrow = S("onesrow_s", (65, 128), BF16)
        onesrow_f = S("onesrow_f_s", (1, 128), F32)
        iota1 = S("iota1_s", (1, 128), F32)
        mubd = S("mubd_s", (128, 128), BF16)
        mlbd = S("mlbd_s", (128, 128), BF16)
        mlbs = S("mlbs_s", (128, 128), BF16)
        mus = S("mus_s", (128, 128), BF16)
        if32 = S("if32_s", (128, 128), F32)
        one11 = S("one11_s", (1, 1), F32)
        tf_rep = S("tf_rep", (128, D), BF16)
        td128_rep = S("td128_rep", (128, D), F32)
        cols = S("cols", (128, 48), F32)
        p2t = S("p2t", (128, D), F32)
        carry_f = S("carry_f", (1, D), F32)
        rows_bf = S("rows_bf", (65, D), BF16)
        carry_bf = rows_bf[64:65, :]
        negbv = rows_bf[32:33, :]
        bkq_r = S("bkq_r", (1, 128), BF16)
        rb_r = S("rb_r", (1, NE), BF16)
        bnbuf = S("bnbuf", (128, 12), F32)
        bnbuf2 = S("bnbuf2", (128, 6), F32)
        bnagg = [S("bnagg0", (128, 2), F32), S("bnagg1", (128, 2), F32)]
        stmu = S("stmu", (128, NT), F32)
        stvar = S("stvar", (128, NT), F32)
        strs = S("strs", (128, NT), F32)
        strr = S("strr", (128, NT), F32)
        stt1 = S("stt1", (128, NT), F32)
        stt2 = S("stt2", (128, NT), F32)
        stnm1 = S("stnm1", (128, NT), F32)
        stnm2 = S("stnm2", (128, NT), F32)
        sf = [S(f"sf{i}", (128, D), F32) for i in range(2)]
        sbf = [S(f"sbf{i}", (128, D), BF16) for i in range(6)]
        w0t_f = S("w0t_f", (64, D), F32)
        lntd = p2t[0:1, :]
        w0t_bf = S("w0t_bf", (64, D), BF16)
        pt_bfp = [S(f"pt_bf{i}", (128, 128), BF16) for i in range(2)]
        qt_bfp = [S(f"qt_bf{i}", (128, 128), BF16) for i in range(2)]
        gt_bfp = [S(f"gt_bf{i}", (128, 128), BF16) for i in range(2)]
        k_bfp = [S(f"k_bf{i}", (64, 128), BF16) for i in range(2)]
        q_bfp = [S(f"q_bf{i}", (64, 128), BF16) for i in range(2)]
        ktm_bf = S("ktm_bf", (128, 64), BF16)
        cX = S("cX", (128, 128), BF16)
        cY = S("cY", (128, 128), BF16)
        cIY = S("cIY", (128, 128), BF16)
        cRa = S("cRa", (128, 128), BF16)
        cRb = S("cRb", (128, 128), BF16)
        cSb = S("cSb", (128, 128), BF16)
        lg = S("lg", (128, NE), F32)
        rmx = S("rmx", (128, 1), F32)
        sme = S("sme", (128, 1), F32)
        wnv = S("wnv", (128, 1), F32)
        nty = S("nty", (128, 1), F32)
        eqc = S("eqc", (128, 1), F32)
        mcc = S("mcc", (128, 1), F32)
        mw_f = S("mw_f", (128, NT * NE), F32)
        mwt_bf = S("mwt_bf", (NE, 128), BF16)

        pT = es.enter_context(nc.psum_tensor("pT", [128, 1024], BF16))
        pA0 = es.enter_context(nc.psum_tensor("pA0", [128, 512], F32))
        pA1 = es.enter_context(nc.psum_tensor("pA1", [128, 512], F32))
        pB0 = es.enter_context(nc.psum_tensor("pB0", [128, 512], F32))
        pB1 = es.enter_context(nc.psum_tensor("pB1", [128, 512], F32))
        pC0 = es.enter_context(nc.psum_tensor("pC0", [128, 512], F32))
        pC1 = es.enter_context(nc.psum_tensor("pC1", [128, 512], F32))
        pT2 = es.enter_context(nc.psum_tensor("pT2", [128, 1024], BF16))

        def E(eng, fn, reads=(), writes=(), tag=""):
            return p.emit(eng, fn, reads=reads, writes=writes, tag=tag)

        def dma(q, out_ap, in_ap, reads=(), writes=(), tag=""):
            return E(q, lambda e, o=out_ap, i=in_ap: e.dma_start(out=o, in_=i),
                     reads=reads, writes=writes, tag=tag)

        def mm(out_ap, lhsT_ap, rhs_ap, start, stop, reads, writes, tag=""):
            return E("tensor",
                     lambda e, o=out_ap, l=lhsT_ap, r=rhs_ap, s1=start, s2=stop:
                     e.matmul(o, l, r, start=s1, stop=s2),
                     reads=reads, writes=writes, tag=tag)

        def tr(out_ap, in_ap, id_ap, reads, writes, tag=""):
            return E("tensor",
                     lambda e, o=out_ap, i=in_ap, d=id_ap: e.transpose(o, i, d),
                     reads=reads, writes=writes, tag=tag)

        def vop(eng, meth, *aps, reads=(), writes=(), tag="", **kw):
            def fn(e, m=meth, a=aps, k=dict(kw)):
                return getattr(e, m)(*a, **k)
            return E(eng, fn, reads=reads, writes=writes, tag=tag)

        dma("syncq", idbf[:, :], idbf_d[:, :], writes=["idbf_s"])
        dma("syncq", usbf[:, :], usbf_d[:, :], writes=["usbf_s"])
        dma("syncq", onescol[:, :], onescol_d[:, :], writes=["onescol_s"])
        dma("syncq", onesrow[:, :], onesrow_d[:, :], writes=["onesrow_s"])
        dma("syncq", onesrow_f[:, :], onesrow_f_d[:, :], writes=["onesrow_f_s"])
        dma("syncq", iota1[:, :], iota1_d[:, :], writes=["iota1_s"])
        dma("syncq", mubd[:, :], mubd_d[:, :], writes=["mubd_s"])
        dma("syncq", mlbd[:, :], mlbd_d[:, :], writes=["mlbd_s"])
        dma("syncq", mlbs[:, :], mlbs_d[:, :], writes=["mlbs_s"])
        dma("syncq", mus[:, :], mus_d[:, :], writes=["mus_s"])
        dma("syncq", if32[:, :], if32_d[:, :], writes=["if32_s"])
        dma("syncq", one11[:, :], one11_d[:, :], writes=["one11_s"])
        def load_w1024(dst_name, dst, src_ap, tag, extra=()):
            src3 = src_ap.rearrange("(c p) d -> p c d", p=128)
            dst3 = dst[:, :].rearrange("p (c d) -> p c d", d=D)
            for h in range(2):
                dma("gpq", dst3[:, h * 4:(h + 1) * 4, :],
                    src3[:, h * 4:(h + 1) * 4, :],
                    writes=[dst_name, *extra], tag=tag)

        load_w1024("wA0", wA[0], wk0_d, "wk0")
        load_w1024("wA1", wA[1], wv0_d, "wv0")
        load_w1024("wA2", wA[2], wr0_d, "wr0")
        load_w1024("wA3", wA[3], wo0_d, "wo0")
        dma("gpq", rt_sb[:, :].rearrange("p (c e) -> p c e", e=NE),
            rt0_d.rearrange("(c p) e -> p c e", p=128), writes=["rt_sb"])
        dma("gpq", rows_bf[0:8, :], vecs_d[5:13, :], writes=["rows_bf"])
        dma("gpq", wkq_sb[:, :].rearrange("p (c s) -> p c s", s=128),
            wkq1_d.rearrange("(c p) s -> p c s", p=128), writes=["wkq_sb"])
        dma("gpq", bkq_r[:, :], vecs_d[13:14, 0:128], writes=["bkq_r"])
        dma("gpq", rb_r[:, :], vecs_d[4:5, 0:NE], writes=["rb_r"])

        srow = sf[0][0:1, :]
        srow2 = sf[1][0:1, :]
        dma("syncq", srow, vecs_d[2:3, :], writes=["sf0"])
        vop("scalar", "activation", srow, srow, AF.Sigmoid,
            reads=["sf0"], writes=["sf0"], tag="td")
        vop("scalar", "activation", lntd[:, :], srow, AF.Ln,
            reads=["sf0"], writes=["p2t"], tag="lntd")
        dma("syncq", lntd_d[:, :], lntd[:, :], reads=["p2t"], tag="lntd_sv")
        for _ in range(7):
            vop("vector", "tensor_mul", srow, srow, srow,
                reads=["sf0"], writes=["sf0"], tag="tdsq")
        for h in range(2):
            pp = (pA0, pA1)[h]
            mm(pp[:, :], onesrow_f[:, :], srow[:, h * 512:(h + 1) * 512], True, True,
               reads=["onesrow_f_s", "sf0"], writes=[("pA0", "pA1")[h]], tag="tdrep")
            vop("vector", "tensor_copy", td128_rep[:, h * 512:(h + 1) * 512], pp[:, :],
                reads=[("pA0", "pA1")[h]], writes=["td128_rep"])
        dma("syncq", srow, vecs_d[3:4, :], writes=["sf0"])
        vop("scalar", "activation", srow, srow, AF.Sigmoid,
            reads=["sf0"], writes=["sf0"], tag="tf")
        for h in range(2):
            pp = (pA0, pA1)[h]
            mm(pp[:, :], onesrow_f[:, :], srow[:, h * 512:(h + 1) * 512], True, True,
               reads=["onesrow_f_s", "sf0"], writes=[("pA0", "pA1")[h]], tag="tfrep")
            vop("vector", "tensor_copy", tf_rep[:, h * 512:(h + 1) * 512], pp[:, :],
                reads=[("pA0", "pA1")[h]], writes=["tf_rep"])
        for i, row in enumerate((0, 1, 15, 16, 17, 18)):
            dma("syncq", srow, vecs_d[row:row + 1, :], writes=["sf0"], tag=f"vrow{row}")
            for c in range(NC_CH):
                mm(pA0[:, i * 8 + c:i * 8 + c + 1],
                   srow[:, c * 128:(c + 1) * 128], one11[:, :], True, True,
                   reads=["sf0", "one11_s"], writes=["pA0"], tag="colify")
        vop("vector", "tensor_copy", cols[:, :], pA0[:, 0:48],
            reads=["pA0"], writes=["cols"], tag="cols_evac")
        g0c, b0c = cols[:, 0:8], cols[:, 8:16]
        g1c, b1c = cols[:, 16:24], cols[:, 24:32]
        murc, mukc = cols[:, 32:40], cols[:, 40:48]
        dma("syncq", srow, vecs_d[14:15, :], writes=["sf0"])
        vop("scalar", "mul", negbv, srow, -1.0, reads=["sf0"],
            writes=["rows_bf"], tag="rows_bf")

        for t in range(NT):
            dma("syncq", res[t][:, :], x_d[t * 128:(t + 1) * 128, :],
                writes=[f"res{t}"], tag=f"x0_{t}")


        stats_done = {}

        stats_pending = {}
        stats_cnt = [0]

        def stats_flush(tag):
            if tag in stats_pending:
                t, par = stats_pending.pop(tag)
                ag, agn = bnagg[par], f"bnagg{par}"
                vop("vector", "tensor_scalar", stmu[:, t:t + 1], ag[:, 0:1],
                    1.0, None, ALU.mult, reads=[agn], writes=["stmu"],
                    tag=f"{tag}bnm{t}")
                vop("vector", "tensor_scalar", stvar[:, t:t + 1], ag[:, 1:2],
                    1.0, None, ALU.mult, reads=[agn], writes=["stvar"],
                    tag=f"{tag}bnv{t}")

        def stats_tile(tag, t):
            stats_done.setdefault(tag, set()).add(t)
            par = stats_cnt[0] % 2
            stats_cnt[0] += 1
            vop("vector", "bn_stats", bnbuf[:, 0:6], res[t][:, 0:512],
                reads=[f"res{t}"], writes=["bnbuf"], tag=f"{tag}bn{t}")
            vop("vector", "bn_stats", bnbuf2[:, 0:6], res[t][:, 512:1024],
                reads=[f"res{t}"], writes=["bnbuf2"], tag=f"{tag}bn{t}")
            vop("vector", "tensor_copy", bnbuf[:, 6:12], bnbuf2[:, 0:6],
                reads=["bnbuf2"], writes=["bnbuf"], tag=f"{tag}bnc{t}")
            vop("vector", "bn_aggr", bnagg[par][:, :], bnbuf[:, 0:12],
                reads=["bnbuf"], writes=[f"bnagg{par}"], tag=f"{tag}bna{t}")
            stats_flush(tag)
            stats_pending[tag] = (t, par)

        def stats(tag, inner):
            for t in range(NT):
                if t not in stats_done.get(tag, set()):
                    stats_tile(tag, t)
            p._pad("vector")
            p._pad("vector")
            p._pad("vector")
            stats_flush(tag)
            stats_done.pop(tag, None)
            vop("vector", "tensor_scalar", stt1[:, :], stvar[:, :], EPS, None,
                ALU.add, reads=["stvar"], writes=["stt1"])
            vop("scalar", "activation", stt2[:, :], stt1[:, :], AF.Ln,
                reads=["stt1"], writes=["stt2"], tag=f"{tag}ln")
            vop("scalar", "activation", strs[:, :], stt2[:, :], AF.Exp,
                scale=-0.5, reads=["stt2"], writes=["strs"], tag=f"{tag}rs")
            if inner:
                vop("scalar", "activation", stt1[:, :], stt2[:, :], AF.Exp,
                    scale=-1.0, reads=["stt2"], writes=["stt1"], tag=f"{tag}ia")
                vop("vector", "tensor_mul", stt1[:, :], stvar[:, :], stt1[:, :],
                    reads=["stvar", "stt1"], writes=["stt1"])
                vop("vector", "tensor_scalar", stt1[:, :], stt1[:, :], EPS, None,
                    ALU.add, reads=["stt1"], writes=["stt1"])
                vop("scalar", "activation", stt2[:, :], stt1[:, :], AF.Ln,
                    reads=["stt1"], writes=["stt2"], tag=f"{tag}ln2")
                vop("scalar", "activation", stt1[:, :], stt2[:, :], AF.Exp,
                    scale=-0.5, reads=["stt2"], writes=["stt1"], tag=f"{tag}rs2")
                vop("vector", "tensor_mul", strr[:, :], strs[:, :], stt1[:, :],
                    reads=["strs", "stt1"], writes=["strr"])
            vop("vector", "tensor_mul", stnm1[:, :], stmu[:, :], strs[:, :],
                reads=["stmu", "strs"], writes=["stnm1"])
            vop("vector", "tensor_scalar", stnm1[:, :], stnm1[:, :], -1.0, None,
                ALU.mult, reads=["stnm1"], writes=["stnm1"])
            if inner:
                vop("vector", "tensor_mul", stnm2[:, :], stmu[:, :], strr[:, :],
                    reads=["stmu", "strr"], writes=["stnm2"])
                vop("vector", "tensor_scalar", stnm2[:, :], stnm2[:, :], -1.0, None,
                    ALU.mult, reads=["stnm2"], writes=["stnm2"])

        def transpose_tile(src_bf, src_name, dst_bf, dst_name, gcol=None, bcol=None,
                           dst_base=0, tag=""):
            for c in range(NC_CH):
                tr(pT[:, c * 128:(c + 1) * 128], src_bf[:, c * 128:(c + 1) * 128],
                   idbf[:, :], reads=[src_name, "idbf_s"], writes=["pT"], tag=tag)
            if gcol is None:
                vop("vector", "tensor_copy", dst_bf[:, dst_base:dst_base + 1024],
                    pT[:, :], reads=["pT"], writes=[dst_name], tag=tag)
            else:
                for c in range(NC_CH):
                    vop("vector", "tensor_scalar",
                        dst_bf[:, dst_base + c * 128:dst_base + (c + 1) * 128],
                        pT[:, c * 128:(c + 1) * 128],
                        gcol[:, c:c + 1], bcol[:, c:c + 1], ALU.mult, ALU.add,
                        reads=["pT", "cols"], writes=[dst_name], tag=tag)

        def proj1024(fm_bf, fm_name, w_sb, w_name, pair, pair_names, extra=(),
                     tag=""):
            for h in range(2):
                for c in range(NC_CH):
                    mm(pair[h][:, :], fm_bf[:, c * 128:(c + 1) * 128],
                       w_sb[:, c * D + h * 512:c * D + (h + 1) * 512],
                       start=(c == 0), stop=(c == NC_CH - 1 and not extra),
                       reads=[fm_name, w_name], writes=[pair_names[h]], tag=tag)
                for j, (lh, rh, rnames) in enumerate(extra):
                    mm(pair[h][:, :], lh, rh[h], start=False,
                       stop=(j == len(extra) - 1),
                       reads=list(rnames), writes=[pair_names[h]], tag=tag + "x")

        def phase_A(s):
            def tap(name, t, ap, reads):
                if DEBUG_DUMP == name and s == 0:
                    dma("gpq", y_d[t * 128:(t + 1) * 128, :], ap, reads=reads,
                        tag=f"tap{name}{t}")
            stats("A", inner=True)
            if DEBUG_DUMP == "stats" and s == 0:
                dma("gpq", y_d[0:128, 0:NT], stmu[:, :], reads=["stmu"], tag="tapst0")
                dma("gpq", y_d[128:256, 0:NT], stvar[:, :], reads=["stvar"], tag="tapst1")
                dma("gpq", y_d[256:384, 0:NT], strs[:, :], reads=["strs"], tag="tapst2")
                dma("gpq", y_d[384:512, 0:NT], strr[:, :], reads=["strr"], tag="tapst3")
            vop("vector", "memset", carry_f[:, :], 0.0, writes=["carry_f"], tag="Ainit")
            vop("vector", "memset", carry_bf, 0.0, writes=["rows_bf"], tag="Ainit")
            for h in range(2):
                pp, ppn = (pA0, pA1)[h], ("pA0", "pA1")[h]
                mm(pp[:, :], iota1[:, :], lntd[:, h * 512:(h + 1) * 512],
                   True, True, reads=["iota1_s", "p2t"], writes=[ppn],
                   tag=f"A{s}p2o")
                vop("scalar", "activation", p2t[:, h * 512:(h + 1) * 512],
                    pp[:, :], AF.Exp, reads=[ppn], writes=["p2t"], tag=f"A{s}p2e")
            def stageA1(t):
                rn = f"res{t}"
                par = t % 2
                hfm, hfm_n = (sbf[1], "sbf1") if par == 0 else (sbf[4], "sbf4")
                vop("scalar", "activation", sbf[0][:, :], res[t][:, :],
                    AF.Identity, bias=stnm2[:, t:t + 1], scale=strr[:, t:t + 1],
                    reads=[rn, "strr", "stnm2"], writes=["sbf0"], tag=f"A{t}h")
                for c in range(NC_CH):
                    tr(pT[:, c * 128:(c + 1) * 128], sbf[0][:, c * 128:(c + 1) * 128],
                       idbf[:, :], reads=["sbf0", "idbf_s"], writes=["pT"],
                       tag=f"A{t}tr")
                for c in range(NC_CH):
                    vop("scalar", "activation", hfm[:, c * 128:(c + 1) * 128],
                        pT[:, c * 128:(c + 1) * 128], AF.Identity,
                        bias=b0c[:, c:c + 1], scale=g0c[:, c:c + 1],
                        reads=["pT", "cols"], writes=[hfm_n], tag=f"A{t}ge")

            def stageA2a(t):
                rn = f"res{t}"
                par = t % 2
                hfm, hfm_n = (sbf[1], "sbf1") if par == 0 else (sbf[4], "sbf4")
                gbuf, gn = (sbf[3], "sbf3") if par == 0 else (sbf[5], "sbf5")
                proj1024(hfm, hfm_n, wA[0], "wA0", (pA0, pA1), ("pA0", "pA1"),
                         tag=f"A{t}k")
                for h in range(2):
                    vop("scalar", "copy", sf[0][:, h * 512:(h + 1) * 512],
                        (pA0, pA1)[h][:, :], reads=[("pA0", "pA1")[h]],
                        writes=["sf0"], tag=f"A{t}ke")
                proj1024(hfm, hfm_n, wA[1], "wA1", (pB0, pB1), ("pB0", "pB1"),
                         tag=f"A{t}v")
                for h in range(2):
                    vop("vector", "tensor_mul", sf[1][:, h * 512:(h + 1) * 512],
                        sf[0][:, h * 512:(h + 1) * 512], (pB0, pB1)[h][:, :],
                        reads=["sf0", ("pB0", "pB1")[h]], writes=["sf1"],
                        tag=f"A{t}kv")
                proj1024(hfm, hfm_n, wA[2], "wA2", (pA0, pA1), ("pA0", "pA1"),
                         tag=f"A{t}r")
                for h in range(2):
                    vop("scalar", "activation", sf[0][:, h * 512:(h + 1) * 512],
                        (pA0, pA1)[h][:, :], AF.Sigmoid,
                        reads=[("pA0", "pA1")[h]], writes=["sf0"], tag=f"A{t}sr")
                vop("vector", "tensor_mul", sbf[2][:, :], sf[1][:, :], p2t[:, :],
                    reads=["sf1", "p2t"], writes=["sbf2"], tag=f"A{t}w2")
                vop("vector", "tensor_mul", sf[1][:, :], sf[1][:, :], tf_rep[:, :],
                    reads=["sf1", "tf_rep"], writes=["sf1"], tag=f"A{t}t3")
                for h in range(2):
                    ph, phn = (pC0, pC1)[h], ("pC0", "pC1")[h]
                    mm(ph[:, :], usbf[:, :], sbf[2][:, h * 512:(h + 1) * 512],
                       True, False, reads=["usbf_s", "sbf2"], writes=[phn],
                       tag=f"A{t}cs")
                    mm(ph[:, :], onesrow[64:65, :],
                       carry_bf[:, h * 512:(h + 1) * 512],
                       False, True, reads=["onesrow_s", "rows_bf"], writes=[phn],
                       tag=f"A{t}cs")
                for h in range(2):
                    mm(pB0[0:1, :], onescol[:, :], sbf[2][:, h * 512:(h + 1) * 512],
                       True, True, reads=["onescol_s", "sbf2"], writes=["pB0"],
                       tag=f"A{t}col")
                    vop("vector", "tensor_add", carry_f[:, h * 512:(h + 1) * 512],
                        carry_f[:, h * 512:(h + 1) * 512], pB0[0:1, :],
                        reads=["carry_f", "pB0"], writes=["carry_f"], tag=f"A{t}col")
                vop("scalar", "copy", carry_bf, carry_f[:, :],
                    reads=["carry_f"], writes=["rows_bf"], tag=f"A{t}cbf")
                for h in range(2):
                    vop("vector", "tensor_add", sf[1][:, h * 512:(h + 1) * 512],
                        sf[1][:, h * 512:(h + 1) * 512], (pC0, pC1)[h][:, :],
                        reads=["sf1", ("pC0", "pC1")[h]], writes=["sf1"],
                        tag=f"A{t}wkv")
                vop("vector", "tensor_mul", gbuf[:, :], sf[1][:, :], sf[0][:, :],
                    reads=["sf1", "sf0"], writes=[gn], tag=f"A{t}g")
                vop("vector", "tensor_mul", p2t[:, :], p2t[:, :], td128_rep[:, :],
                    reads=["p2t", "td128_rep"], writes=["p2t"], tag=f"A{t}p2")

            def stageA2b(t):
                rn = f"res{t}"
                par = t % 2
                gbuf, gn = (sbf[3], "sbf3") if par == 0 else (sbf[5], "sbf5")
                for c in range(NC_CH):
                    tr(pT2[:, c * 128:(c + 1) * 128],
                       gbuf[:, c * 128:(c + 1) * 128], idbf[:, :],
                       reads=[gn, "idbf_s"], writes=["pT2"], tag=f"A{t}gtr")
                vop("scalar", "copy", sbf[0][:, :], pT2[:, :],
                    reads=["pT2"], writes=["sbf0"], tag=f"A{t}gfe")
                proj1024(sbf[0], "sbf0", wA[3], "wA3", (pA0, pA1), ("pA0", "pA1"),
                         tag=f"A{t}o")
                for h in range(2):
                    vop("vector", "tensor_add", res[t][:, h * 512:(h + 1) * 512],
                        res[t][:, h * 512:(h + 1) * 512], (pA0, pA1)[h][:, :],
                        reads=[rn, ("pA0", "pA1")[h]], writes=[rn], tag=f"A{t}res")
                stats_tile("B", t)

            stageA1(0)
            stageA1(1)
            stageA2a(0)
            for t in range(NT):
                if t + 2 < NT:
                    stageA1(t + 2)
                if t + 1 < NT:
                    stageA2a(t + 1)
                stageA2b(t)

        def phase_B(s):
            def tapB(name, t, ap, reads):
                if DEBUG_DUMP == name and s == 0:
                    dma("gpq", y_d[t * 128:(t + 1) * 128, 0:8], ap, reads=reads,
                        tag=f"tapB{name}{t}")
            def load_half(H):
                e, j = H // 2, H % 2
                slot = H % 3
                src3 = et0_d[e * D:(e + 1) * D, :].rearrange("(c p) d -> p c d",
                                                             p=128)
                dma("gpq", wB[slot][:, :].rearrange("p (c d) -> p c d", d=D),
                    src3[:, j * 4:(j + 1) * 4, :],
                    writes=[f"wB{slot}"], tag=f"B_e{e}h{j}")

            for H in range(3):
                load_half(H)
            if DEBUG_DUMP == "wb0" and s == 0:
                vop("vector", "tensor_copy", sf[1][:, :], wB[0][:, 0:1024],
                    reads=["wB0"], writes=["sf1"], tag="tapwb0")
                dma("syncq", y_d[0:128, :], sf[1][:, :],
                    reads=["sf1"], tag="tapwb0d")
            stats("B", inner=False)

            def stageB1(t):
                rn = f"res{t}"
                vop("scalar", "activation", sbf[0][:, :], res[t][:, :],
                    AF.Identity, bias=stnm1[:, t:t + 1], scale=strs[:, t:t + 1],
                    reads=[rn, "strs", "stnm1"], writes=["sbf0"], tag=f"B{t}xo")
                dst, dname = (wA[0], "wA0") if t < 8 else (wA[1], "wA1")
                for c in range(NC_CH):
                    tr(pT[:, c * 128:(c + 1) * 128],
                       sbf[0][:, c * 128:(c + 1) * 128], idbf[:, :],
                       reads=["sbf0", "idbf_s"], writes=["pT"], tag=f"B{t}tr")
                base = (t % 8) * 1024
                vop("scalar", "copy", dst[:, base:base + 1024], pT[:, :],
                    reads=["pT"], writes=[dname, f"fm{t}"], tag=f"B{t}fme")

            def stageB2(t):
                rn = f"res{t}"
                dst, dname = (wA[0], "wA0") if t < 8 else (wA[1], "wA1")
                for c in range(NC_CH):
                    mm(pA0[:, 0:NE], dst[:, (t % 8) * 1024 + c * 128:
                                         (t % 8) * 1024 + (c + 1) * 128],
                       rt_sb[:, c * NE:(c + 1) * NE], start=(c == 0), stop=False,
                       reads=[f"fm{t}", "rt_sb"], writes=["pA0"], tag=f"B{t}rt")
                mm(pA0[:, 0:NE], onesrow[0:1, :], rb_r[:, :], False, True,
                   reads=["onesrow_s", "rb_r"], writes=["pA0"], tag=f"B{t}rtb")
                vop("vector", "tensor_copy", lg[:, :], pA0[:, 0:NE],
                    reads=["pA0"], writes=["lg"], tag=f"B{t}lg")
                tapB("lgits", t, lg[:, :], ["lg"])
                vop("vector", "reduce_max", rmx[:, :], lg[:, :], AX.X,
                    reads=["lg"], writes=["rmx"])
                vop("vector", "tensor_scalar", lg[:, :], lg[:, :], rmx[:, :], None,
                    ALU.subtract, reads=["lg", "rmx"], writes=["lg"])
                vop("scalar", "activation", lg[:, :], lg[:, :], AF.Exp,
                    reads=["lg"], writes=["lg"], tag=f"B{t}exp")
                vop("vector", "reduce_sum", sme[:, :], lg[:, :], AX.X,
                    reads=["lg"], writes=["sme"])
                vop("scalar", "activation", wnv[:, :], sme[:, :], AF.Ln,
                    reads=["sme"], writes=["wnv"], tag=f"B{t}lnw")
                vop("scalar", "activation", wnv[:, :], wnv[:, :], AF.Exp,
                    scale=-1.0, reads=["wnv"], writes=["wnv"], tag=f"B{t}invw")
                vop("vector", "tensor_scalar", lg[:, :], lg[:, :], 1.0,
                    None, ALU.is_ge, reads=["lg"], writes=["lg"], tag=f"B{t}eq")
                vop("vector", "tensor_scalar", mw_f[:, t * NE:(t + 1) * NE],
                    lg[:, :], wnv[:, :], None, ALU.mult,
                    reads=["lg", "wnv"], writes=["mw_f"], tag=f"B{t}mw")
            for t in range(NT):
                tapB("mw", t, mw_f[:, t * NE:(t + 1) * NE], ["mw_f"])
            def stageB3(t):
                rn = f"res{t}"
                tr(pA1[0:NE, 0:128], mw_f[:, t * NE:(t + 1) * NE], if32[:, :],
                   reads=["mw_f", "if32_s"], writes=["pA1"], tag=f"B{t}mwt")
                vop("vector", "tensor_copy", mwt_bf[:, :], pA1[0:NE, 0:128],
                    reads=["pA1"], writes=["mwt_bf"], tag=f"B{t}mwt")
                for h in range(2):
                    ph, phn = (pC0, pC1)[h], ("pC0", "pC1")[h]
                    mm(ph[:, :], mwt_bf[:, :], rows_bf[0:8, h * 512:(h + 1) * 512],
                       True, True, reads=["mwt_bf", "rows_bf"], writes=[phn],
                       tag=f"B{t}bias")
                    vop("vector", "tensor_add", res[t][:, h * 512:(h + 1) * 512],
                        res[t][:, h * 512:(h + 1) * 512], ph[:, :],
                        reads=[rn, phn], writes=[rn], tag=f"B{t}biasadd")

            pairs = ((pA0, pA1, "pA0", "pA1"), (pB0, pB1, "pB0", "pB1"),
                     (pC0, pC1, "pC0", "pC1"))

            def expertTile(e, t):
                rn = f"res{t}"
                fm = wA[0] if t < 8 else wA[1]
                fmn = f"fm{t}"
                base = (t % 8) * 1024
                q0, q1, qn0, qn1 = pairs[(e * NT + t) % 3]
                for h in range(2):
                    ph, phn = (q0, q1)[h], (qn0, qn1)[h]
                    for c in range(NC_CH):
                        slot = (2 * e + c // 4) % 3
                        mm(ph[:, :], fm[:, base + c * 128:base + (c + 1) * 128],
                           wB[slot][:, (c % 4) * D + h * 512:
                                     (c % 4) * D + (h + 1) * 512],
                           start=(c == 0), stop=(c == NC_CH - 1),
                           reads=[fmn, f"wB{slot}"], writes=[phn],
                           tag=f"B{t}e{e}")
                    vop("vector", "scalar_tensor_tensor",
                        res[t][:, h * 512:(h + 1) * 512], ph[:, :],
                        mw_f[:, t * NE + e:t * NE + e + 1],
                        res[t][:, h * 512:(h + 1) * 512], ALU.mult, ALU.add,
                        reads=[phn, "mw_f", rn], writes=[rn], tag=f"B{t}e{e}acc")
                if e == NE - 1:
                    stats_tile("C", t)
                if t == NT - 1:
                    for H in (2 * e + 3, 2 * e + 4):
                        if H < 2 * NE:
                            load_half(H)

            dobias = DEBUG_DUMP not in ("nobias", "e0")
            doexp = DEBUG_DUMP != "noexp"
            stageB1(0)
            for t in range(NT):
                if t + 1 < NT:
                    stageB1(t + 1)
                stageB2(t)
                if doexp:
                    expertTile(0, t)
                if dobias:
                    stageB3(t)
            exp_rng = range(1, NE)
            if DEBUG_DUMP == "noexp":
                exp_rng = range(0)
            elif DEBUG_DUMP == "e0":
                exp_rng = range(0)
            for e in exp_rng:
                for t in range(NT):
                    expertTile(e, t)

        def phase_C(s):
            for j in range(2):
                dma("gpq", wB[j][:, :].rearrange("p (c d) -> p c d", d=D),
                    wv1_d.rearrange("(c p) d -> p c d", p=128)[:, j * 4:(j + 1) * 4, :],
                    writes=[f"wB{j}"], tag=f"C_wv{j}")
                vop("scalar", "mul", wB[j][:, :], wB[j][:, :], -1.0,
                    reads=[f"wB{j}"], writes=[f"wB{j}"], tag=f"C_negwv{j}")
            stats("C", inner=True)
            vop("vector", "memset", w0t_f[0:64, :], 0.0, writes=["w0t_f"],
                tag="Cinit")
            vop("vector", "memset", w0t_bf[:, :], 0.0, writes=["w0t_bf"], tag="Cinit")

            def stage1(t):
                rn = f"res{t}"
                par = t % 2
                k_bf, k_n = k_bfp[par], f"k_bf{par}"
                q_bf, q_n = q_bfp[par], f"q_bf{par}"
                pt_bf, pt_n = pt_bfp[par], f"pt_bf{par}"
                qt_bf, qt_n = qt_bfp[par], f"qt_bf{par}"
                gt_bf, gt_n = gt_bfp[par], f"gt_bf{par}"
                hfm, hfm_n = (sbf[1], "sbf1") if par == 0 else (sbf[4], "sbf4")
                vop("scalar", "activation", sf[1][:, :], res[t][:, :], AF.Identity,
                    bias=stnm1[:, t:t + 1], scale=strs[:, t:t + 1],
                    reads=[rn, "strs", "stnm1"], writes=["sf1"], tag=f"C{t}xo")
                vop("scalar", "activation", sbf[0][:, :], res[t][:, :], AF.Identity,
                    bias=stnm2[:, t:t + 1], scale=strr[:, t:t + 1],
                    reads=[rn, "strr", "stnm2"], writes=["sbf0"], tag=f"C{t}h")
                vop("vector", "tensor_add", res[t][:, :], res[t][:, :], sf[1][:, :],
                    reads=[rn, "sf1"], writes=[rn], tag=f"C{t}rx")
                for c in range(NC_CH):
                    tr(pT[:, c * 128:(c + 1) * 128], sbf[0][:, c * 128:(c + 1) * 128],
                       idbf[:, :], reads=["sbf0", "idbf_s"], writes=["pT"],
                       tag=f"C{t}tr")
                for c in range(NC_CH):
                    vop("scalar", "activation", hfm[:, c * 128:(c + 1) * 128],
                        pT[:, c * 128:(c + 1) * 128], AF.Identity,
                        bias=b1c[:, c:c + 1], scale=g1c[:, c:c + 1],
                        reads=["pT", "cols"], writes=[hfm_n], tag=f"C{t}ge")
                for j, (pq, pqn, dst, dn) in enumerate(
                        ((pA0, "pA0", k_bf, k_n), (pA1, "pA1", q_bf, q_n))):
                    for c in range(NC_CH):
                        mm(pq[0:64, 0:128],
                           wkq_sb[:, c * 128 + j * 64:c * 128 + j * 64 + 64],
                           hfm[:, c * 128:(c + 1) * 128],
                           start=(c == 0), stop=False,
                           reads=["wkq_sb", hfm_n], writes=[pqn], tag=f"C{t}kq")
                    mm(pq[0:64, 0:128], bkq_r[:, j * 64:j * 64 + 64],
                       onesrow[0:1, :], False, True,
                       reads=["bkq_r", "onesrow_s"], writes=[pqn], tag=f"C{t}kqb")
                    vop("scalar", "copy", dst[:, :], pq[0:64, 0:128],
                        reads=[pqn], writes=[dn], tag=f"C{t}kqe")
                mm(pA0[:, 0:128], k_bf[:, :], k_bf[:, :], True, True,
                   reads=[k_n], writes=["pA0"], tag=f"C{t}S")
                vop("vector", "scalar_tensor_tensor", cX[:, :], pA0[:, 0:128], -LR,
                    mubd[:, :], ALU.mult, ALU.mult,
                    reads=["pA0", "mubd_s"], writes=["cX"], tag=f"C{t}X0")
                vop("vector", "scalar_tensor_tensor", cY[:, :], pA0[:, 0:128], -LR,
                    mlbd[:, :], ALU.mult, ALU.mult,
                    reads=["pA0", "mlbd_s"], writes=["cY"], tag=f"C{t}Y0")
                vop("vector", "scalar_tensor_tensor", cSb[:, :], pA0[:, 0:128], -LR,
                    mlbs[:, :], ALU.mult, ALU.mult,
                    reads=["pA0", "mlbs_s"], writes=["cSb"], tag=f"C{t}Sb")
                mm(pA1[:, 0:128], k_bf[:, :], q_bf[:, :], True, True,
                   reads=[k_n, q_n], writes=["pA1"], tag=f"C{t}KQ")
                vop("vector", "scalar_tensor_tensor", gt_bf[:, :], pA1[:, 0:128],
                    -1.0, mus[:, :], ALU.mult, ALU.mult,
                    reads=["pA1", "mus_s"], writes=[gt_n], tag=f"C{t}Gt")
                vop("vector", "tensor_add", cRa[:, :], cX[:, :], if32[:, :],
                    reads=["cX", "if32_s"], writes=["cRa"], tag=f"C{t}P0")

            def tbuild(t):
                par = t % 2
                pt_bf, pt_n = pt_bfp[par], f"pt_bf{par}"
                qt_bf, qt_n = qt_bfp[par], f"qt_bf{par}"
                Rcur, Rnam, Roth, Rothn = cRa, "cRa", cRb, "cRb"
                for k in range(1, 4):
                    mm(pA1[:, 0:128], cX[:, :], cY[:, :], True, True,
                       reads=["cX", "cY"], writes=["pA1"], tag=f"C{t}Y2_{k}")
                    if k < 3:
                        mm(pA0[:, 0:128], cY[:, :], cX[:, :], True, True,
                           reads=["cY", "cX"], writes=["pA0"], tag=f"C{t}X2_{k}")
                    vop("vector", "tensor_add", cIY[:, :], pA1[:, 0:128],
                        if32[:, :], reads=["pA1", "if32_s"], writes=["cIY"],
                        tag=f"C{t}IY{k}")
                    if k < 3:
                        vop("vector", "tensor_copy", cY[:, :], pA1[:, 0:128],
                            reads=["pA1"], writes=["cY"], tag=f"C{t}Ye{k}")
                        vop("scalar", "copy", cX[:, :], pA0[:, 0:128],
                            reads=["pA0"], writes=["cX"], tag=f"C{t}Xe{k}")
                    yield
                    mm(pB0[:, 0:128], cIY[:, :], Rcur[:, :], True, True,
                       reads=["cIY", Rnam], writes=["pB0"], tag=f"C{t}P{k}")
                    vop("vector", "tensor_copy", Roth[:, :], pB0[:, 0:128],
                        reads=["pB0"], writes=[Rothn], tag=f"C{t}Pe{k}")
                    if k == 3:
                        vop("vector", "tensor_copy", pt_bf[:, :], pB0[:, 0:128],
                            reads=["pB0"], writes=[pt_n], tag=f"C{t}Ptb")
                    Rcur, Rnam, Roth, Rothn = Roth, Rothn, Rcur, Rnam
                    yield
                PT3, PT3n, Qs, Qsn = Rcur, Rnam, Roth, Rothn
                mm(pA0[:, 0:128], cSb[:, :], PT3[:, :], True, True,
                   reads=["cSb", PT3n], writes=["pA0"], tag=f"C{t}Mt")
                mm(pA1[:, 0:128], PT3[:, :], cSb[:, :], True, True,
                   reads=[PT3n, "cSb"], writes=["pA1"], tag=f"C{t}M")
                vop("scalar", "copy", cX[:, :], pA0[:, 0:128],
                    reads=["pA0"], writes=["cX"], tag=f"C{t}Mte")
                vop("scalar", "copy", cY[:, :], pA1[:, 0:128],
                    reads=["pA1"], writes=["cY"], tag=f"C{t}Me")
                yield
                vop("vector", "tensor_add", Qs[:, :], cX[:, :], if32[:, :],
                    reads=["cX", "if32_s"], writes=[Qsn], tag=f"C{t}Q0")
                mm(pA0[:, 0:128], cX[:, :], cY[:, :], True, True,
                   reads=["cX", "cY"], writes=["pA0"], tag=f"C{t}M2")
                mm(pA1[:, 0:128], cY[:, :], cX[:, :], True, True,
                   reads=["cY", "cX"], writes=["pA1"], tag=f"C{t}M2t")
                vop("vector", "tensor_add", cIY[:, :], pA0[:, 0:128], if32[:, :],
                    reads=["pA0", "if32_s"], writes=["cIY"], tag=f"C{t}IM2")
                vop("vector", "tensor_copy", cSb[:, :], pA0[:, 0:128],
                    reads=["pA0"], writes=["cSb"], tag=f"C{t}M2e")
                vop("scalar", "copy", cX[:, :], pA1[:, 0:128],
                    reads=["pA1"], writes=["cX"], tag=f"C{t}M2te")
                yield
                mm(pB0[:, 0:128], cIY[:, :], Qs[:, :], True, True,
                   reads=["cIY", Qsn], writes=["pB0"], tag=f"C{t}Q1")
                vop("vector", "tensor_copy", PT3[:, :], pB0[:, 0:128],
                    reads=["pB0"], writes=[PT3n], tag=f"C{t}Q1e")
                yield
                mm(pA0[:, 0:128], cX[:, :], cSb[:, :], True, True,
                   reads=["cX", "cSb"], writes=["pA0"], tag=f"C{t}M4")
                vop("vector", "tensor_add", cIY[:, :], pA0[:, 0:128], if32[:, :],
                    reads=["pA0", "if32_s"], writes=["cIY"], tag=f"C{t}IM4")
                yield
                mm(pB0[:, 0:128], cIY[:, :], PT3[:, :], True, True,
                   reads=["cIY", PT3n], writes=["pB0"], tag=f"C{t}Q2")
                vop("scalar", "copy", qt_bf[:, :], pB0[:, 0:128],
                    reads=["pB0"], writes=[qt_n], tag=f"C{t}Q2e")

            def stage2(t):
                rn = f"res{t}"
                par = t % 2
                k_bf, k_n = k_bfp[par], f"k_bf{par}"
                q_bf, q_n = q_bfp[par], f"q_bf{par}"
                pt_bf, pt_n = pt_bfp[par], f"pt_bf{par}"
                qt_bf, qt_n = qt_bfp[par], f"qt_bf{par}"
                gt_bf, gt_n = gt_bfp[par], f"gt_bf{par}"
                hfm, hfm_n = (sbf[1], "sbf1") if par == 0 else (sbf[4], "sbf4")
                for h in range(2):
                    ph, phn = (pC0, pC1)[h], ("pC0", "pC1")[h]
                    mm(ph[:, :], k_bf[:, :], w0t_bf[:, h * 512:(h + 1) * 512],
                       True, False, reads=[k_n, "w0t_bf"], writes=[phn],
                       tag=f"C{t}BKW")
                    for c in range(NC_CH):
                        slot = c // 4
                        mm(ph[:, :], hfm[:, c * 128:(c + 1) * 128],
                           wB[slot][:, (c % 4) * D + h * 512:
                                     (c % 4) * D + (h + 1) * 512],
                           start=False, stop=False,
                           reads=[hfm_n, f"wB{slot}"], writes=[phn],
                           tag=f"C{t}BV")
                    mm(ph[:, :], onesrow[32:33, :],
                       negbv[:, h * 512:(h + 1) * 512],
                       False, True, reads=["onesrow_s", "rows_bf"], writes=[phn],
                       tag=f"C{t}Bb")
                    vop("scalar", "mul", sbf[2][:, h * 512:(h + 1) * 512],
                        ph[:, :], LR, reads=[phn], writes=["sbf2"], tag=f"C{t}Bs")
                    yield
                for h in range(2):
                    ph, phn = (pC0, pC1)[h], ("pC0", "pC1")[h]
                    mm(ph[:, :], pt_bf[:, :], sbf[2][:, h * 512:(h + 1) * 512],
                       True, True, reads=[pt_n, "sbf2"], writes=[phn],
                       tag=f"C{t}U1")
                    vop("scalar", "copy", sbf[0][:, h * 512:(h + 1) * 512],
                        ph[:, :], reads=[phn], writes=["sbf0"], tag=f"C{t}U1e")
                yield
                for h in range(2):
                    ph, phn = (pC0, pC1)[h], ("pC0", "pC1")[h]
                    mm(ph[:, :], qt_bf[:, :], sbf[0][:, h * 512:(h + 1) * 512],
                       True, True, reads=[qt_n, "sbf0"], writes=[phn],
                       tag=f"C{t}U")
                    vop("vector", "tensor_copy", sbf[3][:, h * 512:(h + 1) * 512],
                        ph[:, :], reads=[phn], writes=["sbf3"], tag=f"C{t}Ue")
                yield
                for h in range(2):
                    ph, phn = (pB1, pC0)[h], ("pB1", "pC0")[h]
                    mm(ph[:, :], q_bf[:, :], w0t_bf[:, h * 512:(h + 1) * 512],
                       True, False, reads=[q_n, "w0t_bf"], writes=[phn],
                       tag=f"C{t}YQ")
                    mm(ph[:, :], gt_bf[:, :], sbf[3][:, h * 512:(h + 1) * 512],
                       False, True, reads=[gt_n, "sbf3"], writes=[phn],
                       tag=f"C{t}YG")
                    vop("vector", "tensor_add", res[t][:, h * 512:(h + 1) * 512],
                        res[t][:, h * 512:(h + 1) * 512], ph[:, :],
                        reads=[rn, phn], writes=[rn], tag=f"C{t}ry")
                    yield
                tr(pT2[:, 0:64], k_bf[:, :], idbf[0:64, 0:64],
                   reads=[k_n, "idbf_s"], writes=["pT2"], tag=f"C{t}ktr")
                vop("scalar", "copy", ktm_bf[:, :], pT2[0:128, 0:64],
                    reads=["pT2"], writes=["ktm_bf"], tag=f"C{t}kte")
                yield
                for h in range(2):
                    ph, phn = (pC1, pB1)[h], ("pC1", "pB1")[h]
                    mm(ph[0:64, :], ktm_bf[:, :], sbf[3][:, h * 512:(h + 1) * 512],
                       True, True, reads=["ktm_bf", "sbf3"], writes=[phn],
                       tag=f"C{t}KtU")
                    vop("vector", "scalar_tensor_tensor",
                        w0t_bf[:, h * 512:(h + 1) * 512], ph[0:64, :], -1.0,
                        w0t_f[0:64, h * 512:(h + 1) * 512], ALU.mult, ALU.add,
                        reads=[phn, "w0t_f"], writes=["w0t_bf"], tag=f"C{t}Wbf")
                    vop("vector", "scalar_tensor_tensor",
                        w0t_f[0:64, h * 512:(h + 1) * 512], ph[0:64, :], -1.0,
                        w0t_f[0:64, h * 512:(h + 1) * 512], ALU.mult, ALU.add,
                        reads=[phn, "w0t_f"], writes=["w0t_f"], tag=f"C{t}Wf")
                    yield
                stats_tile("D", t)

            stage1(0)
            for _ in tbuild(0):
                pass
            for t in range(NT):
                gens = []
                if t + 1 < NT:
                    stage1(t + 1)
                    gens.append(tbuild(t + 1))
                gens.append(stage2(t))
                while gens:
                    for g in list(gens):
                        try:
                            next(g)
                        except StopIteration:
                            gens.remove(g)

        def phase_D(s):
            load_w1024("wA0", wA[0], wr2_d, "wr2",
                       extra=tuple(f"fm{t}" for t in range(8)))
            load_w1024("wA1", wA[1], wk2_d, "wk2",
                       extra=tuple(f"fm{t}" for t in range(8, 16)))
            load_w1024("wA2", wA[2], wv2_d, "wv2")
            if s == 0:
                dma("gpq", lntd[:, :], lntd_d[:, :], writes=["p2t"],
                    tag="lntd_rd")
            stats("D", inner=False)

            def x4c(j):
                if j < 6:
                    return wB[j // 2], f"wB{j // 2}", (j % 2) * T_SEQ
                return wA[3], "wA3", (j - 6) * T_SEQ

            def slots(t):
                if t % 2 == 0:
                    return (sbf[0], "sbf0", sbf[1], "sbf1")
                return (sbf[2], "sbf2", sbf[3], "sbf3")

            def stageD1(t):
                rn = f"res{t}"
                t0 = t * 128
                xr, xr_n, xk, xk_n = slots(t)
                vop("scalar", "activation", xr[:, :], res[t][:, :], AF.Identity,
                    bias=stnm1[:, t:t + 1], scale=strs[:, t:t + 1],
                    reads=[rn, "strs", "stnm1"], writes=[xr_n], tag=f"D{t}xo")
                for c in range(NC_CH):
                    tr(pT[:, c * 128:(c + 1) * 128], xr[:, c * 128:(c + 1) * 128],
                       idbf[:, :], reads=[xr_n, "idbf_s"], writes=["pT"],
                       tag=f"D{t}tr")
                for c in range(NC_CH):
                    dst, dn, db = x4c(c)
                    vop("scalar", "copy", dst[:, db + t0:db + t0 + 128],
                        pT[:, c * 128:(c + 1) * 128], reads=["pT"], writes=[dn],
                        tag=f"D{t}x4")
                for c in range(NC_CH):
                    src_, sn, db = x4c(c)
                    xo = src_[:, db + t0:db + t0 + 128]
                    if t == 0:
                        xop = src_[:, db + t0:db + t0 + 127]
                        vop("vector", "tensor_sub",
                            xk[:, c * 128 + 1:(c + 1) * 128],
                            src_[:, db + t0 + 1:db + t0 + 128], xop,
                            reads=[sn], writes=[xk_n], tag=f"D{t}d{c}")
                        vop("vector", "tensor_copy", xk[:, c * 128:c * 128 + 1],
                            xo[:, 0:1], reads=[sn], writes=[xk_n], tag=f"D{t}d0{c}")
                        vop("vector", "scalar_tensor_tensor",
                            xr[:, c * 128 + 1:(c + 1) * 128],
                            xk[:, c * 128 + 1:(c + 1) * 128], murc[:, c:c + 1],
                            xop, ALU.mult, ALU.add,
                            reads=[xk_n, "cols", sn], writes=[xr_n],
                            tag=f"D{t}mr{c}")
                        vop("vector", "tensor_scalar", xr[:, c * 128:c * 128 + 1],
                            xk[:, c * 128:c * 128 + 1], murc[:, c:c + 1], None,
                            ALU.mult, reads=[xk_n, "cols"], writes=[xr_n],
                            tag=f"D{t}mr0{c}")
                        vop("vector", "scalar_tensor_tensor",
                            xk[:, c * 128 + 1:(c + 1) * 128],
                            xk[:, c * 128 + 1:(c + 1) * 128], mukc[:, c:c + 1],
                            xop, ALU.mult, ALU.add,
                            reads=[xk_n, "cols", sn], writes=[xk_n],
                            tag=f"D{t}mk{c}")
                        vop("vector", "tensor_scalar", xk[:, c * 128:c * 128 + 1],
                            xk[:, c * 128:c * 128 + 1], mukc[:, c:c + 1], None,
                            ALU.mult, reads=[xk_n, "cols"], writes=[xk_n],
                            tag=f"D{t}mk0{c}")
                    else:
                        xop = src_[:, db + t0 - 1:db + t0 + 127]
                        vop("vector", "tensor_sub", xk[:, c * 128:(c + 1) * 128],
                            xo, xop, reads=[sn], writes=[xk_n], tag=f"D{t}d{c}")
                        vop("vector", "scalar_tensor_tensor",
                            xr[:, c * 128:(c + 1) * 128],
                            xk[:, c * 128:(c + 1) * 128], murc[:, c:c + 1],
                            xop, ALU.mult, ALU.add,
                            reads=[xk_n, "cols", sn], writes=[xr_n],
                            tag=f"D{t}mr{c}")
                        vop("vector", "scalar_tensor_tensor",
                            xk[:, c * 128:(c + 1) * 128],
                            xk[:, c * 128:(c + 1) * 128], mukc[:, c:c + 1],
                            xop, ALU.mult, ALU.add,
                            reads=[xk_n, "cols", sn], writes=[xk_n],
                            tag=f"D{t}mk{c}")

            def stageD2(t):
                rn = f"res{t}"
                t0 = t * 128
                xr, xr_n, xk, xk_n = slots(t)
                proj1024(xr, xr_n, wA[0], "wA0", (pA0, pA1), ("pA0", "pA1"),
                         tag=f"D{t}r")
                for h in range(2):
                    vop("scalar", "activation", sf[0][:, h * 512:(h + 1) * 512],
                        (pA0, pA1)[h][:, :], AF.Sigmoid,
                        reads=[("pA0", "pA1")[h]], writes=["sf0"], tag=f"D{t}sig")
                proj1024(xk, xk_n, wA[1], "wA1", (pB0, pB1), ("pB0", "pB1"),
                         tag=f"D{t}k")
                for h in range(2):
                    vop("scalar", "activation", sf[1][:, h * 512:(h + 1) * 512],
                        (pB0, pB1)[h][:, :], AF.Relu,
                        reads=[("pB0", "pB1")[h]], writes=["sf1"], tag=f"D{t}relu")
                vop("vector", "tensor_mul", sbf[4][:, :], sf[1][:, :], sf[1][:, :],
                    reads=["sf1"], writes=["sbf4"], tag=f"D{t}k2")
                for c in range(NC_CH):
                    tr(pT2[:, c * 128:(c + 1) * 128],
                       sbf[4][:, c * 128:(c + 1) * 128], idbf[:, :],
                       reads=["sbf4", "idbf_s"], writes=["pT2"], tag=f"D{t}ktr")
                vop("scalar", "copy", sbf[5][:, :], pT2[:, :],
                    reads=["pT2"], writes=["sbf5"], tag=f"D{t}kfe")
                proj1024(sbf[5], "sbf5", wA[2], "wA2", (pC0, pC1), ("pC0", "pC1"),
                         tag=f"D{t}v")
                for h in range(2):
                    vop("vector", "tensor_mul", sf[1][:, h * 512:(h + 1) * 512],
                        (pC0, pC1)[h][:, :], sf[0][:, h * 512:(h + 1) * 512],
                        reads=[("pC0", "pC1")[h], "sf0"], writes=["sf1"],
                        tag=f"D{t}rv")
                vop("vector", "tensor_add", res[t][:, :], res[t][:, :], sf[1][:, :],
                    reads=[rn, "sf1"], writes=[rn], tag=f"D{t}res")
                dma("syncq", y_d[s * T_SEQ + t0:s * T_SEQ + t0 + 128, :],
                    res[t][:, :], reads=[rn], tag=f"D{t}out")
                if s == 0:
                    dma("syncq", res[t][:, :],
                        x_d[T_SEQ + t0:T_SEQ + t0 + 128, :],
                        writes=[rn], tag=f"x1_{t}")
                    if t >= 2:
                        stats_tile("A", t - 2)

            stageD1(0)
            for t in range(NT):
                if t + 1 < NT:
                    stageD1(t + 1)
                stageD2(t)
            if s == 0:
                stats_tile("A", NT - 2)
                stats_tile("A", NT - 1)

        for s in range(2):
            if s == 1:
                load_w1024("wA0", wA[0], wk0_d, "wk0b",
                           extra=tuple(f"fm{t}" for t in range(8)))
                load_w1024("wA1", wA[1], wv0_d, "wv0b",
                           extra=tuple(f"fm{t}" for t in range(8, 16)))
                load_w1024("wA2", wA[2], wr0_d, "wr0b")
                load_w1024("wA3", wA[3], wo0_d, "wo0b")
            if "A" in PHASES:
                phase_A(s)
            if "B" in PHASES:
                phase_B(s)
            if "C" in PHASES:
                phase_C(s)
            if "D" in PHASES:
                phase_D(s)
            if "D" not in PHASES and DEBUG_DUMP in (None, "noexp", "nobias", "e0"):
                for t in range(NT):
                    dma("syncq", y_d[s * T_SEQ + t * 128:s * T_SEQ + (t + 1) * 128, :],
                        res[t][:, :], reads=[f"res{t}"], tag=f"dbgout{t}")

        p.finalize(nc)
    return nc


def _blockmask(f, upper):
    m = np.zeros((128, 128), f)
    for i in range(8):
        b = np.triu(np.ones((16, 16), f), 1) if upper else \
            np.tril(np.ones((16, 16), f), -1)
        m[i * 16:(i + 1) * 16, i * 16:(i + 1) * 16] = b
    return m


def _blockstrict(f):
    m = np.zeros((128, 128), f)
    for i in range(8):
        m[i * 16:, :i * 16][:16] = 1.0 if False else m[i * 16:, :i * 16][:16]
    for bi in range(8):
        for bj in range(bi):
            m[bi * 16:(bi + 1) * 16, bj * 16:(bj + 1) * 16] = 1.0
    return m


def _prep_inputs(inputs):
    f = np.float32
    c = np.ascontiguousarray

    def T(a):
        return c(np.asarray(a).T.astype(f, copy=False))

    vecs = np.zeros((19, D), f)
    vecs[0] = inputs["L0_attn_ln_gamma"]
    vecs[1] = inputs["L0_attn_ln_beta"]
    vecs[2] = inputs["L0_attn_time_decay"]
    vecs[3] = inputs["L0_attn_time_first"]
    vecs[4, :NE] = inputs["L0_ffn_router_b"]
    vecs[5:13] = inputs["L0_ffn_expert_b"]
    vecs[13, :128] = np.concatenate([inputs["L1_attn_bk"], inputs["L1_attn_bq"]])
    vecs[14] = inputs["L1_attn_bv"]
    vecs[15] = inputs["L1_attn_gamma"]
    vecs[16] = inputs["L1_attn_beta"]
    vecs[17] = inputs["L1_ffn_mu_r"]
    vecs[18] = inputs["L1_ffn_mu_k"]

    shared = dict(
        wk0t=T(inputs["L0_attn_Wk"]),
        wv0t=T(inputs["L0_attn_Wv"]),
        wr0t=T(inputs["L0_attn_Wr"]),
        wo0t=T(inputs["L0_attn_Wo"]),
        rt0=T(inputs["L0_ffn_router_w"]),
        et0=c(np.asarray(inputs["L0_ffn_expert_w"]).transpose(0, 2, 1)
              .reshape(NE * D, D).astype(f, copy=False)),
        wkq1t=c(np.concatenate([np.asarray(inputs["L1_attn_Wk"]).T,
                                np.asarray(inputs["L1_attn_Wq"]).T],
                               axis=1).astype(f, copy=False)),
        wv1t=T(inputs["L1_attn_Wv"]),
        wr2t=T(inputs["L1_ffn_W_r"]),
        wk2t=T(inputs["L1_ffn_W_k"]),
        wv2t=T(inputs["L1_ffn_W_v"]),
        vecs=vecs,
        idbf=np.eye(128, dtype=BF),
        usbf=np.triu(np.ones((128, 128), BF), 1),
        onescol=np.ones((128, 1), BF),
        onesrow=np.ones((65, 128), BF),
        onesrow_f=np.ones((1, 128), f),
        iota1=np.arange(1, 129, dtype=f).reshape(1, 128),
        mubd=_blockmask(BF, upper=True),
        mlbd=_blockmask(BF, upper=False),
        mlbs=_blockstrict(BF),
        mus_f=np.triu(np.ones((128, 128), BF), 1),
        if32=np.eye(128, dtype=f),
        one11_f=np.ones((1, 1), f),
    )
    x = np.asarray(inputs["L0_attn_x"]).astype(f, copy=False)
    in_maps = []
    for i in range(N_CORES):
        m = dict(shared)
        m["x"] = c(x[2 * i:2 * i + 2].reshape(2 * T_SEQ, D))
        in_maps.append(m)
    return in_maps


def kernel(**inputs):
    in_maps = _prep_inputs(inputs)
    nc = build_program()
    res = run_bass_kernel_spmd(nc, in_maps, core_ids=list(range(N_CORES)))
    out = np.empty((16, T_SEQ, D), np.float32)
    for i in range(N_CORES):
        out[2 * i:2 * i + 2] = res.results[i]["y"].reshape(2, T_SEQ, D)
    return out


# version 14
# speedup vs baseline: 1.0488x; 1.0488x over previous
import contextlib

import numpy as np
import ml_dtypes

import concourse.bass as bass
import concourse.mybir as mybir
from concourse.bass_utils import run_bass_kernel_spmd

F32 = mybir.dt.float32
BF16 = mybir.dt.bfloat16
AF = mybir.ActivationFunctionType
ALU = mybir.AluOpType
AX = mybir.AxisListType
BF = ml_dtypes.bfloat16

N_CORES = 8
T_SEQ = 2048
NT = T_SEQ // 128
D = 1024
NC_CH = D // 128
NE = 8
LR = 0.01
EPS = 1e-5

PHASES = "ABCD"
DEBUG_DUMP = None

COMPUTE_ENG = ("tensor", "vector", "scalar")
QUEUE_ENG = ("syncq", "gpq")


class Prog:

    def __init__(self):
        self.ops = []
        self.eng_ops = {e: [] for e in COMPUTE_ENG + QUEUE_ENG}
        self.last_w = {}
        self.readers = {}
        self.pad_fn = {}

    def _pad(self, eng):
        return self.emit(eng, self.pad_fn[eng], tag="hazpad")

    def emit(self, eng, fn, reads=(), writes=(), tag=""):
        if eng in self.pad_fn and self.eng_ops.get(eng):
            pw = self.ops[self.eng_ops[eng][-1]]["writes"]
            if any(b in pw for b in reads):
                self._pad(eng)
        rdeps = set()
        wdeps = set()
        for b in reads:
            w = self.last_w.get(b)
            if w is not None:
                rdeps.add(w)
        for b in writes:
            for r in self.readers.get(b, ()):
                wdeps.add(r)
            w = self.last_w.get(b)
            if w is not None:
                rdeps.add(w)
        rdeps = {d for d in rdeps if self.ops[d]["eng"] != eng}
        wdeps = {d for d in wdeps if self.ops[d]["eng"] != eng}
        deps = set(wdeps)
        for d in rdeps:
            te = self.ops[d]["eng"]
            if te in self.pad_fn and not self.ops[d]["dma"]:
                lst = self.eng_ops[te]
                k = self.ops[d]["idx_on_eng"]
                if k + 1 >= len(lst):
                    self._pad(te)
                    lst = self.eng_ops[te]
                deps.add(lst[k + 1])
            else:
                deps.add(d)
        opid = len(self.ops)
        self.ops.append(dict(eng=eng, fn=fn, deps=deps, dma=eng in QUEUE_ENG,
                             tag=tag, writes=tuple(writes),
                             idx_on_eng=len(self.eng_ops[eng])))
        self.eng_ops[eng].append(opid)
        for b in reads:
            self.readers.setdefault(b, []).append(opid)
        for b in writes:
            self.readers[b] = [opid]
            self.last_w[b] = opid
        return opid

    def finalize(self, nc):
        marked = [op["dma"] for op in self.ops]
        for op in self.ops:
            for d in op["deps"]:
                marked[d] = True
        val = [0] * len(self.ops)
        for eng in COMPUTE_ENG + QUEUE_ENG:
            c = 0
            for i in self.eng_ops[eng]:
                if marked[i]:
                    c += 16 if self.ops[i]["dma"] else 1
                val[i] = c
        totals = {eng: (val[self.eng_ops[eng][-1]] if self.eng_ops[eng] else 0)
                  for eng in COMPUTE_ENG + QUEUE_ENG}

        with contextlib.ExitStack() as es:
            sems = {e: es.enter_context(nc.semaphore("sem_" + e))
                    for e in COMPUTE_ENG + QUEUE_ENG}
            block = es.enter_context(nc.Block())

            def replay(eng_name, eng):
                last_wait = {}
                for i in self.eng_ops[eng_name]:
                    op = self.ops[i]
                    for d in sorted(op["deps"]):
                        te = self.ops[d]["eng"]
                        v = val[d]
                        if last_wait.get(te, 0) < v:
                            eng.wait_ge(sems[te], v)
                            last_wait[te] = v
                    inst = op["fn"](eng)
                    if marked[i]:
                        inst.then_inc(sems[eng_name], 16 if op["dma"] else 1)

            @block.tensor
            def _(eng):
                replay("tensor", eng)

            @block.vector
            def _(eng):
                replay("vector", eng)

            @block.scalar
            def _(eng):
                replay("scalar", eng)

            @block.sync
            def _(eng):
                replay("syncq", eng)
                if totals["syncq"]:
                    eng.wait_ge(sems["syncq"], totals["syncq"])

            @block.gpsimd
            def _(eng):
                replay("gpq", eng)
                if totals["gpq"]:
                    eng.wait_ge(sems["gpq"], totals["gpq"])


def build_program():
    nc = bass.Bass("TRN2", target_bir_lowering=False)
    p = Prog()

    def din(name, shape, dt=F32):
        return nc.dram_tensor(name, list(shape), dt, kind="ExternalInput").ap()

    x_d = din("x", (2 * T_SEQ, D))
    y_d = nc.dram_tensor("y", [2 * T_SEQ, D], F32, kind="ExternalOutput").ap()
    lntd_d = nc.dram_tensor("lntd_hbm", [1, D], F32, kind="Internal").ap()
    wk0_d = din("wk0t", (D, D))
    wv0_d = din("wv0t", (D, D))
    wr0_d = din("wr0t", (D, D))
    wo0_d = din("wo0t", (D, D))
    rt0_d = din("rt0", (D, NE))
    et0_d = din("et0", (NE * D, D))
    wkq1_d = din("wkq1t", (D, 128))
    wv1_d = din("wv1t", (D, D))
    wr2_d = din("wr2t", (D, D))
    wk2_d = din("wk2t", (D, D))
    wv2_d = din("wv2t", (D, D))
    vecs_d = din("vecs", (19, D))
    idbf_d = din("idbf", (128, 128), BF16)
    usbf_d = din("usbf", (128, 128), BF16)
    onescol_d = din("onescol", (128, 1), BF16)
    onesrow_d = din("onesrow", (65, 128), BF16)
    onesrow_f_d = din("onesrow_f", (1, 128))
    iota1_d = din("iota1", (1, 128))
    mubd_d = din("mubd", (128, 128), BF16)
    mlbd_d = din("mlbd", (128, 128), BF16)
    mlbs_d = din("mlbs", (128, 128), BF16)
    mus_d = din("mus_f", (128, 128), BF16)
    if32_d = din("if32", (128, 128))
    one11_d = din("one11_f", (1, 1))

    es = contextlib.ExitStack()
    sb = {}

    def S(name, shape, dt):
        sb[name] = es.enter_context(nc.sbuf_tensor(name, list(shape), dt))
        return sb[name]

    with es:
        hzv0 = S("hzv0", (128, 1), F32)
        hzv1 = S("hzv1", (128, 1), F32)
        hzs0 = S("hzs0", (128, 1), F32)
        hzs1 = S("hzs1", (128, 1), F32)
        p.pad_fn["vector"] = lambda e: e.tensor_copy(hzv1[:, :], hzv0[:, :])
        p.pad_fn["scalar"] = lambda e: e.copy(hzs1[:, :], hzs0[:, :])
        res = [S(f"res{t}", (128, D), F32) for t in range(NT)]
        wA = [S(f"wA{i}", (128, NC_CH * D), BF16) for i in range(4)]
        wB = [S(f"wB{i}", (128, 4 * D), BF16) for i in range(3)]
        wkq_sb = S("wkq_sb", (128, NC_CH * 128), BF16)
        rt_sb = S("rt_sb", (128, NC_CH * NE), BF16)
        idbf = S("idbf_s", (128, 128), BF16)
        usbf = S("usbf_s", (128, 128), BF16)
        onescol = S("onescol_s", (128, 1), BF16)
        onesrow = S("onesrow_s", (65, 128), BF16)
        onesrow_f = S("onesrow_f_s", (1, 128), F32)
        iota1 = S("iota1_s", (1, 128), F32)
        mubd = S("mubd_s", (128, 128), BF16)
        mlbd = S("mlbd_s", (128, 128), BF16)
        mlbs = S("mlbs_s", (128, 128), BF16)
        mus = S("mus_s", (128, 128), BF16)
        if32 = S("if32_s", (128, 128), F32)
        one11 = S("one11_s", (1, 1), F32)
        tf_rep = S("tf_rep", (128, D), BF16)
        td128_rep = S("td128_rep", (128, D), F32)
        cols = S("cols", (128, 48), F32)
        p2t = S("p2t", (128, D), F32)
        carry_f = S("carry_f", (1, D), F32)
        rows_bf = S("rows_bf", (65, D), BF16)
        carry_bf = rows_bf[64:65, :]
        negbv = rows_bf[32:33, :]
        bkq_r = S("bkq_r", (1, 128), BF16)
        rb_r = S("rb_r", (1, NE), BF16)
        bnbuf = S("bnbuf", (128, 12), F32)
        bnbuf2 = S("bnbuf2", (128, 6), F32)
        bnagg = [S("bnagg0", (128, 2), F32), S("bnagg1", (128, 2), F32)]
        stmu = S("stmu", (128, NT), F32)
        stvar = S("stvar", (128, NT), F32)
        strs = S("strs", (128, NT), F32)
        strr = S("strr", (128, NT), F32)
        stt1 = S("stt1", (128, NT), F32)
        stt2 = S("stt2", (128, NT), F32)
        stnm1 = S("stnm1", (128, NT), F32)
        stnm2 = S("stnm2", (128, NT), F32)
        sf = [S(f"sf{i}", (128, D), F32) for i in range(2)]
        sbf = [S(f"sbf{i}", (128, D), BF16) for i in range(6)]
        w0t_f = S("w0t_f", (64, D), F32)
        lntd = p2t[0:1, :]
        w0t_bf = S("w0t_bf", (64, D), BF16)
        pt_bfp = [S(f"pt_bf{i}", (128, 128), BF16) for i in range(2)]
        qt_bfp = [S(f"qt_bf{i}", (128, 128), BF16) for i in range(2)]
        gt_bfp = [S(f"gt_bf{i}", (128, 128), BF16) for i in range(2)]
        k_bfp = [S(f"k_bf{i}", (64, 128), BF16) for i in range(2)]
        q_bfp = [S(f"q_bf{i}", (64, 128), BF16) for i in range(2)]
        ktm_bf = S("ktm_bf", (128, 64), BF16)
        cX = S("cX", (128, 128), BF16)
        cY = S("cY", (128, 128), BF16)
        cIY = S("cIY", (128, 128), BF16)
        cRa = S("cRa", (128, 128), BF16)
        cRb = S("cRb", (128, 128), BF16)
        cSb = S("cSb", (128, 128), BF16)
        lg = S("lg", (128, NE), F32)
        rmx = S("rmx", (128, 1), F32)
        sme = S("sme", (128, 1), F32)
        wnv = S("wnv", (128, 1), F32)
        nty = S("nty", (128, 1), F32)
        eqc = S("eqc", (128, 1), F32)
        mcc = S("mcc", (128, 1), F32)
        mw_f = S("mw_f", (128, NT * NE), F32)
        mwt_bf = S("mwt_bf", (NE, 128), BF16)

        pT = es.enter_context(nc.psum_tensor("pT", [128, 1024], BF16))
        pA0 = es.enter_context(nc.psum_tensor("pA0", [128, 512], F32))
        pA1 = es.enter_context(nc.psum_tensor("pA1", [128, 512], F32))
        pB0 = es.enter_context(nc.psum_tensor("pB0", [128, 512], F32))
        pB1 = es.enter_context(nc.psum_tensor("pB1", [128, 512], F32))
        pC0 = es.enter_context(nc.psum_tensor("pC0", [128, 512], F32))
        pC1 = es.enter_context(nc.psum_tensor("pC1", [128, 512], F32))
        pT2 = es.enter_context(nc.psum_tensor("pT2", [128, 1024], BF16))

        def E(eng, fn, reads=(), writes=(), tag=""):
            return p.emit(eng, fn, reads=reads, writes=writes, tag=tag)

        def dma(q, out_ap, in_ap, reads=(), writes=(), tag=""):
            return E(q, lambda e, o=out_ap, i=in_ap: e.dma_start(out=o, in_=i),
                     reads=reads, writes=writes, tag=tag)

        def mm(out_ap, lhsT_ap, rhs_ap, start, stop, reads, writes, tag=""):
            return E("tensor",
                     lambda e, o=out_ap, l=lhsT_ap, r=rhs_ap, s1=start, s2=stop:
                     e.matmul(o, l, r, start=s1, stop=s2),
                     reads=reads, writes=writes, tag=tag)

        def tr(out_ap, in_ap, id_ap, reads, writes, tag=""):
            return E("tensor",
                     lambda e, o=out_ap, i=in_ap, d=id_ap: e.transpose(o, i, d),
                     reads=reads, writes=writes, tag=tag)

        def vop(eng, meth, *aps, reads=(), writes=(), tag="", **kw):
            def fn(e, m=meth, a=aps, k=dict(kw)):
                return getattr(e, m)(*a, **k)
            return E(eng, fn, reads=reads, writes=writes, tag=tag)

        dma("syncq", idbf[:, :], idbf_d[:, :], writes=["idbf_s"])
        dma("syncq", usbf[:, :], usbf_d[:, :], writes=["usbf_s"])
        dma("syncq", onescol[:, :], onescol_d[:, :], writes=["onescol_s"])
        dma("syncq", onesrow[:, :], onesrow_d[:, :], writes=["onesrow_s"])
        dma("syncq", onesrow_f[:, :], onesrow_f_d[:, :], writes=["onesrow_f_s"])
        dma("syncq", iota1[:, :], iota1_d[:, :], writes=["iota1_s"])
        dma("syncq", mubd[:, :], mubd_d[:, :], writes=["mubd_s"])
        dma("syncq", mlbd[:, :], mlbd_d[:, :], writes=["mlbd_s"])
        dma("syncq", mlbs[:, :], mlbs_d[:, :], writes=["mlbs_s"])
        dma("syncq", mus[:, :], mus_d[:, :], writes=["mus_s"])
        dma("syncq", if32[:, :], if32_d[:, :], writes=["if32_s"])
        dma("syncq", one11[:, :], one11_d[:, :], writes=["one11_s"])
        def load_w1024(dst_name, dst, src_ap, tag, extra=()):
            src3 = src_ap.rearrange("(c p) d -> p c d", p=128)
            dst3 = dst[:, :].rearrange("p (c d) -> p c d", d=D)
            for h in range(2):
                dma("gpq", dst3[:, h * 4:(h + 1) * 4, :],
                    src3[:, h * 4:(h + 1) * 4, :],
                    writes=[dst_name, *extra], tag=tag)

        load_w1024("wA0", wA[0], wk0_d, "wk0")
        load_w1024("wA1", wA[1], wv0_d, "wv0")
        load_w1024("wA2", wA[2], wr0_d, "wr0")
        load_w1024("wA3", wA[3], wo0_d, "wo0")
        dma("gpq", rt_sb[:, :].rearrange("p (c e) -> p c e", e=NE),
            rt0_d.rearrange("(c p) e -> p c e", p=128), writes=["rt_sb"])
        dma("gpq", rows_bf[0:8, :], vecs_d[5:13, :], writes=["rows_bf"])
        dma("gpq", wkq_sb[:, :].rearrange("p (c s) -> p c s", s=128),
            wkq1_d.rearrange("(c p) s -> p c s", p=128), writes=["wkq_sb"])
        dma("gpq", bkq_r[:, :], vecs_d[13:14, 0:128], writes=["bkq_r"])
        dma("gpq", rb_r[:, :], vecs_d[4:5, 0:NE], writes=["rb_r"])

        srow = sf[0][0:1, :]
        srow2 = sf[1][0:1, :]
        dma("syncq", srow, vecs_d[2:3, :], writes=["sf0"])
        vop("scalar", "activation", srow, srow, AF.Sigmoid,
            reads=["sf0"], writes=["sf0"], tag="td")
        vop("scalar", "activation", lntd[:, :], srow, AF.Ln,
            reads=["sf0"], writes=["p2t"], tag="lntd")
        dma("syncq", lntd_d[:, :], lntd[:, :], reads=["p2t"], tag="lntd_sv")
        for _ in range(7):
            vop("vector", "tensor_mul", srow, srow, srow,
                reads=["sf0"], writes=["sf0"], tag="tdsq")
        for h in range(2):
            pp = (pA0, pA1)[h]
            mm(pp[:, :], onesrow_f[:, :], srow[:, h * 512:(h + 1) * 512], True, True,
               reads=["onesrow_f_s", "sf0"], writes=[("pA0", "pA1")[h]], tag="tdrep")
            vop("vector", "tensor_copy", td128_rep[:, h * 512:(h + 1) * 512], pp[:, :],
                reads=[("pA0", "pA1")[h]], writes=["td128_rep"])
        dma("syncq", srow, vecs_d[3:4, :], writes=["sf0"])
        vop("scalar", "activation", srow, srow, AF.Sigmoid,
            reads=["sf0"], writes=["sf0"], tag="tf")
        for h in range(2):
            pp = (pA0, pA1)[h]
            mm(pp[:, :], onesrow_f[:, :], srow[:, h * 512:(h + 1) * 512], True, True,
               reads=["onesrow_f_s", "sf0"], writes=[("pA0", "pA1")[h]], tag="tfrep")
            vop("vector", "tensor_copy", tf_rep[:, h * 512:(h + 1) * 512], pp[:, :],
                reads=[("pA0", "pA1")[h]], writes=["tf_rep"])
        for i, row in enumerate((0, 1, 15, 16, 17, 18)):
            dma("syncq", srow, vecs_d[row:row + 1, :], writes=["sf0"], tag=f"vrow{row}")
            for c in range(NC_CH):
                mm(pA0[:, i * 8 + c:i * 8 + c + 1],
                   srow[:, c * 128:(c + 1) * 128], one11[:, :], True, True,
                   reads=["sf0", "one11_s"], writes=["pA0"], tag="colify")
        vop("vector", "tensor_copy", cols[:, :], pA0[:, 0:48],
            reads=["pA0"], writes=["cols"], tag="cols_evac")
        g0c, b0c = cols[:, 0:8], cols[:, 8:16]
        g1c, b1c = cols[:, 16:24], cols[:, 24:32]
        murc, mukc = cols[:, 32:40], cols[:, 40:48]
        dma("syncq", srow, vecs_d[14:15, :], writes=["sf0"])
        vop("scalar", "mul", negbv, srow, -1.0, reads=["sf0"],
            writes=["rows_bf"], tag="rows_bf")

        for t in range(NT):
            dma("syncq", res[t][:, :], x_d[t * 128:(t + 1) * 128, :],
                writes=[f"res{t}"], tag=f"x0_{t}")


        stats_done = {}

        stats_pending = {}
        stats_cnt = [0]

        def stats_flush(tag):
            if tag in stats_pending:
                t, par = stats_pending.pop(tag)
                ag, agn = bnagg[par], f"bnagg{par}"
                vop("vector", "tensor_scalar", stmu[:, t:t + 1], ag[:, 0:1],
                    1.0, None, ALU.mult, reads=[agn], writes=["stmu"],
                    tag=f"{tag}bnm{t}")
                vop("vector", "tensor_scalar", stvar[:, t:t + 1], ag[:, 1:2],
                    1.0, None, ALU.mult, reads=[agn], writes=["stvar"],
                    tag=f"{tag}bnv{t}")

        def stats_tile(tag, t):
            stats_done.setdefault(tag, set()).add(t)
            par = stats_cnt[0] % 2
            stats_cnt[0] += 1
            vop("vector", "bn_stats", bnbuf[:, 0:6], res[t][:, 0:512],
                reads=[f"res{t}"], writes=["bnbuf"], tag=f"{tag}bn{t}")
            vop("vector", "bn_stats", bnbuf2[:, 0:6], res[t][:, 512:1024],
                reads=[f"res{t}"], writes=["bnbuf2"], tag=f"{tag}bn{t}")
            vop("vector", "tensor_copy", bnbuf[:, 6:12], bnbuf2[:, 0:6],
                reads=["bnbuf2"], writes=["bnbuf"], tag=f"{tag}bnc{t}")
            vop("vector", "bn_aggr", bnagg[par][:, :], bnbuf[:, 0:12],
                reads=["bnbuf"], writes=[f"bnagg{par}"], tag=f"{tag}bna{t}")
            stats_flush(tag)
            stats_pending[tag] = (t, par)

        def stats(tag, inner):
            for t in range(NT):
                if t not in stats_done.get(tag, set()):
                    stats_tile(tag, t)
            p._pad("vector")
            p._pad("vector")
            p._pad("vector")
            stats_flush(tag)
            stats_done.pop(tag, None)
            vop("vector", "tensor_scalar", stt1[:, :], stvar[:, :], EPS, None,
                ALU.add, reads=["stvar"], writes=["stt1"])
            vop("scalar", "activation", stt2[:, :], stt1[:, :], AF.Ln,
                reads=["stt1"], writes=["stt2"], tag=f"{tag}ln")
            vop("scalar", "activation", strs[:, :], stt2[:, :], AF.Exp,
                scale=-0.5, reads=["stt2"], writes=["strs"], tag=f"{tag}rs")
            if inner:
                vop("scalar", "activation", stt1[:, :], stt2[:, :], AF.Exp,
                    scale=-1.0, reads=["stt2"], writes=["stt1"], tag=f"{tag}ia")
                vop("vector", "tensor_mul", stt1[:, :], stvar[:, :], stt1[:, :],
                    reads=["stvar", "stt1"], writes=["stt1"])
                vop("vector", "tensor_scalar", stt1[:, :], stt1[:, :], EPS, None,
                    ALU.add, reads=["stt1"], writes=["stt1"])
                vop("scalar", "activation", stt2[:, :], stt1[:, :], AF.Ln,
                    reads=["stt1"], writes=["stt2"], tag=f"{tag}ln2")
                vop("scalar", "activation", stt1[:, :], stt2[:, :], AF.Exp,
                    scale=-0.5, reads=["stt2"], writes=["stt1"], tag=f"{tag}rs2")
                vop("vector", "tensor_mul", strr[:, :], strs[:, :], stt1[:, :],
                    reads=["strs", "stt1"], writes=["strr"])
            vop("vector", "tensor_mul", stnm1[:, :], stmu[:, :], strs[:, :],
                reads=["stmu", "strs"], writes=["stnm1"])
            vop("vector", "tensor_scalar", stnm1[:, :], stnm1[:, :], -1.0, None,
                ALU.mult, reads=["stnm1"], writes=["stnm1"])
            if inner:
                vop("vector", "tensor_mul", stnm2[:, :], stmu[:, :], strr[:, :],
                    reads=["stmu", "strr"], writes=["stnm2"])
                vop("vector", "tensor_scalar", stnm2[:, :], stnm2[:, :], -1.0, None,
                    ALU.mult, reads=["stnm2"], writes=["stnm2"])

        def transpose_tile(src_bf, src_name, dst_bf, dst_name, gcol=None, bcol=None,
                           dst_base=0, tag=""):
            for c in range(NC_CH):
                tr(pT[:, c * 128:(c + 1) * 128], src_bf[:, c * 128:(c + 1) * 128],
                   idbf[:, :], reads=[src_name, "idbf_s"], writes=["pT"], tag=tag)
            if gcol is None:
                vop("vector", "tensor_copy", dst_bf[:, dst_base:dst_base + 1024],
                    pT[:, :], reads=["pT"], writes=[dst_name], tag=tag)
            else:
                for c in range(NC_CH):
                    vop("vector", "tensor_scalar",
                        dst_bf[:, dst_base + c * 128:dst_base + (c + 1) * 128],
                        pT[:, c * 128:(c + 1) * 128],
                        gcol[:, c:c + 1], bcol[:, c:c + 1], ALU.mult, ALU.add,
                        reads=["pT", "cols"], writes=[dst_name], tag=tag)

        def proj1024(fm_bf, fm_name, w_sb, w_name, pair, pair_names, extra=(),
                     tag=""):
            for h in range(2):
                for c in range(NC_CH):
                    mm(pair[h][:, :], fm_bf[:, c * 128:(c + 1) * 128],
                       w_sb[:, c * D + h * 512:c * D + (h + 1) * 512],
                       start=(c == 0), stop=(c == NC_CH - 1 and not extra),
                       reads=[fm_name, w_name], writes=[pair_names[h]], tag=tag)
                for j, (lh, rh, rnames) in enumerate(extra):
                    mm(pair[h][:, :], lh, rh[h], start=False,
                       stop=(j == len(extra) - 1),
                       reads=list(rnames), writes=[pair_names[h]], tag=tag + "x")

        def phase_A(s):
            def tap(name, t, ap, reads):
                if DEBUG_DUMP == name and s == 0:
                    dma("gpq", y_d[t * 128:(t + 1) * 128, :], ap, reads=reads,
                        tag=f"tap{name}{t}")
            stats("A", inner=True)
            if DEBUG_DUMP == "stats" and s == 0:
                dma("gpq", y_d[0:128, 0:NT], stmu[:, :], reads=["stmu"], tag="tapst0")
                dma("gpq", y_d[128:256, 0:NT], stvar[:, :], reads=["stvar"], tag="tapst1")
                dma("gpq", y_d[256:384, 0:NT], strs[:, :], reads=["strs"], tag="tapst2")
                dma("gpq", y_d[384:512, 0:NT], strr[:, :], reads=["strr"], tag="tapst3")
            vop("vector", "memset", carry_f[:, :], 0.0, writes=["carry_f"], tag="Ainit")
            vop("vector", "memset", carry_bf, 0.0, writes=["rows_bf"], tag="Ainit")
            for h in range(2):
                pp, ppn = (pA0, pA1)[h], ("pA0", "pA1")[h]
                mm(pp[:, :], iota1[:, :], lntd[:, h * 512:(h + 1) * 512],
                   True, True, reads=["iota1_s", "p2t"], writes=[ppn],
                   tag=f"A{s}p2o")
                vop("scalar", "activation", p2t[:, h * 512:(h + 1) * 512],
                    pp[:, :], AF.Exp, reads=[ppn], writes=["p2t"], tag=f"A{s}p2e")
            def stageA1(t):
                rn = f"res{t}"
                par = t % 2
                hfm, hfm_n = (sbf[1], "sbf1") if par == 0 else (sbf[4], "sbf4")
                vop("scalar", "activation", sbf[0][:, :], res[t][:, :],
                    AF.Identity, bias=stnm2[:, t:t + 1], scale=strr[:, t:t + 1],
                    reads=[rn, "strr", "stnm2"], writes=["sbf0"], tag=f"A{t}h")
                for c in range(NC_CH):
                    tr(pT[:, c * 128:(c + 1) * 128], sbf[0][:, c * 128:(c + 1) * 128],
                       idbf[:, :], reads=["sbf0", "idbf_s"], writes=["pT"],
                       tag=f"A{t}tr")
                for c in range(NC_CH):
                    vop("scalar", "activation", hfm[:, c * 128:(c + 1) * 128],
                        pT[:, c * 128:(c + 1) * 128], AF.Identity,
                        bias=b0c[:, c:c + 1], scale=g0c[:, c:c + 1],
                        reads=["pT", "cols"], writes=[hfm_n], tag=f"A{t}ge")

            def stageA2a(t):
                rn = f"res{t}"
                par = t % 2
                hfm, hfm_n = (sbf[1], "sbf1") if par == 0 else (sbf[4], "sbf4")
                gbuf, gn = (sbf[3], "sbf3") if par == 0 else (sbf[5], "sbf5")
                proj1024(hfm, hfm_n, wA[0], "wA0", (pA0, pA1), ("pA0", "pA1"),
                         tag=f"A{t}k")
                for h in range(2):
                    vop("scalar", "copy", sf[0][:, h * 512:(h + 1) * 512],
                        (pA0, pA1)[h][:, :], reads=[("pA0", "pA1")[h]],
                        writes=["sf0"], tag=f"A{t}ke")
                proj1024(hfm, hfm_n, wA[1], "wA1", (pB0, pB1), ("pB0", "pB1"),
                         tag=f"A{t}v")
                for h in range(2):
                    vop("vector", "tensor_mul", sf[1][:, h * 512:(h + 1) * 512],
                        sf[0][:, h * 512:(h + 1) * 512], (pB0, pB1)[h][:, :],
                        reads=["sf0", ("pB0", "pB1")[h]], writes=["sf1"],
                        tag=f"A{t}kv")
                proj1024(hfm, hfm_n, wA[2], "wA2", (pA0, pA1), ("pA0", "pA1"),
                         tag=f"A{t}r")
                for h in range(2):
                    vop("scalar", "activation", sf[0][:, h * 512:(h + 1) * 512],
                        (pA0, pA1)[h][:, :], AF.Sigmoid,
                        reads=[("pA0", "pA1")[h]], writes=["sf0"], tag=f"A{t}sr")
                vop("vector", "tensor_mul", sbf[2][:, :], sf[1][:, :], p2t[:, :],
                    reads=["sf1", "p2t"], writes=["sbf2"], tag=f"A{t}w2")
                vop("vector", "tensor_mul", sf[1][:, :], sf[1][:, :], tf_rep[:, :],
                    reads=["sf1", "tf_rep"], writes=["sf1"], tag=f"A{t}t3")
                for h in range(2):
                    ph, phn = (pC0, pC1)[h], ("pC0", "pC1")[h]
                    mm(ph[:, :], usbf[:, :], sbf[2][:, h * 512:(h + 1) * 512],
                       True, False, reads=["usbf_s", "sbf2"], writes=[phn],
                       tag=f"A{t}cs")
                    mm(ph[:, :], onesrow[64:65, :],
                       carry_bf[:, h * 512:(h + 1) * 512],
                       False, True, reads=["onesrow_s", "rows_bf"], writes=[phn],
                       tag=f"A{t}cs")
                for h in range(2):
                    mm(pB0[0:1, :], onescol[:, :], sbf[2][:, h * 512:(h + 1) * 512],
                       True, True, reads=["onescol_s", "sbf2"], writes=["pB0"],
                       tag=f"A{t}col")
                    vop("vector", "tensor_add", carry_f[:, h * 512:(h + 1) * 512],
                        carry_f[:, h * 512:(h + 1) * 512], pB0[0:1, :],
                        reads=["carry_f", "pB0"], writes=["carry_f"], tag=f"A{t}col")
                vop("scalar", "copy", carry_bf, carry_f[:, :],
                    reads=["carry_f"], writes=["rows_bf"], tag=f"A{t}cbf")
                for h in range(2):
                    vop("vector", "tensor_add", sf[1][:, h * 512:(h + 1) * 512],
                        sf[1][:, h * 512:(h + 1) * 512], (pC0, pC1)[h][:, :],
                        reads=["sf1", ("pC0", "pC1")[h]], writes=["sf1"],
                        tag=f"A{t}wkv")
                vop("vector", "tensor_mul", gbuf[:, :], sf[1][:, :], sf[0][:, :],
                    reads=["sf1", "sf0"], writes=[gn], tag=f"A{t}g")
                vop("vector", "tensor_mul", p2t[:, :], p2t[:, :], td128_rep[:, :],
                    reads=["p2t", "td128_rep"], writes=["p2t"], tag=f"A{t}p2")

            def stageA2b(t):
                rn = f"res{t}"
                par = t % 2
                gbuf, gn = (sbf[3], "sbf3") if par == 0 else (sbf[5], "sbf5")
                for c in range(NC_CH):
                    tr(pT2[:, c * 128:(c + 1) * 128],
                       gbuf[:, c * 128:(c + 1) * 128], idbf[:, :],
                       reads=[gn, "idbf_s"], writes=["pT2"], tag=f"A{t}gtr")
                vop("scalar", "copy", sbf[0][:, :], pT2[:, :],
                    reads=["pT2"], writes=["sbf0"], tag=f"A{t}gfe")
                proj1024(sbf[0], "sbf0", wA[3], "wA3", (pA0, pA1), ("pA0", "pA1"),
                         tag=f"A{t}o")
                for h in range(2):
                    vop("vector", "tensor_add", res[t][:, h * 512:(h + 1) * 512],
                        res[t][:, h * 512:(h + 1) * 512], (pA0, pA1)[h][:, :],
                        reads=[rn, ("pA0", "pA1")[h]], writes=[rn], tag=f"A{t}res")
                stats_tile("B", t)

            stageA1(0)
            stageA1(1)
            stageA2a(0)
            for t in range(NT):
                if t + 2 < NT:
                    stageA1(t + 2)
                if t + 1 < NT:
                    stageA2a(t + 1)
                stageA2b(t)

        def phase_B(s):
            def tapB(name, t, ap, reads):
                if DEBUG_DUMP == name and s == 0:
                    dma("gpq", y_d[t * 128:(t + 1) * 128, 0:8], ap, reads=reads,
                        tag=f"tapB{name}{t}")
            def load_half(H):
                e, j = H // 2, H % 2
                slot = H % 3
                src3 = et0_d[e * D:(e + 1) * D, :].rearrange("(c p) d -> p c d",
                                                             p=128)
                dma("gpq", wB[slot][:, :].rearrange("p (c d) -> p c d", d=D),
                    src3[:, j * 4:(j + 1) * 4, :],
                    writes=[f"wB{slot}"], tag=f"B_e{e}h{j}")

            for H in range(3):
                load_half(H)
            if DEBUG_DUMP == "wb0" and s == 0:
                vop("vector", "tensor_copy", sf[1][:, :], wB[0][:, 0:1024],
                    reads=["wB0"], writes=["sf1"], tag="tapwb0")
                dma("syncq", y_d[0:128, :], sf[1][:, :],
                    reads=["sf1"], tag="tapwb0d")
            stats("B", inner=False)

            def stageB1(t):
                rn = f"res{t}"
                vop("scalar", "activation", sbf[0][:, :], res[t][:, :],
                    AF.Identity, bias=stnm1[:, t:t + 1], scale=strs[:, t:t + 1],
                    reads=[rn, "strs", "stnm1"], writes=["sbf0"], tag=f"B{t}xo")
                dst, dname = (wA[0], "wA0") if t < 8 else (wA[1], "wA1")
                for c in range(NC_CH):
                    tr(pT[:, c * 128:(c + 1) * 128],
                       sbf[0][:, c * 128:(c + 1) * 128], idbf[:, :],
                       reads=["sbf0", "idbf_s"], writes=["pT"], tag=f"B{t}tr")
                base = (t % 8) * 1024
                vop("scalar", "copy", dst[:, base:base + 1024], pT[:, :],
                    reads=["pT"], writes=[dname, f"fm{t}"], tag=f"B{t}fme")

            def stageB2(t):
                rn = f"res{t}"
                dst, dname = (wA[0], "wA0") if t < 8 else (wA[1], "wA1")
                for c in range(NC_CH):
                    mm(pA0[:, 0:NE], dst[:, (t % 8) * 1024 + c * 128:
                                         (t % 8) * 1024 + (c + 1) * 128],
                       rt_sb[:, c * NE:(c + 1) * NE], start=(c == 0), stop=False,
                       reads=[f"fm{t}", "rt_sb"], writes=["pA0"], tag=f"B{t}rt")
                mm(pA0[:, 0:NE], onesrow[0:1, :], rb_r[:, :], False, True,
                   reads=["onesrow_s", "rb_r"], writes=["pA0"], tag=f"B{t}rtb")
                vop("vector", "tensor_copy", lg[:, :], pA0[:, 0:NE],
                    reads=["pA0"], writes=["lg"], tag=f"B{t}lg")
                tapB("lgits", t, lg[:, :], ["lg"])
                vop("vector", "reduce_max", rmx[:, :], lg[:, :], AX.X,
                    reads=["lg"], writes=["rmx"])
                vop("vector", "tensor_scalar", lg[:, :], lg[:, :], rmx[:, :], None,
                    ALU.subtract, reads=["lg", "rmx"], writes=["lg"])
                vop("scalar", "activation", lg[:, :], lg[:, :], AF.Exp,
                    reads=["lg"], writes=["lg"], tag=f"B{t}exp")
                vop("vector", "reduce_sum", sme[:, :], lg[:, :], AX.X,
                    reads=["lg"], writes=["sme"])
                vop("scalar", "activation", wnv[:, :], sme[:, :], AF.Ln,
                    reads=["sme"], writes=["wnv"], tag=f"B{t}lnw")
                vop("scalar", "activation", wnv[:, :], wnv[:, :], AF.Exp,
                    scale=-1.0, reads=["wnv"], writes=["wnv"], tag=f"B{t}invw")
                vop("vector", "tensor_scalar", lg[:, :], lg[:, :], 1.0,
                    None, ALU.is_ge, reads=["lg"], writes=["lg"], tag=f"B{t}eq")
                vop("vector", "tensor_scalar", mw_f[:, t * NE:(t + 1) * NE],
                    lg[:, :], wnv[:, :], None, ALU.mult,
                    reads=["lg", "wnv"], writes=["mw_f"], tag=f"B{t}mw")
            for t in range(NT):
                tapB("mw", t, mw_f[:, t * NE:(t + 1) * NE], ["mw_f"])
            def stageB3(t):
                rn = f"res{t}"
                tr(pA1[0:NE, 0:128], mw_f[:, t * NE:(t + 1) * NE], if32[:, :],
                   reads=["mw_f", "if32_s"], writes=["pA1"], tag=f"B{t}mwt")
                vop("vector", "tensor_copy", mwt_bf[:, :], pA1[0:NE, 0:128],
                    reads=["pA1"], writes=["mwt_bf"], tag=f"B{t}mwt")
                for h in range(2):
                    ph, phn = (pC0, pC1)[h], ("pC0", "pC1")[h]
                    mm(ph[:, :], mwt_bf[:, :], rows_bf[0:8, h * 512:(h + 1) * 512],
                       True, True, reads=["mwt_bf", "rows_bf"], writes=[phn],
                       tag=f"B{t}bias")
                    vop("vector", "tensor_add", res[t][:, h * 512:(h + 1) * 512],
                        res[t][:, h * 512:(h + 1) * 512], ph[:, :],
                        reads=[rn, phn], writes=[rn], tag=f"B{t}biasadd")

            pairs = ((pA0, pA1, "pA0", "pA1"), (pB0, pB1, "pB0", "pB1"),
                     (pC0, pC1, "pC0", "pC1"))

            def expertTile(e, t):
                rn = f"res{t}"
                fm = wA[0] if t < 8 else wA[1]
                fmn = f"fm{t}"
                base = (t % 8) * 1024
                q0, q1, qn0, qn1 = pairs[(e * NT + t) % 3]
                for h in range(2):
                    ph, phn = (q0, q1)[h], (qn0, qn1)[h]
                    for c in range(NC_CH):
                        slot = (2 * e + c // 4) % 3
                        mm(ph[:, :], fm[:, base + c * 128:base + (c + 1) * 128],
                           wB[slot][:, (c % 4) * D + h * 512:
                                     (c % 4) * D + (h + 1) * 512],
                           start=(c == 0), stop=(c == NC_CH - 1),
                           reads=[fmn, f"wB{slot}"], writes=[phn],
                           tag=f"B{t}e{e}")
                    vop("vector", "scalar_tensor_tensor",
                        res[t][:, h * 512:(h + 1) * 512], ph[:, :],
                        mw_f[:, t * NE + e:t * NE + e + 1],
                        res[t][:, h * 512:(h + 1) * 512], ALU.mult, ALU.add,
                        reads=[phn, "mw_f", rn], writes=[rn], tag=f"B{t}e{e}acc")
                if e == NE - 1:
                    stats_tile("C", t)
                if t == NT - 1:
                    for H in (2 * e + 3, 2 * e + 4):
                        if H < 2 * NE:
                            load_half(H)

            dobias = DEBUG_DUMP not in ("nobias", "e0")
            doexp = DEBUG_DUMP != "noexp"
            stageB1(0)
            for t in range(NT):
                if t + 1 < NT:
                    stageB1(t + 1)
                stageB2(t)
                if doexp:
                    expertTile(0, t)
                if dobias:
                    stageB3(t)
            exp_rng = range(1, NE)
            if DEBUG_DUMP == "noexp":
                exp_rng = range(0)
            elif DEBUG_DUMP == "e0":
                exp_rng = range(0)
            for e in exp_rng:
                if e == NE - 1:
                    dma("gpq", wB[1][:, :].rearrange("p (c d) -> p c d", d=D),
                        wv1_d.rearrange("(c p) d -> p c d", p=128)[:, 0:4, :],
                        writes=["wB1"], tag="C_wv0pre")
                    vop("scalar", "mul", wB[1][:, :], wB[1][:, :], -1.0,
                        reads=["wB1"], writes=["wB1"], tag="C_negwv0")
                for t in range(NT):
                    expertTile(e, t)

        def phase_C(s):
            dma("gpq", wB[0][:, :].rearrange("p (c d) -> p c d", d=D),
                wv1_d.rearrange("(c p) d -> p c d", p=128)[:, 4:8, :],
                writes=["wB0"], tag="C_wv1")
            vop("scalar", "mul", wB[0][:, :], wB[0][:, :], -1.0,
                reads=["wB0"], writes=["wB0"], tag="C_negwv1")
            stats("C", inner=True)
            vop("vector", "memset", w0t_f[0:64, :], 0.0, writes=["w0t_f"],
                tag="Cinit")
            vop("vector", "memset", w0t_bf[:, :], 0.0, writes=["w0t_bf"], tag="Cinit")

            def stage1(t):
                rn = f"res{t}"
                par = t % 2
                k_bf, k_n = k_bfp[par], f"k_bf{par}"
                q_bf, q_n = q_bfp[par], f"q_bf{par}"
                pt_bf, pt_n = pt_bfp[par], f"pt_bf{par}"
                qt_bf, qt_n = qt_bfp[par], f"qt_bf{par}"
                gt_bf, gt_n = gt_bfp[par], f"gt_bf{par}"
                hfm, hfm_n = (sbf[1], "sbf1") if par == 0 else (sbf[4], "sbf4")
                vop("scalar", "activation", sf[1][:, :], res[t][:, :], AF.Identity,
                    bias=stnm1[:, t:t + 1], scale=strs[:, t:t + 1],
                    reads=[rn, "strs", "stnm1"], writes=["sf1"], tag=f"C{t}xo")
                vop("scalar", "activation", sbf[0][:, :], res[t][:, :], AF.Identity,
                    bias=stnm2[:, t:t + 1], scale=strr[:, t:t + 1],
                    reads=[rn, "strr", "stnm2"], writes=["sbf0"], tag=f"C{t}h")
                vop("vector", "tensor_add", res[t][:, :], res[t][:, :], sf[1][:, :],
                    reads=[rn, "sf1"], writes=[rn], tag=f"C{t}rx")
                for c in range(NC_CH):
                    tr(pT[:, c * 128:(c + 1) * 128], sbf[0][:, c * 128:(c + 1) * 128],
                       idbf[:, :], reads=["sbf0", "idbf_s"], writes=["pT"],
                       tag=f"C{t}tr")
                for c in range(NC_CH):
                    vop("scalar", "activation", hfm[:, c * 128:(c + 1) * 128],
                        pT[:, c * 128:(c + 1) * 128], AF.Identity,
                        bias=b1c[:, c:c + 1], scale=g1c[:, c:c + 1],
                        reads=["pT", "cols"], writes=[hfm_n], tag=f"C{t}ge")
                for j, (pq, pqn, dst, dn) in enumerate(
                        ((pA0, "pA0", k_bf, k_n), (pA1, "pA1", q_bf, q_n))):
                    for c in range(NC_CH):
                        mm(pq[0:64, 0:128],
                           wkq_sb[:, c * 128 + j * 64:c * 128 + j * 64 + 64],
                           hfm[:, c * 128:(c + 1) * 128],
                           start=(c == 0), stop=False,
                           reads=["wkq_sb", hfm_n], writes=[pqn], tag=f"C{t}kq")
                    mm(pq[0:64, 0:128], bkq_r[:, j * 64:j * 64 + 64],
                       onesrow[0:1, :], False, True,
                       reads=["bkq_r", "onesrow_s"], writes=[pqn], tag=f"C{t}kqb")
                    vop("scalar", "copy", dst[:, :], pq[0:64, 0:128],
                        reads=[pqn], writes=[dn], tag=f"C{t}kqe")
                mm(pA0[:, 0:128], k_bf[:, :], k_bf[:, :], True, True,
                   reads=[k_n], writes=["pA0"], tag=f"C{t}S")
                vop("vector", "scalar_tensor_tensor", cX[:, :], pA0[:, 0:128], -LR,
                    mubd[:, :], ALU.mult, ALU.mult,
                    reads=["pA0", "mubd_s"], writes=["cX"], tag=f"C{t}X0")
                vop("vector", "scalar_tensor_tensor", cY[:, :], pA0[:, 0:128], -LR,
                    mlbd[:, :], ALU.mult, ALU.mult,
                    reads=["pA0", "mlbd_s"], writes=["cY"], tag=f"C{t}Y0")
                vop("vector", "scalar_tensor_tensor", cSb[:, :], pA0[:, 0:128], -LR,
                    mlbs[:, :], ALU.mult, ALU.mult,
                    reads=["pA0", "mlbs_s"], writes=["cSb"], tag=f"C{t}Sb")
                mm(pA1[:, 0:128], k_bf[:, :], q_bf[:, :], True, True,
                   reads=[k_n, q_n], writes=["pA1"], tag=f"C{t}KQ")
                vop("vector", "scalar_tensor_tensor", gt_bf[:, :], pA1[:, 0:128],
                    -1.0, mus[:, :], ALU.mult, ALU.mult,
                    reads=["pA1", "mus_s"], writes=[gt_n], tag=f"C{t}Gt")
                vop("vector", "tensor_add", cRa[:, :], cX[:, :], if32[:, :],
                    reads=["cX", "if32_s"], writes=["cRa"], tag=f"C{t}P0")

            def tbuild(t):
                par = t % 2
                pt_bf, pt_n = pt_bfp[par], f"pt_bf{par}"
                qt_bf, qt_n = qt_bfp[par], f"qt_bf{par}"
                Rcur, Rnam, Roth, Rothn = cRa, "cRa", cRb, "cRb"
                for k in range(1, 4):
                    mm(pA1[:, 0:128], cX[:, :], cY[:, :], True, True,
                       reads=["cX", "cY"], writes=["pA1"], tag=f"C{t}Y2_{k}")
                    if k < 3:
                        mm(pA0[:, 0:128], cY[:, :], cX[:, :], True, True,
                           reads=["cY", "cX"], writes=["pA0"], tag=f"C{t}X2_{k}")
                    vop("vector", "tensor_add", cIY[:, :], pA1[:, 0:128],
                        if32[:, :], reads=["pA1", "if32_s"], writes=["cIY"],
                        tag=f"C{t}IY{k}")
                    if k < 3:
                        vop("vector", "tensor_copy", cY[:, :], pA1[:, 0:128],
                            reads=["pA1"], writes=["cY"], tag=f"C{t}Ye{k}")
                        vop("scalar", "copy", cX[:, :], pA0[:, 0:128],
                            reads=["pA0"], writes=["cX"], tag=f"C{t}Xe{k}")
                    yield
                    mm(pB0[:, 0:128], cIY[:, :], Rcur[:, :], True, True,
                       reads=["cIY", Rnam], writes=["pB0"], tag=f"C{t}P{k}")
                    vop("vector", "tensor_copy", Roth[:, :], pB0[:, 0:128],
                        reads=["pB0"], writes=[Rothn], tag=f"C{t}Pe{k}")
                    if k == 3:
                        vop("vector", "tensor_copy", pt_bf[:, :], pB0[:, 0:128],
                            reads=["pB0"], writes=[pt_n], tag=f"C{t}Ptb")
                    Rcur, Rnam, Roth, Rothn = Roth, Rothn, Rcur, Rnam
                    yield
                PT3, PT3n, Qs, Qsn = Rcur, Rnam, Roth, Rothn
                mm(pA0[:, 0:128], cSb[:, :], PT3[:, :], True, True,
                   reads=["cSb", PT3n], writes=["pA0"], tag=f"C{t}Mt")
                mm(pA1[:, 0:128], PT3[:, :], cSb[:, :], True, True,
                   reads=[PT3n, "cSb"], writes=["pA1"], tag=f"C{t}M")
                vop("scalar", "copy", cX[:, :], pA0[:, 0:128],
                    reads=["pA0"], writes=["cX"], tag=f"C{t}Mte")
                vop("scalar", "copy", cY[:, :], pA1[:, 0:128],
                    reads=["pA1"], writes=["cY"], tag=f"C{t}Me")
                yield
                vop("vector", "tensor_add", Qs[:, :], cX[:, :], if32[:, :],
                    reads=["cX", "if32_s"], writes=[Qsn], tag=f"C{t}Q0")
                mm(pA0[:, 0:128], cX[:, :], cY[:, :], True, True,
                   reads=["cX", "cY"], writes=["pA0"], tag=f"C{t}M2")
                mm(pA1[:, 0:128], cY[:, :], cX[:, :], True, True,
                   reads=["cY", "cX"], writes=["pA1"], tag=f"C{t}M2t")
                vop("vector", "tensor_add", cIY[:, :], pA0[:, 0:128], if32[:, :],
                    reads=["pA0", "if32_s"], writes=["cIY"], tag=f"C{t}IM2")
                vop("vector", "tensor_copy", cSb[:, :], pA0[:, 0:128],
                    reads=["pA0"], writes=["cSb"], tag=f"C{t}M2e")
                vop("scalar", "copy", cX[:, :], pA1[:, 0:128],
                    reads=["pA1"], writes=["cX"], tag=f"C{t}M2te")
                yield
                mm(pB0[:, 0:128], cIY[:, :], Qs[:, :], True, True,
                   reads=["cIY", Qsn], writes=["pB0"], tag=f"C{t}Q1")
                vop("vector", "tensor_copy", PT3[:, :], pB0[:, 0:128],
                    reads=["pB0"], writes=[PT3n], tag=f"C{t}Q1e")
                yield
                mm(pA0[:, 0:128], cX[:, :], cSb[:, :], True, True,
                   reads=["cX", "cSb"], writes=["pA0"], tag=f"C{t}M4")
                vop("vector", "tensor_add", cIY[:, :], pA0[:, 0:128], if32[:, :],
                    reads=["pA0", "if32_s"], writes=["cIY"], tag=f"C{t}IM4")
                yield
                mm(pB0[:, 0:128], cIY[:, :], PT3[:, :], True, True,
                   reads=["cIY", PT3n], writes=["pB0"], tag=f"C{t}Q2")
                vop("scalar", "copy", qt_bf[:, :], pB0[:, 0:128],
                    reads=["pB0"], writes=[qt_n], tag=f"C{t}Q2e")

            def stage2(t):
                rn = f"res{t}"
                par = t % 2
                k_bf, k_n = k_bfp[par], f"k_bf{par}"
                q_bf, q_n = q_bfp[par], f"q_bf{par}"
                pt_bf, pt_n = pt_bfp[par], f"pt_bf{par}"
                qt_bf, qt_n = qt_bfp[par], f"qt_bf{par}"
                gt_bf, gt_n = gt_bfp[par], f"gt_bf{par}"
                hfm, hfm_n = (sbf[1], "sbf1") if par == 0 else (sbf[4], "sbf4")
                for h in range(2):
                    ph, phn = (pC0, pC1)[h], ("pC0", "pC1")[h]
                    mm(ph[:, :], k_bf[:, :], w0t_bf[:, h * 512:(h + 1) * 512],
                       True, False, reads=[k_n, "w0t_bf"], writes=[phn],
                       tag=f"C{t}BKW")
                    for c in range(NC_CH):
                        slot = 1 - c // 4
                        mm(ph[:, :], hfm[:, c * 128:(c + 1) * 128],
                           wB[slot][:, (c % 4) * D + h * 512:
                                     (c % 4) * D + (h + 1) * 512],
                           start=False, stop=False,
                           reads=[hfm_n, f"wB{slot}"], writes=[phn],
                           tag=f"C{t}BV")
                    mm(ph[:, :], onesrow[32:33, :],
                       negbv[:, h * 512:(h + 1) * 512],
                       False, True, reads=["onesrow_s", "rows_bf"], writes=[phn],
                       tag=f"C{t}Bb")
                    vop("scalar", "mul", sbf[2][:, h * 512:(h + 1) * 512],
                        ph[:, :], LR, reads=[phn], writes=["sbf2"], tag=f"C{t}Bs")
                    yield
                for h in range(2):
                    ph, phn = (pC0, pC1)[h], ("pC0", "pC1")[h]
                    mm(ph[:, :], pt_bf[:, :], sbf[2][:, h * 512:(h + 1) * 512],
                       True, True, reads=[pt_n, "sbf2"], writes=[phn],
                       tag=f"C{t}U1")
                    vop("scalar", "copy", sbf[0][:, h * 512:(h + 1) * 512],
                        ph[:, :], reads=[phn], writes=["sbf0"], tag=f"C{t}U1e")
                yield
                for h in range(2):
                    ph, phn = (pC0, pC1)[h], ("pC0", "pC1")[h]
                    mm(ph[:, :], qt_bf[:, :], sbf[0][:, h * 512:(h + 1) * 512],
                       True, True, reads=[qt_n, "sbf0"], writes=[phn],
                       tag=f"C{t}U")
                    vop("vector", "tensor_copy", sbf[3][:, h * 512:(h + 1) * 512],
                        ph[:, :], reads=[phn], writes=["sbf3"], tag=f"C{t}Ue")
                yield
                for h in range(2):
                    ph, phn = (pB1, pC0)[h], ("pB1", "pC0")[h]
                    mm(ph[:, :], q_bf[:, :], w0t_bf[:, h * 512:(h + 1) * 512],
                       True, False, reads=[q_n, "w0t_bf"], writes=[phn],
                       tag=f"C{t}YQ")
                    mm(ph[:, :], gt_bf[:, :], sbf[3][:, h * 512:(h + 1) * 512],
                       False, True, reads=[gt_n, "sbf3"], writes=[phn],
                       tag=f"C{t}YG")
                    vop("vector", "tensor_add", res[t][:, h * 512:(h + 1) * 512],
                        res[t][:, h * 512:(h + 1) * 512], ph[:, :],
                        reads=[rn, phn], writes=[rn], tag=f"C{t}ry")
                    yield
                tr(pT2[:, 0:64], k_bf[:, :], idbf[0:64, 0:64],
                   reads=[k_n, "idbf_s"], writes=["pT2"], tag=f"C{t}ktr")
                vop("scalar", "copy", ktm_bf[:, :], pT2[0:128, 0:64],
                    reads=["pT2"], writes=["ktm_bf"], tag=f"C{t}kte")
                yield
                for h in range(2):
                    ph, phn = (pC1, pB1)[h], ("pC1", "pB1")[h]
                    mm(ph[0:64, :], ktm_bf[:, :], sbf[3][:, h * 512:(h + 1) * 512],
                       True, True, reads=["ktm_bf", "sbf3"], writes=[phn],
                       tag=f"C{t}KtU")
                    vop("vector", "scalar_tensor_tensor",
                        w0t_bf[:, h * 512:(h + 1) * 512], ph[0:64, :], -1.0,
                        w0t_f[0:64, h * 512:(h + 1) * 512], ALU.mult, ALU.add,
                        reads=[phn, "w0t_f"], writes=["w0t_bf"], tag=f"C{t}Wbf")
                    vop("vector", "scalar_tensor_tensor",
                        w0t_f[0:64, h * 512:(h + 1) * 512], ph[0:64, :], -1.0,
                        w0t_f[0:64, h * 512:(h + 1) * 512], ALU.mult, ALU.add,
                        reads=[phn, "w0t_f"], writes=["w0t_f"], tag=f"C{t}Wf")
                    yield
                stats_tile("D", t)

            stage1(0)
            for _ in tbuild(0):
                pass
            for t in range(NT):
                gens = []
                if t + 1 < NT:
                    stage1(t + 1)
                    gens.append(tbuild(t + 1))
                gens.append(stage2(t))
                while gens:
                    for g in list(gens):
                        try:
                            next(g)
                        except StopIteration:
                            gens.remove(g)

        def phase_D(s):
            load_w1024("wA0", wA[0], wr2_d, "wr2",
                       extra=tuple(f"fm{t}" for t in range(8)))
            load_w1024("wA1", wA[1], wk2_d, "wk2",
                       extra=tuple(f"fm{t}" for t in range(8, 16)))
            load_w1024("wA2", wA[2], wv2_d, "wv2")
            if s == 0:
                dma("gpq", lntd[:, :], lntd_d[:, :], writes=["p2t"],
                    tag="lntd_rd")
            stats("D", inner=False)

            def x4c(j):
                if j < 6:
                    return wB[j // 2], f"wB{j // 2}", (j % 2) * T_SEQ
                return wA[3], "wA3", (j - 6) * T_SEQ

            def slots(t):
                if t % 2 == 0:
                    return (sbf[0], "sbf0", sbf[1], "sbf1")
                return (sbf[2], "sbf2", sbf[3], "sbf3")

            def stageD1(t):
                rn = f"res{t}"
                t0 = t * 128
                xr, xr_n, xk, xk_n = slots(t)
                vop("scalar", "activation", xr[:, :], res[t][:, :], AF.Identity,
                    bias=stnm1[:, t:t + 1], scale=strs[:, t:t + 1],
                    reads=[rn, "strs", "stnm1"], writes=[xr_n], tag=f"D{t}xo")
                for c in range(NC_CH):
                    tr(pT[:, c * 128:(c + 1) * 128], xr[:, c * 128:(c + 1) * 128],
                       idbf[:, :], reads=[xr_n, "idbf_s"], writes=["pT"],
                       tag=f"D{t}tr")
                for c in range(NC_CH):
                    dst, dn, db = x4c(c)
                    vop("scalar", "copy", dst[:, db + t0:db + t0 + 128],
                        pT[:, c * 128:(c + 1) * 128], reads=["pT"], writes=[dn],
                        tag=f"D{t}x4")
                for c in range(NC_CH):
                    src_, sn, db = x4c(c)
                    xo = src_[:, db + t0:db + t0 + 128]
                    if t == 0:
                        xop = src_[:, db + t0:db + t0 + 127]
                        vop("vector", "tensor_sub",
                            xk[:, c * 128 + 1:(c + 1) * 128],
                            src_[:, db + t0 + 1:db + t0 + 128], xop,
                            reads=[sn], writes=[xk_n], tag=f"D{t}d{c}")
                        vop("vector", "tensor_copy", xk[:, c * 128:c * 128 + 1],
                            xo[:, 0:1], reads=[sn], writes=[xk_n], tag=f"D{t}d0{c}")
                        vop("vector", "scalar_tensor_tensor",
                            xr[:, c * 128 + 1:(c + 1) * 128],
                            xk[:, c * 128 + 1:(c + 1) * 128], murc[:, c:c + 1],
                            xop, ALU.mult, ALU.add,
                            reads=[xk_n, "cols", sn], writes=[xr_n],
                            tag=f"D{t}mr{c}")
                        vop("vector", "tensor_scalar", xr[:, c * 128:c * 128 + 1],
                            xk[:, c * 128:c * 128 + 1], murc[:, c:c + 1], None,
                            ALU.mult, reads=[xk_n, "cols"], writes=[xr_n],
                            tag=f"D{t}mr0{c}")
                        vop("vector", "scalar_tensor_tensor",
                            xk[:, c * 128 + 1:(c + 1) * 128],
                            xk[:, c * 128 + 1:(c + 1) * 128], mukc[:, c:c + 1],
                            xop, ALU.mult, ALU.add,
                            reads=[xk_n, "cols", sn], writes=[xk_n],
                            tag=f"D{t}mk{c}")
                        vop("vector", "tensor_scalar", xk[:, c * 128:c * 128 + 1],
                            xk[:, c * 128:c * 128 + 1], mukc[:, c:c + 1], None,
                            ALU.mult, reads=[xk_n, "cols"], writes=[xk_n],
                            tag=f"D{t}mk0{c}")
                    else:
                        xop = src_[:, db + t0 - 1:db + t0 + 127]
                        vop("vector", "tensor_sub", xk[:, c * 128:(c + 1) * 128],
                            xo, xop, reads=[sn], writes=[xk_n], tag=f"D{t}d{c}")
                        vop("vector", "scalar_tensor_tensor",
                            xr[:, c * 128:(c + 1) * 128],
                            xk[:, c * 128:(c + 1) * 128], murc[:, c:c + 1],
                            xop, ALU.mult, ALU.add,
                            reads=[xk_n, "cols", sn], writes=[xr_n],
                            tag=f"D{t}mr{c}")
                        vop("vector", "scalar_tensor_tensor",
                            xk[:, c * 128:(c + 1) * 128],
                            xk[:, c * 128:(c + 1) * 128], mukc[:, c:c + 1],
                            xop, ALU.mult, ALU.add,
                            reads=[xk_n, "cols", sn], writes=[xk_n],
                            tag=f"D{t}mk{c}")

            def stageD2(t):
                rn = f"res{t}"
                t0 = t * 128
                xr, xr_n, xk, xk_n = slots(t)
                proj1024(xr, xr_n, wA[0], "wA0", (pA0, pA1), ("pA0", "pA1"),
                         tag=f"D{t}r")
                for h in range(2):
                    vop("scalar", "activation", sf[0][:, h * 512:(h + 1) * 512],
                        (pA0, pA1)[h][:, :], AF.Sigmoid,
                        reads=[("pA0", "pA1")[h]], writes=["sf0"], tag=f"D{t}sig")
                proj1024(xk, xk_n, wA[1], "wA1", (pB0, pB1), ("pB0", "pB1"),
                         tag=f"D{t}k")
                for h in range(2):
                    vop("scalar", "activation", sf[1][:, h * 512:(h + 1) * 512],
                        (pB0, pB1)[h][:, :], AF.Relu,
                        reads=[("pB0", "pB1")[h]], writes=["sf1"], tag=f"D{t}relu")
                vop("vector", "tensor_mul", sbf[4][:, :], sf[1][:, :], sf[1][:, :],
                    reads=["sf1"], writes=["sbf4"], tag=f"D{t}k2")
                for c in range(NC_CH):
                    tr(pT2[:, c * 128:(c + 1) * 128],
                       sbf[4][:, c * 128:(c + 1) * 128], idbf[:, :],
                       reads=["sbf4", "idbf_s"], writes=["pT2"], tag=f"D{t}ktr")
                vop("scalar", "copy", sbf[5][:, :], pT2[:, :],
                    reads=["pT2"], writes=["sbf5"], tag=f"D{t}kfe")
                proj1024(sbf[5], "sbf5", wA[2], "wA2", (pC0, pC1), ("pC0", "pC1"),
                         tag=f"D{t}v")
                for h in range(2):
                    vop("vector", "tensor_mul", sf[1][:, h * 512:(h + 1) * 512],
                        (pC0, pC1)[h][:, :], sf[0][:, h * 512:(h + 1) * 512],
                        reads=[("pC0", "pC1")[h], "sf0"], writes=["sf1"],
                        tag=f"D{t}rv")
                vop("vector", "tensor_add", res[t][:, :], res[t][:, :], sf[1][:, :],
                    reads=[rn, "sf1"], writes=[rn], tag=f"D{t}res")
                dma("syncq", y_d[s * T_SEQ + t0:s * T_SEQ + t0 + 128, :],
                    res[t][:, :], reads=[rn], tag=f"D{t}out")
                if s == 0:
                    dma("syncq", res[t][:, :],
                        x_d[T_SEQ + t0:T_SEQ + t0 + 128, :],
                        writes=[rn], tag=f"x1_{t}")
                    if t >= 2:
                        stats_tile("A", t - 2)

            stageD1(0)
            for t in range(NT):
                if t + 1 < NT:
                    stageD1(t + 1)
                stageD2(t)
            if s == 0:
                stats_tile("A", NT - 2)
                stats_tile("A", NT - 1)

        for s in range(2):
            if s == 1:
                load_w1024("wA0", wA[0], wk0_d, "wk0b",
                           extra=tuple(f"fm{t}" for t in range(8)))
                load_w1024("wA1", wA[1], wv0_d, "wv0b",
                           extra=tuple(f"fm{t}" for t in range(8, 16)))
                load_w1024("wA2", wA[2], wr0_d, "wr0b")
                load_w1024("wA3", wA[3], wo0_d, "wo0b")
            if "A" in PHASES:
                phase_A(s)
            if "B" in PHASES:
                phase_B(s)
            if "C" in PHASES:
                phase_C(s)
            if "D" in PHASES:
                phase_D(s)
            if "D" not in PHASES and DEBUG_DUMP in (None, "noexp", "nobias", "e0"):
                for t in range(NT):
                    dma("syncq", y_d[s * T_SEQ + t * 128:s * T_SEQ + (t + 1) * 128, :],
                        res[t][:, :], reads=[f"res{t}"], tag=f"dbgout{t}")

        p.finalize(nc)
    return nc


def _blockmask(f, upper):
    m = np.zeros((128, 128), f)
    for i in range(8):
        b = np.triu(np.ones((16, 16), f), 1) if upper else \
            np.tril(np.ones((16, 16), f), -1)
        m[i * 16:(i + 1) * 16, i * 16:(i + 1) * 16] = b
    return m


def _blockstrict(f):
    m = np.zeros((128, 128), f)
    for i in range(8):
        m[i * 16:, :i * 16][:16] = 1.0 if False else m[i * 16:, :i * 16][:16]
    for bi in range(8):
        for bj in range(bi):
            m[bi * 16:(bi + 1) * 16, bj * 16:(bj + 1) * 16] = 1.0
    return m


def _prep_inputs(inputs):
    f = np.float32
    c = np.ascontiguousarray

    def T(a):
        return c(np.asarray(a).T.astype(f, copy=False))

    vecs = np.zeros((19, D), f)
    vecs[0] = inputs["L0_attn_ln_gamma"]
    vecs[1] = inputs["L0_attn_ln_beta"]
    vecs[2] = inputs["L0_attn_time_decay"]
    vecs[3] = inputs["L0_attn_time_first"]
    vecs[4, :NE] = inputs["L0_ffn_router_b"]
    vecs[5:13] = inputs["L0_ffn_expert_b"]
    vecs[13, :128] = np.concatenate([inputs["L1_attn_bk"], inputs["L1_attn_bq"]])
    vecs[14] = inputs["L1_attn_bv"]
    vecs[15] = inputs["L1_attn_gamma"]
    vecs[16] = inputs["L1_attn_beta"]
    vecs[17] = inputs["L1_ffn_mu_r"]
    vecs[18] = inputs["L1_ffn_mu_k"]

    shared = dict(
        wk0t=T(inputs["L0_attn_Wk"]),
        wv0t=T(inputs["L0_attn_Wv"]),
        wr0t=T(inputs["L0_attn_Wr"]),
        wo0t=T(inputs["L0_attn_Wo"]),
        rt0=T(inputs["L0_ffn_router_w"]),
        et0=c(np.asarray(inputs["L0_ffn_expert_w"]).transpose(0, 2, 1)
              .reshape(NE * D, D).astype(f, copy=False)),
        wkq1t=c(np.concatenate([np.asarray(inputs["L1_attn_Wk"]).T,
                                np.asarray(inputs["L1_attn_Wq"]).T],
                               axis=1).astype(f, copy=False)),
        wv1t=T(inputs["L1_attn_Wv"]),
        wr2t=T(inputs["L1_ffn_W_r"]),
        wk2t=T(inputs["L1_ffn_W_k"]),
        wv2t=T(inputs["L1_ffn_W_v"]),
        vecs=vecs,
        idbf=np.eye(128, dtype=BF),
        usbf=np.triu(np.ones((128, 128), BF), 1),
        onescol=np.ones((128, 1), BF),
        onesrow=np.ones((65, 128), BF),
        onesrow_f=np.ones((1, 128), f),
        iota1=np.arange(1, 129, dtype=f).reshape(1, 128),
        mubd=_blockmask(BF, upper=True),
        mlbd=_blockmask(BF, upper=False),
        mlbs=_blockstrict(BF),
        mus_f=np.triu(np.ones((128, 128), BF), 1),
        if32=np.eye(128, dtype=f),
        one11_f=np.ones((1, 1), f),
    )
    x = np.asarray(inputs["L0_attn_x"]).astype(f, copy=False)
    in_maps = []
    for i in range(N_CORES):
        m = dict(shared)
        m["x"] = c(x[2 * i:2 * i + 2].reshape(2 * T_SEQ, D))
        in_maps.append(m)
    return in_maps


def kernel(**inputs):
    in_maps = _prep_inputs(inputs)
    nc = build_program()
    res = run_bass_kernel_spmd(nc, in_maps, core_ids=list(range(N_CORES)))
    out = np.empty((16, T_SEQ, D), np.float32)
    for i in range(N_CORES):
        out[2 * i:2 * i + 2] = res.results[i]["y"].reshape(2, T_SEQ, D)
    return out


# version 15 (final)
# speedup vs baseline: 1.0734x; 1.0234x over previous
import contextlib

import numpy as np
import ml_dtypes

import concourse.bass as bass
import concourse.mybir as mybir
from concourse.bass_utils import run_bass_kernel_spmd

F32 = mybir.dt.float32
BF16 = mybir.dt.bfloat16
AF = mybir.ActivationFunctionType
ALU = mybir.AluOpType
AX = mybir.AxisListType
BF = ml_dtypes.bfloat16

N_CORES = 8
T_SEQ = 2048
NT = T_SEQ // 128
D = 1024
NC_CH = D // 128
NE = 8
LR = 0.01
EPS = 1e-5

PHASES = "ABCD"
DEBUG_DUMP = None

COMPUTE_ENG = ("tensor", "vector", "scalar")
QUEUE_ENG = ("syncq", "gpq")


class Prog:

    def __init__(self):
        self.ops = []
        self.eng_ops = {e: [] for e in COMPUTE_ENG + QUEUE_ENG}
        self.last_w = {}
        self.readers = {}
        self.pad_fn = {}

    def _pad(self, eng):
        return self.emit(eng, self.pad_fn[eng], tag="hazpad")

    def emit(self, eng, fn, reads=(), writes=(), tag=""):
        if eng in self.pad_fn and self.eng_ops.get(eng):
            pw = self.ops[self.eng_ops[eng][-1]]["writes"]
            if any(b in pw for b in reads):
                self._pad(eng)
        rdeps = set()
        wdeps = set()
        for b in reads:
            w = self.last_w.get(b)
            if w is not None:
                rdeps.add(w)
        for b in writes:
            for r in self.readers.get(b, ()):
                wdeps.add(r)
            w = self.last_w.get(b)
            if w is not None:
                rdeps.add(w)
        rdeps = {d for d in rdeps if self.ops[d]["eng"] != eng}
        wdeps = {d for d in wdeps if self.ops[d]["eng"] != eng}
        deps = set(wdeps)
        for d in rdeps:
            te = self.ops[d]["eng"]
            if te in self.pad_fn and not self.ops[d]["dma"]:
                lst = self.eng_ops[te]
                k = self.ops[d]["idx_on_eng"]
                if k + 1 >= len(lst):
                    self._pad(te)
                    lst = self.eng_ops[te]
                deps.add(lst[k + 1])
            else:
                deps.add(d)
        opid = len(self.ops)
        self.ops.append(dict(eng=eng, fn=fn, deps=deps, dma=eng in QUEUE_ENG,
                             tag=tag, writes=tuple(writes),
                             idx_on_eng=len(self.eng_ops[eng])))
        self.eng_ops[eng].append(opid)
        for b in reads:
            self.readers.setdefault(b, []).append(opid)
        for b in writes:
            self.readers[b] = [opid]
            self.last_w[b] = opid
        return opid

    def finalize(self, nc):
        marked = [op["dma"] for op in self.ops]
        for op in self.ops:
            for d in op["deps"]:
                marked[d] = True
        val = [0] * len(self.ops)
        for eng in COMPUTE_ENG + QUEUE_ENG:
            c = 0
            for i in self.eng_ops[eng]:
                if marked[i]:
                    c += 16 if self.ops[i]["dma"] else 1
                val[i] = c
        totals = {eng: (val[self.eng_ops[eng][-1]] if self.eng_ops[eng] else 0)
                  for eng in COMPUTE_ENG + QUEUE_ENG}

        with contextlib.ExitStack() as es:
            sems = {e: es.enter_context(nc.semaphore("sem_" + e))
                    for e in COMPUTE_ENG + QUEUE_ENG}
            block = es.enter_context(nc.Block())

            def replay(eng_name, eng):
                last_wait = {}
                for i in self.eng_ops[eng_name]:
                    op = self.ops[i]
                    for d in sorted(op["deps"]):
                        te = self.ops[d]["eng"]
                        v = val[d]
                        if last_wait.get(te, 0) < v:
                            eng.wait_ge(sems[te], v)
                            last_wait[te] = v
                    inst = op["fn"](eng)
                    if marked[i]:
                        inst.then_inc(sems[eng_name], 16 if op["dma"] else 1)

            @block.tensor
            def _(eng):
                replay("tensor", eng)

            @block.vector
            def _(eng):
                replay("vector", eng)

            @block.scalar
            def _(eng):
                replay("scalar", eng)

            @block.sync
            def _(eng):
                replay("syncq", eng)
                if totals["syncq"]:
                    eng.wait_ge(sems["syncq"], totals["syncq"])

            @block.gpsimd
            def _(eng):
                replay("gpq", eng)
                if totals["gpq"]:
                    eng.wait_ge(sems["gpq"], totals["gpq"])


def build_program():
    nc = bass.Bass("TRN2", target_bir_lowering=False)
    p = Prog()

    def din(name, shape, dt=F32):
        return nc.dram_tensor(name, list(shape), dt, kind="ExternalInput").ap()

    x_d = din("x", (2 * T_SEQ, D))
    y_d = nc.dram_tensor("y", [2 * T_SEQ, D], F32, kind="ExternalOutput").ap()
    lntd_d = nc.dram_tensor("lntd_hbm", [1, D], F32, kind="Internal").ap()
    wk0_d = din("wk0t", (D, D))
    wv0_d = din("wv0t", (D, D))
    wr0_d = din("wr0t", (D, D))
    wo0_d = din("wo0t", (D, D))
    rt0_d = din("rt0", (D, NE))
    et0_d = din("et0", (NE * D, D))
    wkq1_d = din("wkq1t", (D, 128))
    wv1_d = din("wv1t", (D, D))
    wr2_d = din("wr2t", (D, D))
    wk2_d = din("wk2t", (D, D))
    wv2_d = din("wv2t", (D, D))
    vecs_d = din("vecs", (19, D))
    idbf_d = din("idbf", (128, 128), BF16)
    usbf_d = din("usbf", (128, 128), BF16)
    onescol_d = din("onescol", (128, 1), BF16)
    onesrow_d = din("onesrow", (65, 128), BF16)
    onesrow_f_d = din("onesrow_f", (1, 128))
    iota1_d = din("iota1", (1, 128))
    mubd_d = din("mubd", (128, 128), BF16)
    mlbd_d = din("mlbd", (128, 128), BF16)
    mlbs_d = din("mlbs", (128, 128), BF16)
    mus_d = din("mus_f", (128, 128), BF16)
    if32_d = din("if32", (128, 128))
    one11_d = din("one11_f", (1, 1))

    es = contextlib.ExitStack()
    sb = {}

    def S(name, shape, dt):
        sb[name] = es.enter_context(nc.sbuf_tensor(name, list(shape), dt))
        return sb[name]

    with es:
        hzv0 = S("hzv0", (128, 1), F32)
        hzv1 = S("hzv1", (128, 1), F32)
        hzs0 = S("hzs0", (128, 1), F32)
        hzs1 = S("hzs1", (128, 1), F32)
        p.pad_fn["vector"] = lambda e: e.tensor_copy(hzv1[:, :], hzv0[:, :])
        p.pad_fn["scalar"] = lambda e: e.copy(hzs1[:, :], hzs0[:, :])
        res = [S(f"res{t}", (128, D), F32) for t in range(NT)]
        wA = [S(f"wA{i}", (128, NC_CH * D), BF16) for i in range(4)]
        wB = [S(f"wB{i}", (128, 4 * D), BF16) for i in range(3)]
        wkq_sb = S("wkq_sb", (128, NC_CH * 128), BF16)
        rt_sb = S("rt_sb", (128, NC_CH * NE), BF16)
        idbf = S("idbf_s", (128, 128), BF16)
        usbf = S("usbf_s", (128, 128), BF16)
        onescol = S("onescol_s", (128, 1), BF16)
        onesrow = S("onesrow_s", (65, 128), BF16)
        onesrow_f = S("onesrow_f_s", (1, 128), F32)
        iota1 = S("iota1_s", (1, 128), F32)
        mubd = S("mubd_s", (128, 128), BF16)
        mlbd = S("mlbd_s", (128, 128), BF16)
        mlbs = S("mlbs_s", (128, 128), BF16)
        mus = S("mus_s", (128, 128), BF16)
        if32 = S("if32_s", (128, 128), F32)
        one11 = S("one11_s", (1, 1), F32)
        tf_rep = S("tf_rep", (128, D), BF16)
        td128_rep = S("td128_rep", (128, D), F32)
        cols = S("cols", (128, 48), F32)
        p2t = S("p2t", (128, D), F32)
        carry_f = S("carry_f", (1, D), F32)
        rows_bf = S("rows_bf", (65, D), BF16)
        carry_bf = rows_bf[64:65, :]
        negbv = rows_bf[32:33, :]
        bkq_r = S("bkq_r", (1, 128), BF16)
        rb_r = S("rb_r", (1, NE), BF16)
        bnbuf = S("bnbuf", (128, 12), F32)
        bnbuf2 = S("bnbuf2", (128, 6), F32)
        bnagg = [S("bnagg0", (128, 2), F32), S("bnagg1", (128, 2), F32)]
        stmu = S("stmu", (128, NT), F32)
        stvar = S("stvar", (128, NT), F32)
        strs = S("strs", (128, NT), F32)
        strr = S("strr", (128, NT), F32)
        stt1 = S("stt1", (128, NT), F32)
        stt2 = S("stt2", (128, NT), F32)
        stnm1 = S("stnm1", (128, NT), F32)
        stnm2 = S("stnm2", (128, NT), F32)
        sf = [S(f"sf{i}", (128, D), F32) for i in range(2)]
        sbf = [S(f"sbf{i}", (128, D), BF16) for i in range(6)]
        w0t_f = S("w0t_f", (64, D), F32)
        lntd = p2t[0:1, :]
        w0t_bf = S("w0t_bf", (64, D), BF16)
        pt_bfp = [S(f"pt_bf{i}", (128, 128), BF16) for i in range(2)]
        qt_bfp = [S(f"qt_bf{i}", (128, 128), BF16) for i in range(2)]
        gt_bfp = [S(f"gt_bf{i}", (128, 128), BF16) for i in range(2)]
        k_bfp = [S(f"k_bf{i}", (64, 128), BF16) for i in range(2)]
        q_bfp = [S(f"q_bf{i}", (64, 128), BF16) for i in range(2)]
        ktm_bf = S("ktm_bf", (128, 64), BF16)
        cX = S("cX", (128, 128), BF16)
        cY = S("cY", (128, 128), BF16)
        cIY = S("cIY", (128, 128), BF16)
        cRa = S("cRa", (128, 128), BF16)
        cRb = S("cRb", (128, 128), BF16)
        cSb = S("cSb", (128, 128), BF16)
        lg = S("lg", (128, NE), F32)
        rmx = S("rmx", (128, 1), F32)
        sme = S("sme", (128, 1), F32)
        wnv = S("wnv", (128, 1), F32)
        nty = S("nty", (128, 1), F32)
        eqc = S("eqc", (128, 1), F32)
        mcc = S("mcc", (128, 1), F32)
        mw_f = S("mw_f", (128, NT * NE), F32)
        mwt_bf = S("mwt_bf", (NE, 128), BF16)

        pT = es.enter_context(nc.psum_tensor("pT", [128, 1024], BF16))
        pA0 = es.enter_context(nc.psum_tensor("pA0", [128, 512], F32))
        pA1 = es.enter_context(nc.psum_tensor("pA1", [128, 512], F32))
        pB0 = es.enter_context(nc.psum_tensor("pB0", [128, 512], F32))
        pB1 = es.enter_context(nc.psum_tensor("pB1", [128, 512], F32))
        pC0 = es.enter_context(nc.psum_tensor("pC0", [128, 512], F32))
        pC1 = es.enter_context(nc.psum_tensor("pC1", [128, 512], F32))
        pT2 = es.enter_context(nc.psum_tensor("pT2", [128, 1024], BF16))

        def E(eng, fn, reads=(), writes=(), tag=""):
            return p.emit(eng, fn, reads=reads, writes=writes, tag=tag)

        def dma(q, out_ap, in_ap, reads=(), writes=(), tag=""):
            return E(q, lambda e, o=out_ap, i=in_ap: e.dma_start(out=o, in_=i),
                     reads=reads, writes=writes, tag=tag)

        def mm(out_ap, lhsT_ap, rhs_ap, start, stop, reads, writes, tag=""):
            return E("tensor",
                     lambda e, o=out_ap, l=lhsT_ap, r=rhs_ap, s1=start, s2=stop:
                     e.matmul(o, l, r, start=s1, stop=s2),
                     reads=reads, writes=writes, tag=tag)

        def tr(out_ap, in_ap, id_ap, reads, writes, tag=""):
            return E("tensor",
                     lambda e, o=out_ap, i=in_ap, d=id_ap: e.transpose(o, i, d),
                     reads=reads, writes=writes, tag=tag)

        def vop(eng, meth, *aps, reads=(), writes=(), tag="", **kw):
            def fn(e, m=meth, a=aps, k=dict(kw)):
                return getattr(e, m)(*a, **k)
            return E(eng, fn, reads=reads, writes=writes, tag=tag)

        dma("syncq", idbf[:, :], idbf_d[:, :], writes=["idbf_s"])
        dma("syncq", usbf[:, :], usbf_d[:, :], writes=["usbf_s"])
        dma("syncq", onescol[:, :], onescol_d[:, :], writes=["onescol_s"])
        dma("syncq", onesrow[:, :], onesrow_d[:, :], writes=["onesrow_s"])
        dma("syncq", onesrow_f[:, :], onesrow_f_d[:, :], writes=["onesrow_f_s"])
        dma("syncq", iota1[:, :], iota1_d[:, :], writes=["iota1_s"])
        dma("syncq", mubd[:, :], mubd_d[:, :], writes=["mubd_s"])
        dma("syncq", mlbd[:, :], mlbd_d[:, :], writes=["mlbd_s"])
        dma("syncq", mlbs[:, :], mlbs_d[:, :], writes=["mlbs_s"])
        dma("syncq", mus[:, :], mus_d[:, :], writes=["mus_s"])
        dma("syncq", if32[:, :], if32_d[:, :], writes=["if32_s"])
        dma("syncq", one11[:, :], one11_d[:, :], writes=["one11_s"])
        def load_w1024(dst_name, dst, src_ap, tag, extra=()):
            src3 = src_ap.rearrange("(c p) d -> p c d", p=128)
            dst3 = dst[:, :].rearrange("p (c d) -> p c d", d=D)
            for h in range(2):
                dma("gpq", dst3[:, h * 4:(h + 1) * 4, :],
                    src3[:, h * 4:(h + 1) * 4, :],
                    writes=[dst_name, *extra], tag=tag)

        load_w1024("wA0", wA[0], wk0_d, "wk0")
        load_w1024("wA1", wA[1], wv0_d, "wv0")
        load_w1024("wA2", wA[2], wr0_d, "wr0")
        load_w1024("wA3", wA[3], wo0_d, "wo0")
        dma("gpq", rt_sb[:, :].rearrange("p (c e) -> p c e", e=NE),
            rt0_d.rearrange("(c p) e -> p c e", p=128), writes=["rt_sb"])
        dma("gpq", rows_bf[0:8, :], vecs_d[5:13, :], writes=["rows_bf"])
        dma("gpq", wkq_sb[:, :].rearrange("p (c s) -> p c s", s=128),
            wkq1_d.rearrange("(c p) s -> p c s", p=128), writes=["wkq_sb"])
        dma("gpq", bkq_r[:, :], vecs_d[13:14, 0:128], writes=["bkq_r"])
        dma("gpq", rb_r[:, :], vecs_d[4:5, 0:NE], writes=["rb_r"])

        srow = sf[0][0:1, :]
        srow2 = sf[1][0:1, :]
        dma("syncq", srow, vecs_d[2:3, :], writes=["sf0"])
        vop("scalar", "activation", srow, srow, AF.Sigmoid,
            reads=["sf0"], writes=["sf0"], tag="td")
        vop("scalar", "activation", lntd[:, :], srow, AF.Ln,
            reads=["sf0"], writes=["p2t"], tag="lntd")
        dma("syncq", lntd_d[:, :], lntd[:, :], reads=["p2t"], tag="lntd_sv")
        for _ in range(7):
            vop("vector", "tensor_mul", srow, srow, srow,
                reads=["sf0"], writes=["sf0"], tag="tdsq")
        for h in range(2):
            pp = (pA0, pA1)[h]
            mm(pp[:, :], onesrow_f[:, :], srow[:, h * 512:(h + 1) * 512], True, True,
               reads=["onesrow_f_s", "sf0"], writes=[("pA0", "pA1")[h]], tag="tdrep")
            vop("vector", "tensor_copy", td128_rep[:, h * 512:(h + 1) * 512], pp[:, :],
                reads=[("pA0", "pA1")[h]], writes=["td128_rep"])
        dma("syncq", srow, vecs_d[3:4, :], writes=["sf0"])
        vop("scalar", "activation", srow, srow, AF.Sigmoid,
            reads=["sf0"], writes=["sf0"], tag="tf")
        for h in range(2):
            pp = (pA0, pA1)[h]
            mm(pp[:, :], onesrow_f[:, :], srow[:, h * 512:(h + 1) * 512], True, True,
               reads=["onesrow_f_s", "sf0"], writes=[("pA0", "pA1")[h]], tag="tfrep")
            vop("vector", "tensor_copy", tf_rep[:, h * 512:(h + 1) * 512], pp[:, :],
                reads=[("pA0", "pA1")[h]], writes=["tf_rep"])
        for i, row in enumerate((0, 1, 15, 16, 17, 18)):
            dma("syncq", srow, vecs_d[row:row + 1, :], writes=["sf0"], tag=f"vrow{row}")
            for c in range(NC_CH):
                mm(pA0[:, i * 8 + c:i * 8 + c + 1],
                   srow[:, c * 128:(c + 1) * 128], one11[:, :], True, True,
                   reads=["sf0", "one11_s"], writes=["pA0"], tag="colify")
        vop("vector", "tensor_copy", cols[:, :], pA0[:, 0:48],
            reads=["pA0"], writes=["cols"], tag="cols_evac")
        g0c, b0c = cols[:, 0:8], cols[:, 8:16]
        g1c, b1c = cols[:, 16:24], cols[:, 24:32]
        murc, mukc = cols[:, 32:40], cols[:, 40:48]
        dma("syncq", srow, vecs_d[14:15, :], writes=["sf0"])
        vop("scalar", "mul", negbv, srow, -1.0, reads=["sf0"],
            writes=["rows_bf"], tag="rows_bf")

        for t in range(NT):
            dma("syncq", res[t][:, :], x_d[t * 128:(t + 1) * 128, :],
                writes=[f"res{t}"], tag=f"x0_{t}")


        stats_done = {}

        stats_pending = {}
        stats_cnt = [0]

        def stats_flush(tag):
            if tag in stats_pending:
                t, par = stats_pending.pop(tag)
                ag, agn = bnagg[par], f"bnagg{par}"
                vop("vector", "tensor_scalar", stmu[:, t:t + 1], ag[:, 0:1],
                    1.0, None, ALU.mult, reads=[agn], writes=["stmu"],
                    tag=f"{tag}bnm{t}")
                vop("vector", "tensor_scalar", stvar[:, t:t + 1], ag[:, 1:2],
                    1.0, None, ALU.mult, reads=[agn], writes=["stvar"],
                    tag=f"{tag}bnv{t}")

        def stats_tile(tag, t):
            stats_done.setdefault(tag, set()).add(t)
            par = stats_cnt[0] % 2
            stats_cnt[0] += 1
            vop("vector", "bn_stats", bnbuf[:, 0:6], res[t][:, 0:512],
                reads=[f"res{t}"], writes=["bnbuf"], tag=f"{tag}bn{t}")
            vop("vector", "bn_stats", bnbuf2[:, 0:6], res[t][:, 512:1024],
                reads=[f"res{t}"], writes=["bnbuf2"], tag=f"{tag}bn{t}")
            vop("vector", "tensor_copy", bnbuf[:, 6:12], bnbuf2[:, 0:6],
                reads=["bnbuf2"], writes=["bnbuf"], tag=f"{tag}bnc{t}")
            vop("vector", "bn_aggr", bnagg[par][:, :], bnbuf[:, 0:12],
                reads=["bnbuf"], writes=[f"bnagg{par}"], tag=f"{tag}bna{t}")
            stats_flush(tag)
            stats_pending[tag] = (t, par)

        def stats(tag, inner):
            for t in range(NT):
                if t not in stats_done.get(tag, set()):
                    stats_tile(tag, t)
            p._pad("vector")
            p._pad("vector")
            p._pad("vector")
            stats_flush(tag)
            stats_done.pop(tag, None)
            vop("vector", "tensor_scalar", stt1[:, :], stvar[:, :], EPS, None,
                ALU.add, reads=["stvar"], writes=["stt1"])
            vop("scalar", "activation", stt2[:, :], stt1[:, :], AF.Ln,
                reads=["stt1"], writes=["stt2"], tag=f"{tag}ln")
            vop("scalar", "activation", strs[:, :], stt2[:, :], AF.Exp,
                scale=-0.5, reads=["stt2"], writes=["strs"], tag=f"{tag}rs")
            if inner:
                vop("scalar", "activation", stt1[:, :], stt2[:, :], AF.Exp,
                    scale=-1.0, reads=["stt2"], writes=["stt1"], tag=f"{tag}ia")
                vop("vector", "tensor_mul", stt1[:, :], stvar[:, :], stt1[:, :],
                    reads=["stvar", "stt1"], writes=["stt1"])
                vop("vector", "tensor_scalar", stt1[:, :], stt1[:, :], EPS, None,
                    ALU.add, reads=["stt1"], writes=["stt1"])
                vop("scalar", "activation", stt2[:, :], stt1[:, :], AF.Ln,
                    reads=["stt1"], writes=["stt2"], tag=f"{tag}ln2")
                vop("scalar", "activation", stt1[:, :], stt2[:, :], AF.Exp,
                    scale=-0.5, reads=["stt2"], writes=["stt1"], tag=f"{tag}rs2")
                vop("vector", "tensor_mul", strr[:, :], strs[:, :], stt1[:, :],
                    reads=["strs", "stt1"], writes=["strr"])
            vop("vector", "tensor_mul", stnm1[:, :], stmu[:, :], strs[:, :],
                reads=["stmu", "strs"], writes=["stnm1"])
            vop("vector", "tensor_scalar", stnm1[:, :], stnm1[:, :], -1.0, None,
                ALU.mult, reads=["stnm1"], writes=["stnm1"])
            if inner:
                vop("vector", "tensor_mul", stnm2[:, :], stmu[:, :], strr[:, :],
                    reads=["stmu", "strr"], writes=["stnm2"])
                vop("vector", "tensor_scalar", stnm2[:, :], stnm2[:, :], -1.0, None,
                    ALU.mult, reads=["stnm2"], writes=["stnm2"])

        def transpose_tile(src_bf, src_name, dst_bf, dst_name, gcol=None, bcol=None,
                           dst_base=0, tag=""):
            for c in range(NC_CH):
                tr(pT[:, c * 128:(c + 1) * 128], src_bf[:, c * 128:(c + 1) * 128],
                   idbf[:, :], reads=[src_name, "idbf_s"], writes=["pT"], tag=tag)
            if gcol is None:
                vop("vector", "tensor_copy", dst_bf[:, dst_base:dst_base + 1024],
                    pT[:, :], reads=["pT"], writes=[dst_name], tag=tag)
            else:
                for c in range(NC_CH):
                    vop("vector", "tensor_scalar",
                        dst_bf[:, dst_base + c * 128:dst_base + (c + 1) * 128],
                        pT[:, c * 128:(c + 1) * 128],
                        gcol[:, c:c + 1], bcol[:, c:c + 1], ALU.mult, ALU.add,
                        reads=["pT", "cols"], writes=[dst_name], tag=tag)

        def proj1024(fm_bf, fm_name, w_sb, w_name, pair, pair_names, extra=(),
                     tag=""):
            for h in range(2):
                for c in range(NC_CH):
                    mm(pair[h][:, :], fm_bf[:, c * 128:(c + 1) * 128],
                       w_sb[:, c * D + h * 512:c * D + (h + 1) * 512],
                       start=(c == 0), stop=(c == NC_CH - 1 and not extra),
                       reads=[fm_name, w_name], writes=[pair_names[h]], tag=tag)
                for j, (lh, rh, rnames) in enumerate(extra):
                    mm(pair[h][:, :], lh, rh[h], start=False,
                       stop=(j == len(extra) - 1),
                       reads=list(rnames), writes=[pair_names[h]], tag=tag + "x")

        def phase_A(s):
            def tap(name, t, ap, reads):
                if DEBUG_DUMP == name and s == 0:
                    dma("gpq", y_d[t * 128:(t + 1) * 128, :], ap, reads=reads,
                        tag=f"tap{name}{t}")
            stats("A", inner=True)
            if DEBUG_DUMP == "stats" and s == 0:
                dma("gpq", y_d[0:128, 0:NT], stmu[:, :], reads=["stmu"], tag="tapst0")
                dma("gpq", y_d[128:256, 0:NT], stvar[:, :], reads=["stvar"], tag="tapst1")
                dma("gpq", y_d[256:384, 0:NT], strs[:, :], reads=["strs"], tag="tapst2")
                dma("gpq", y_d[384:512, 0:NT], strr[:, :], reads=["strr"], tag="tapst3")
            vop("vector", "memset", carry_f[:, :], 0.0, writes=["carry_f"], tag="Ainit")
            vop("vector", "memset", carry_bf, 0.0, writes=["rows_bf"], tag="Ainit")
            for h in range(2):
                pp, ppn = (pA0, pA1)[h], ("pA0", "pA1")[h]
                mm(pp[:, :], iota1[:, :], lntd[:, h * 512:(h + 1) * 512],
                   True, True, reads=["iota1_s", "p2t"], writes=[ppn],
                   tag=f"A{s}p2o")
                vop("scalar", "activation", p2t[:, h * 512:(h + 1) * 512],
                    pp[:, :], AF.Exp, reads=[ppn], writes=["p2t"], tag=f"A{s}p2e")
            def stageA1(t):
                rn = f"res{t}"
                par = t % 2
                hfm, hfm_n = (sbf[1], "sbf1") if par == 0 else (sbf[4], "sbf4")
                vop("scalar", "activation", sbf[0][:, :], res[t][:, :],
                    AF.Identity, bias=stnm2[:, t:t + 1], scale=strr[:, t:t + 1],
                    reads=[rn, "strr", "stnm2"], writes=["sbf0"], tag=f"A{t}h")
                for c in range(NC_CH):
                    tr(pT[:, c * 128:(c + 1) * 128], sbf[0][:, c * 128:(c + 1) * 128],
                       idbf[:, :], reads=["sbf0", "idbf_s"], writes=["pT"],
                       tag=f"A{t}tr")
                for c in range(NC_CH):
                    vop("scalar", "activation", hfm[:, c * 128:(c + 1) * 128],
                        pT[:, c * 128:(c + 1) * 128], AF.Identity,
                        bias=b0c[:, c:c + 1], scale=g0c[:, c:c + 1],
                        reads=["pT", "cols"], writes=[hfm_n], tag=f"A{t}ge")

            def stageA2a(t):
                rn = f"res{t}"
                par = t % 2
                hfm, hfm_n = (sbf[1], "sbf1") if par == 0 else (sbf[4], "sbf4")
                gbuf, gn = (sbf[3], "sbf3") if par == 0 else (sbf[5], "sbf5")
                proj1024(hfm, hfm_n, wA[0], "wA0", (pA0, pA1), ("pA0", "pA1"),
                         tag=f"A{t}k")
                for h in range(2):
                    vop("scalar", "copy", sf[0][:, h * 512:(h + 1) * 512],
                        (pA0, pA1)[h][:, :], reads=[("pA0", "pA1")[h]],
                        writes=["sf0"], tag=f"A{t}ke")
                proj1024(hfm, hfm_n, wA[1], "wA1", (pB0, pB1), ("pB0", "pB1"),
                         tag=f"A{t}v")
                for h in range(2):
                    vop("vector", "tensor_mul", sf[1][:, h * 512:(h + 1) * 512],
                        sf[0][:, h * 512:(h + 1) * 512], (pB0, pB1)[h][:, :],
                        reads=["sf0", ("pB0", "pB1")[h]], writes=["sf1"],
                        tag=f"A{t}kv")
                proj1024(hfm, hfm_n, wA[2], "wA2", (pA0, pA1), ("pA0", "pA1"),
                         tag=f"A{t}r")
                for h in range(2):
                    vop("scalar", "activation", sf[0][:, h * 512:(h + 1) * 512],
                        (pA0, pA1)[h][:, :], AF.Sigmoid,
                        reads=[("pA0", "pA1")[h]], writes=["sf0"], tag=f"A{t}sr")
                vop("vector", "tensor_mul", sbf[2][:, :], sf[1][:, :], p2t[:, :],
                    reads=["sf1", "p2t"], writes=["sbf2"], tag=f"A{t}w2")
                vop("vector", "tensor_mul", sf[1][:, :], sf[1][:, :], tf_rep[:, :],
                    reads=["sf1", "tf_rep"], writes=["sf1"], tag=f"A{t}t3")
                for h in range(2):
                    ph, phn = (pC0, pC1)[h], ("pC0", "pC1")[h]
                    mm(ph[:, :], usbf[:, :], sbf[2][:, h * 512:(h + 1) * 512],
                       True, False, reads=["usbf_s", "sbf2"], writes=[phn],
                       tag=f"A{t}cs")
                    mm(ph[:, :], onesrow[64:65, :],
                       carry_bf[:, h * 512:(h + 1) * 512],
                       False, True, reads=["onesrow_s", "rows_bf"], writes=[phn],
                       tag=f"A{t}cs")
                for h in range(2):
                    mm(pB0[0:1, :], onescol[:, :], sbf[2][:, h * 512:(h + 1) * 512],
                       True, True, reads=["onescol_s", "sbf2"], writes=["pB0"],
                       tag=f"A{t}col")
                    vop("vector", "tensor_add", carry_f[:, h * 512:(h + 1) * 512],
                        carry_f[:, h * 512:(h + 1) * 512], pB0[0:1, :],
                        reads=["carry_f", "pB0"], writes=["carry_f"], tag=f"A{t}col")
                vop("scalar", "copy", carry_bf, carry_f[:, :],
                    reads=["carry_f"], writes=["rows_bf"], tag=f"A{t}cbf")
                for h in range(2):
                    vop("vector", "tensor_add", sf[1][:, h * 512:(h + 1) * 512],
                        sf[1][:, h * 512:(h + 1) * 512], (pC0, pC1)[h][:, :],
                        reads=["sf1", ("pC0", "pC1")[h]], writes=["sf1"],
                        tag=f"A{t}wkv")
                vop("vector", "tensor_mul", gbuf[:, :], sf[1][:, :], sf[0][:, :],
                    reads=["sf1", "sf0"], writes=[gn], tag=f"A{t}g")
                vop("vector", "tensor_mul", p2t[:, :], p2t[:, :], td128_rep[:, :],
                    reads=["p2t", "td128_rep"], writes=["p2t"], tag=f"A{t}p2")

            def stageA2b(t):
                rn = f"res{t}"
                par = t % 2
                gbuf, gn = (sbf[3], "sbf3") if par == 0 else (sbf[5], "sbf5")
                for c in range(NC_CH):
                    tr(pT2[:, c * 128:(c + 1) * 128],
                       gbuf[:, c * 128:(c + 1) * 128], idbf[:, :],
                       reads=[gn, "idbf_s"], writes=["pT2"], tag=f"A{t}gtr")
                vop("scalar", "copy", sbf[0][:, :], pT2[:, :],
                    reads=["pT2"], writes=["sbf0"], tag=f"A{t}gfe")
                proj1024(sbf[0], "sbf0", wA[3], "wA3", (pA0, pA1), ("pA0", "pA1"),
                         tag=f"A{t}o")
                for h in range(2):
                    vop("vector", "tensor_add", res[t][:, h * 512:(h + 1) * 512],
                        res[t][:, h * 512:(h + 1) * 512], (pA0, pA1)[h][:, :],
                        reads=[rn, ("pA0", "pA1")[h]], writes=[rn], tag=f"A{t}res")
                stats_tile("B", t)

            stageA1(0)
            stageA1(1)
            stageA2a(0)
            for t in range(NT):
                if t + 2 < NT:
                    stageA1(t + 2)
                if t + 1 < NT:
                    stageA2a(t + 1)
                stageA2b(t)

        def phase_B(s):
            def tapB(name, t, ap, reads):
                if DEBUG_DUMP == name and s == 0:
                    dma("gpq", y_d[t * 128:(t + 1) * 128, 0:8], ap, reads=reads,
                        tag=f"tapB{name}{t}")
            def bslot(slot):
                if slot < 3:
                    return wB[slot][:, 0:4 * D], f"wB{slot}"
                return wA[3][:, (slot - 3) * 4 * D:(slot - 2) * 4 * D], "wA3"

            def load_half(H):
                e, j = H // 2, H % 2
                ap, dn = bslot(H % 5)
                src3 = et0_d[e * D:(e + 1) * D, :].rearrange("(c p) d -> p c d",
                                                             p=128)
                dma("gpq", ap.rearrange("p (c d) -> p c d", d=D),
                    src3[:, j * 4:(j + 1) * 4, :],
                    writes=[dn], tag=f"B_e{e}h{j}")

            for H in range(5):
                load_half(H)
            if DEBUG_DUMP == "wb0" and s == 0:
                vop("vector", "tensor_copy", sf[1][:, :], wB[0][:, 0:1024],
                    reads=["wB0"], writes=["sf1"], tag="tapwb0")
                dma("syncq", y_d[0:128, :], sf[1][:, :],
                    reads=["sf1"], tag="tapwb0d")
            stats("B", inner=False)

            def stageB1(t):
                rn = f"res{t}"
                vop("scalar", "activation", sbf[0][:, :], res[t][:, :],
                    AF.Identity, bias=stnm1[:, t:t + 1], scale=strs[:, t:t + 1],
                    reads=[rn, "strs", "stnm1"], writes=["sbf0"], tag=f"B{t}xo")
                dst, dname = (wA[0], "wA0") if t < 8 else (wA[1], "wA1")
                for c in range(NC_CH):
                    tr(pT[:, c * 128:(c + 1) * 128],
                       sbf[0][:, c * 128:(c + 1) * 128], idbf[:, :],
                       reads=["sbf0", "idbf_s"], writes=["pT"], tag=f"B{t}tr")
                base = (t % 8) * 1024
                vop("scalar", "copy", dst[:, base:base + 1024], pT[:, :],
                    reads=["pT"], writes=[dname, f"fm{t}"], tag=f"B{t}fme")

            def stageB2(t):
                rn = f"res{t}"
                dst, dname = (wA[0], "wA0") if t < 8 else (wA[1], "wA1")
                for c in range(NC_CH):
                    mm(pA0[:, 0:NE], dst[:, (t % 8) * 1024 + c * 128:
                                         (t % 8) * 1024 + (c + 1) * 128],
                       rt_sb[:, c * NE:(c + 1) * NE], start=(c == 0), stop=False,
                       reads=[f"fm{t}", "rt_sb"], writes=["pA0"], tag=f"B{t}rt")
                mm(pA0[:, 0:NE], onesrow[0:1, :], rb_r[:, :], False, True,
                   reads=["onesrow_s", "rb_r"], writes=["pA0"], tag=f"B{t}rtb")
                vop("vector", "tensor_copy", lg[:, :], pA0[:, 0:NE],
                    reads=["pA0"], writes=["lg"], tag=f"B{t}lg")
                tapB("lgits", t, lg[:, :], ["lg"])
                vop("vector", "reduce_max", rmx[:, :], lg[:, :], AX.X,
                    reads=["lg"], writes=["rmx"])
                vop("vector", "tensor_scalar", lg[:, :], lg[:, :], rmx[:, :], None,
                    ALU.subtract, reads=["lg", "rmx"], writes=["lg"])
                vop("scalar", "activation", lg[:, :], lg[:, :], AF.Exp,
                    reads=["lg"], writes=["lg"], tag=f"B{t}exp")
                vop("vector", "reduce_sum", sme[:, :], lg[:, :], AX.X,
                    reads=["lg"], writes=["sme"])
                vop("scalar", "activation", wnv[:, :], sme[:, :], AF.Ln,
                    reads=["sme"], writes=["wnv"], tag=f"B{t}lnw")
                vop("scalar", "activation", wnv[:, :], wnv[:, :], AF.Exp,
                    scale=-1.0, reads=["wnv"], writes=["wnv"], tag=f"B{t}invw")
                vop("vector", "tensor_scalar", lg[:, :], lg[:, :], 1.0,
                    None, ALU.is_ge, reads=["lg"], writes=["lg"], tag=f"B{t}eq")
                vop("vector", "tensor_scalar", mw_f[:, t * NE:(t + 1) * NE],
                    lg[:, :], wnv[:, :], None, ALU.mult,
                    reads=["lg", "wnv"], writes=["mw_f"], tag=f"B{t}mw")
            for t in range(NT):
                tapB("mw", t, mw_f[:, t * NE:(t + 1) * NE], ["mw_f"])
            def stageB3(t):
                rn = f"res{t}"
                tr(pA1[0:NE, 0:128], mw_f[:, t * NE:(t + 1) * NE], if32[:, :],
                   reads=["mw_f", "if32_s"], writes=["pA1"], tag=f"B{t}mwt")
                vop("vector", "tensor_copy", mwt_bf[:, :], pA1[0:NE, 0:128],
                    reads=["pA1"], writes=["mwt_bf"], tag=f"B{t}mwt")
                for h in range(2):
                    ph, phn = (pC0, pC1)[h], ("pC0", "pC1")[h]
                    mm(ph[:, :], mwt_bf[:, :], rows_bf[0:8, h * 512:(h + 1) * 512],
                       True, True, reads=["mwt_bf", "rows_bf"], writes=[phn],
                       tag=f"B{t}bias")
                    vop("vector", "tensor_add", res[t][:, h * 512:(h + 1) * 512],
                        res[t][:, h * 512:(h + 1) * 512], ph[:, :],
                        reads=[rn, phn], writes=[rn], tag=f"B{t}biasadd")

            pairs = ((pA0, pA1, "pA0", "pA1"), (pB0, pB1, "pB0", "pB1"),
                     (pC0, pC1, "pC0", "pC1"))

            def expertTile(e, t):
                rn = f"res{t}"
                fm = wA[0] if t < 8 else wA[1]
                fmn = f"fm{t}"
                base = (t % 8) * 1024
                q0, q1, qn0, qn1 = pairs[(e * NT + t) % 3]
                for h in range(2):
                    ph, phn = (q0, q1)[h], (qn0, qn1)[h]
                    for c in range(NC_CH):
                        wap, wn = bslot((2 * e + c // 4) % 5)
                        mm(ph[:, :], fm[:, base + c * 128:base + (c + 1) * 128],
                           wap[:, (c % 4) * D + h * 512:
                               (c % 4) * D + (h + 1) * 512],
                           start=(c == 0), stop=(c == NC_CH - 1),
                           reads=[fmn, wn], writes=[phn],
                           tag=f"B{t}e{e}")
                    vop("vector", "scalar_tensor_tensor",
                        res[t][:, h * 512:(h + 1) * 512], ph[:, :],
                        mw_f[:, t * NE + e:t * NE + e + 1],
                        res[t][:, h * 512:(h + 1) * 512], ALU.mult, ALU.add,
                        reads=[phn, "mw_f", rn], writes=[rn], tag=f"B{t}e{e}acc")
                if e == NE - 1:
                    stats_tile("C", t)
                if t == NT - 1:
                    for H in (2 * e + 5, 2 * e + 6):
                        if H < 2 * NE:
                            load_half(H)

            dobias = DEBUG_DUMP not in ("nobias", "e0")
            doexp = DEBUG_DUMP != "noexp"
            stageB1(0)
            for t in range(NT):
                if t + 1 < NT:
                    stageB1(t + 1)
                stageB2(t)
                if doexp:
                    expertTile(0, t)
                if dobias:
                    stageB3(t)
            exp_rng = range(1, NE)
            if DEBUG_DUMP == "noexp":
                exp_rng = range(0)
            elif DEBUG_DUMP == "e0":
                exp_rng = range(0)
            for e in exp_rng:
                if e == NE - 1:
                    dma("gpq", wB[1][:, :].rearrange("p (c d) -> p c d", d=D),
                        wv1_d.rearrange("(c p) d -> p c d", p=128)[:, 0:4, :],
                        writes=["wB1"], tag="C_wv0pre")
                    vop("scalar", "mul", wB[1][:, :], wB[1][:, :], -1.0,
                        reads=["wB1"], writes=["wB1"], tag="C_negwv0")
                for t in range(NT):
                    expertTile(e, t)

        def phase_C(s):
            dma("gpq", wB[0][:, :].rearrange("p (c d) -> p c d", d=D),
                wv1_d.rearrange("(c p) d -> p c d", p=128)[:, 4:8, :],
                writes=["wB0"], tag="C_wv1")
            vop("scalar", "mul", wB[0][:, :], wB[0][:, :], -1.0,
                reads=["wB0"], writes=["wB0"], tag="C_negwv1")
            stats("C", inner=True)
            vop("vector", "memset", w0t_f[0:64, :], 0.0, writes=["w0t_f"],
                tag="Cinit")
            vop("vector", "memset", w0t_bf[:, :], 0.0, writes=["w0t_bf"], tag="Cinit")

            def stage1(t):
                rn = f"res{t}"
                par = t % 2
                k_bf, k_n = k_bfp[par], f"k_bf{par}"
                q_bf, q_n = q_bfp[par], f"q_bf{par}"
                pt_bf, pt_n = pt_bfp[par], f"pt_bf{par}"
                qt_bf, qt_n = qt_bfp[par], f"qt_bf{par}"
                gt_bf, gt_n = gt_bfp[par], f"gt_bf{par}"
                hfm, hfm_n = (sbf[1], "sbf1") if par == 0 else (sbf[4], "sbf4")
                vop("scalar", "activation", sf[1][:, :], res[t][:, :], AF.Identity,
                    bias=stnm1[:, t:t + 1], scale=strs[:, t:t + 1],
                    reads=[rn, "strs", "stnm1"], writes=["sf1"], tag=f"C{t}xo")
                vop("scalar", "activation", sbf[0][:, :], res[t][:, :], AF.Identity,
                    bias=stnm2[:, t:t + 1], scale=strr[:, t:t + 1],
                    reads=[rn, "strr", "stnm2"], writes=["sbf0"], tag=f"C{t}h")
                vop("vector", "tensor_add", res[t][:, :], res[t][:, :], sf[1][:, :],
                    reads=[rn, "sf1"], writes=[rn], tag=f"C{t}rx")
                for c in range(NC_CH):
                    tr(pT[:, c * 128:(c + 1) * 128], sbf[0][:, c * 128:(c + 1) * 128],
                       idbf[:, :], reads=["sbf0", "idbf_s"], writes=["pT"],
                       tag=f"C{t}tr")
                for c in range(NC_CH):
                    vop("scalar", "activation", hfm[:, c * 128:(c + 1) * 128],
                        pT[:, c * 128:(c + 1) * 128], AF.Identity,
                        bias=b1c[:, c:c + 1], scale=g1c[:, c:c + 1],
                        reads=["pT", "cols"], writes=[hfm_n], tag=f"C{t}ge")
                for j, (pq, pqn, dst, dn) in enumerate(
                        ((pA0, "pA0", k_bf, k_n), (pA1, "pA1", q_bf, q_n))):
                    for c in range(NC_CH):
                        mm(pq[0:64, 0:128],
                           wkq_sb[:, c * 128 + j * 64:c * 128 + j * 64 + 64],
                           hfm[:, c * 128:(c + 1) * 128],
                           start=(c == 0), stop=False,
                           reads=["wkq_sb", hfm_n], writes=[pqn], tag=f"C{t}kq")
                    mm(pq[0:64, 0:128], bkq_r[:, j * 64:j * 64 + 64],
                       onesrow[0:1, :], False, True,
                       reads=["bkq_r", "onesrow_s"], writes=[pqn], tag=f"C{t}kqb")
                    vop("scalar", "copy", dst[:, :], pq[0:64, 0:128],
                        reads=[pqn], writes=[dn], tag=f"C{t}kqe")
                mm(pA0[:, 0:128], k_bf[:, :], k_bf[:, :], True, True,
                   reads=[k_n], writes=["pA0"], tag=f"C{t}S")
                vop("vector", "scalar_tensor_tensor", cX[:, :], pA0[:, 0:128], -LR,
                    mubd[:, :], ALU.mult, ALU.mult,
                    reads=["pA0", "mubd_s"], writes=["cX"], tag=f"C{t}X0")
                vop("vector", "scalar_tensor_tensor", cY[:, :], pA0[:, 0:128], -LR,
                    mlbd[:, :], ALU.mult, ALU.mult,
                    reads=["pA0", "mlbd_s"], writes=["cY"], tag=f"C{t}Y0")
                vop("vector", "scalar_tensor_tensor", cSb[:, :], pA0[:, 0:128], -LR,
                    mlbs[:, :], ALU.mult, ALU.mult,
                    reads=["pA0", "mlbs_s"], writes=["cSb"], tag=f"C{t}Sb")
                mm(pA1[:, 0:128], k_bf[:, :], q_bf[:, :], True, True,
                   reads=[k_n, q_n], writes=["pA1"], tag=f"C{t}KQ")
                vop("vector", "scalar_tensor_tensor", gt_bf[:, :], pA1[:, 0:128],
                    -1.0, mus[:, :], ALU.mult, ALU.mult,
                    reads=["pA1", "mus_s"], writes=[gt_n], tag=f"C{t}Gt")
                vop("vector", "tensor_add", cRa[:, :], cX[:, :], if32[:, :],
                    reads=["cX", "if32_s"], writes=["cRa"], tag=f"C{t}P0")

            def tbuild(t):
                par = t % 2
                pt_bf, pt_n = pt_bfp[par], f"pt_bf{par}"
                qt_bf, qt_n = qt_bfp[par], f"qt_bf{par}"
                Rcur, Rnam, Roth, Rothn = cRa, "cRa", cRb, "cRb"
                for k in range(1, 4):
                    mm(pA1[:, 0:128], cX[:, :], cY[:, :], True, True,
                       reads=["cX", "cY"], writes=["pA1"], tag=f"C{t}Y2_{k}")
                    if k < 3:
                        mm(pA0[:, 0:128], cY[:, :], cX[:, :], True, True,
                           reads=["cY", "cX"], writes=["pA0"], tag=f"C{t}X2_{k}")
                    vop("vector", "tensor_add", cIY[:, :], pA1[:, 0:128],
                        if32[:, :], reads=["pA1", "if32_s"], writes=["cIY"],
                        tag=f"C{t}IY{k}")
                    if k < 3:
                        vop("vector", "tensor_copy", cY[:, :], pA1[:, 0:128],
                            reads=["pA1"], writes=["cY"], tag=f"C{t}Ye{k}")
                        vop("scalar", "copy", cX[:, :], pA0[:, 0:128],
                            reads=["pA0"], writes=["cX"], tag=f"C{t}Xe{k}")
                    yield
                    mm(pB0[:, 0:128], cIY[:, :], Rcur[:, :], True, True,
                       reads=["cIY", Rnam], writes=["pB0"], tag=f"C{t}P{k}")
                    vop("vector", "tensor_copy", Roth[:, :], pB0[:, 0:128],
                        reads=["pB0"], writes=[Rothn], tag=f"C{t}Pe{k}")
                    if k == 3:
                        vop("vector", "tensor_copy", pt_bf[:, :], pB0[:, 0:128],
                            reads=["pB0"], writes=[pt_n], tag=f"C{t}Ptb")
                    Rcur, Rnam, Roth, Rothn = Roth, Rothn, Rcur, Rnam
                    yield
                PT3, PT3n, Qs, Qsn = Rcur, Rnam, Roth, Rothn
                mm(pA0[:, 0:128], cSb[:, :], PT3[:, :], True, True,
                   reads=["cSb", PT3n], writes=["pA0"], tag=f"C{t}Mt")
                mm(pA1[:, 0:128], PT3[:, :], cSb[:, :], True, True,
                   reads=[PT3n, "cSb"], writes=["pA1"], tag=f"C{t}M")
                vop("scalar", "copy", cX[:, :], pA0[:, 0:128],
                    reads=["pA0"], writes=["cX"], tag=f"C{t}Mte")
                vop("scalar", "copy", cY[:, :], pA1[:, 0:128],
                    reads=["pA1"], writes=["cY"], tag=f"C{t}Me")
                yield
                vop("vector", "tensor_add", Qs[:, :], cX[:, :], if32[:, :],
                    reads=["cX", "if32_s"], writes=[Qsn], tag=f"C{t}Q0")
                mm(pA0[:, 0:128], cX[:, :], cY[:, :], True, True,
                   reads=["cX", "cY"], writes=["pA0"], tag=f"C{t}M2")
                mm(pA1[:, 0:128], cY[:, :], cX[:, :], True, True,
                   reads=["cY", "cX"], writes=["pA1"], tag=f"C{t}M2t")
                vop("vector", "tensor_add", cIY[:, :], pA0[:, 0:128], if32[:, :],
                    reads=["pA0", "if32_s"], writes=["cIY"], tag=f"C{t}IM2")
                vop("vector", "tensor_copy", cSb[:, :], pA0[:, 0:128],
                    reads=["pA0"], writes=["cSb"], tag=f"C{t}M2e")
                vop("scalar", "copy", cX[:, :], pA1[:, 0:128],
                    reads=["pA1"], writes=["cX"], tag=f"C{t}M2te")
                yield
                mm(pB0[:, 0:128], cIY[:, :], Qs[:, :], True, True,
                   reads=["cIY", Qsn], writes=["pB0"], tag=f"C{t}Q1")
                vop("vector", "tensor_copy", PT3[:, :], pB0[:, 0:128],
                    reads=["pB0"], writes=[PT3n], tag=f"C{t}Q1e")
                yield
                mm(pA0[:, 0:128], cX[:, :], cSb[:, :], True, True,
                   reads=["cX", "cSb"], writes=["pA0"], tag=f"C{t}M4")
                vop("vector", "tensor_add", cIY[:, :], pA0[:, 0:128], if32[:, :],
                    reads=["pA0", "if32_s"], writes=["cIY"], tag=f"C{t}IM4")
                yield
                mm(pB0[:, 0:128], cIY[:, :], PT3[:, :], True, True,
                   reads=["cIY", PT3n], writes=["pB0"], tag=f"C{t}Q2")
                vop("scalar", "copy", qt_bf[:, :], pB0[:, 0:128],
                    reads=["pB0"], writes=[qt_n], tag=f"C{t}Q2e")

            def stage2(t):
                rn = f"res{t}"
                par = t % 2
                k_bf, k_n = k_bfp[par], f"k_bf{par}"
                q_bf, q_n = q_bfp[par], f"q_bf{par}"
                pt_bf, pt_n = pt_bfp[par], f"pt_bf{par}"
                qt_bf, qt_n = qt_bfp[par], f"qt_bf{par}"
                gt_bf, gt_n = gt_bfp[par], f"gt_bf{par}"
                hfm, hfm_n = (sbf[1], "sbf1") if par == 0 else (sbf[4], "sbf4")
                for h in range(2):
                    ph, phn = (pC0, pC1)[h], ("pC0", "pC1")[h]
                    mm(ph[:, :], k_bf[:, :], w0t_bf[:, h * 512:(h + 1) * 512],
                       True, False, reads=[k_n, "w0t_bf"], writes=[phn],
                       tag=f"C{t}BKW")
                    for c in range(NC_CH):
                        slot = 1 - c // 4
                        mm(ph[:, :], hfm[:, c * 128:(c + 1) * 128],
                           wB[slot][:, (c % 4) * D + h * 512:
                                     (c % 4) * D + (h + 1) * 512],
                           start=False, stop=False,
                           reads=[hfm_n, f"wB{slot}"], writes=[phn],
                           tag=f"C{t}BV")
                    mm(ph[:, :], onesrow[32:33, :],
                       negbv[:, h * 512:(h + 1) * 512],
                       False, True, reads=["onesrow_s", "rows_bf"], writes=[phn],
                       tag=f"C{t}Bb")
                    vop("scalar", "mul", sbf[2][:, h * 512:(h + 1) * 512],
                        ph[:, :], LR, reads=[phn], writes=["sbf2"], tag=f"C{t}Bs")
                    yield
                for h in range(2):
                    ph, phn = (pC0, pC1)[h], ("pC0", "pC1")[h]
                    mm(ph[:, :], pt_bf[:, :], sbf[2][:, h * 512:(h + 1) * 512],
                       True, True, reads=[pt_n, "sbf2"], writes=[phn],
                       tag=f"C{t}U1")
                    vop("scalar", "copy", sbf[0][:, h * 512:(h + 1) * 512],
                        ph[:, :], reads=[phn], writes=["sbf0"], tag=f"C{t}U1e")
                yield
                for h in range(2):
                    ph, phn = (pC0, pC1)[h], ("pC0", "pC1")[h]
                    mm(ph[:, :], qt_bf[:, :], sbf[0][:, h * 512:(h + 1) * 512],
                       True, True, reads=[qt_n, "sbf0"], writes=[phn],
                       tag=f"C{t}U")
                    vop("vector", "tensor_copy", sbf[3][:, h * 512:(h + 1) * 512],
                        ph[:, :], reads=[phn], writes=["sbf3"], tag=f"C{t}Ue")
                yield
                for h in range(2):
                    ph, phn = (pB1, pC0)[h], ("pB1", "pC0")[h]
                    mm(ph[:, :], q_bf[:, :], w0t_bf[:, h * 512:(h + 1) * 512],
                       True, False, reads=[q_n, "w0t_bf"], writes=[phn],
                       tag=f"C{t}YQ")
                    mm(ph[:, :], gt_bf[:, :], sbf[3][:, h * 512:(h + 1) * 512],
                       False, True, reads=[gt_n, "sbf3"], writes=[phn],
                       tag=f"C{t}YG")
                    vop("vector", "tensor_add", res[t][:, h * 512:(h + 1) * 512],
                        res[t][:, h * 512:(h + 1) * 512], ph[:, :],
                        reads=[rn, phn], writes=[rn], tag=f"C{t}ry")
                    yield
                tr(pT2[:, 0:64], k_bf[:, :], idbf[0:64, 0:64],
                   reads=[k_n, "idbf_s"], writes=["pT2"], tag=f"C{t}ktr")
                vop("scalar", "copy", ktm_bf[:, :], pT2[0:128, 0:64],
                    reads=["pT2"], writes=["ktm_bf"], tag=f"C{t}kte")
                yield
                for h in range(2):
                    ph, phn = (pC1, pB1)[h], ("pC1", "pB1")[h]
                    mm(ph[0:64, :], ktm_bf[:, :], sbf[3][:, h * 512:(h + 1) * 512],
                       True, True, reads=["ktm_bf", "sbf3"], writes=[phn],
                       tag=f"C{t}KtU")
                    vop("vector", "scalar_tensor_tensor",
                        w0t_bf[:, h * 512:(h + 1) * 512], ph[0:64, :], -1.0,
                        w0t_f[0:64, h * 512:(h + 1) * 512], ALU.mult, ALU.add,
                        reads=[phn, "w0t_f"], writes=["w0t_bf"], tag=f"C{t}Wbf")
                    vop("vector", "scalar_tensor_tensor",
                        w0t_f[0:64, h * 512:(h + 1) * 512], ph[0:64, :], -1.0,
                        w0t_f[0:64, h * 512:(h + 1) * 512], ALU.mult, ALU.add,
                        reads=[phn, "w0t_f"], writes=["w0t_f"], tag=f"C{t}Wf")
                    yield
                stats_tile("D", t)

            stage1(0)
            for _ in tbuild(0):
                pass
            for t in range(NT):
                gens = []
                if t + 1 < NT:
                    stage1(t + 1)
                    gens.append(tbuild(t + 1))
                gens.append(stage2(t))
                while gens:
                    for g in list(gens):
                        try:
                            next(g)
                        except StopIteration:
                            gens.remove(g)

        def phase_D(s):
            load_w1024("wA0", wA[0], wr2_d, "wr2",
                       extra=tuple(f"fm{t}" for t in range(8)))
            load_w1024("wA1", wA[1], wk2_d, "wk2",
                       extra=tuple(f"fm{t}" for t in range(8, 16)))
            load_w1024("wA2", wA[2], wv2_d, "wv2")
            if s == 0:
                dma("gpq", lntd[:, :], lntd_d[:, :], writes=["p2t"],
                    tag="lntd_rd")
            stats("D", inner=False)

            def x4c(j):
                if j < 6:
                    return wB[j // 2], f"wB{j // 2}", (j % 2) * T_SEQ
                return wA[3], "wA3", (j - 6) * T_SEQ

            def slots(t):
                if t % 2 == 0:
                    return (sbf[0], "sbf0", sbf[1], "sbf1")
                return (sbf[2], "sbf2", sbf[3], "sbf3")

            def stageD1(t):
                rn = f"res{t}"
                t0 = t * 128
                xr, xr_n, xk, xk_n = slots(t)
                vop("scalar", "activation", xr[:, :], res[t][:, :], AF.Identity,
                    bias=stnm1[:, t:t + 1], scale=strs[:, t:t + 1],
                    reads=[rn, "strs", "stnm1"], writes=[xr_n], tag=f"D{t}xo")
                for c in range(NC_CH):
                    tr(pT[:, c * 128:(c + 1) * 128], xr[:, c * 128:(c + 1) * 128],
                       idbf[:, :], reads=[xr_n, "idbf_s"], writes=["pT"],
                       tag=f"D{t}tr")
                for c in range(NC_CH):
                    dst, dn, db = x4c(c)
                    vop("scalar", "copy", dst[:, db + t0:db + t0 + 128],
                        pT[:, c * 128:(c + 1) * 128], reads=["pT"], writes=[dn],
                        tag=f"D{t}x4")
                for c in range(NC_CH):
                    src_, sn, db = x4c(c)
                    xo = src_[:, db + t0:db + t0 + 128]
                    if t == 0:
                        xop = src_[:, db + t0:db + t0 + 127]
                        vop("vector", "tensor_sub",
                            xk[:, c * 128 + 1:(c + 1) * 128],
                            src_[:, db + t0 + 1:db + t0 + 128], xop,
                            reads=[sn], writes=[xk_n], tag=f"D{t}d{c}")
                        vop("vector", "tensor_copy", xk[:, c * 128:c * 128 + 1],
                            xo[:, 0:1], reads=[sn], writes=[xk_n], tag=f"D{t}d0{c}")
                        vop("vector", "scalar_tensor_tensor",
                            xr[:, c * 128 + 1:(c + 1) * 128],
                            xk[:, c * 128 + 1:(c + 1) * 128], murc[:, c:c + 1],
                            xop, ALU.mult, ALU.add,
                            reads=[xk_n, "cols", sn], writes=[xr_n],
                            tag=f"D{t}mr{c}")
                        vop("vector", "tensor_scalar", xr[:, c * 128:c * 128 + 1],
                            xk[:, c * 128:c * 128 + 1], murc[:, c:c + 1], None,
                            ALU.mult, reads=[xk_n, "cols"], writes=[xr_n],
                            tag=f"D{t}mr0{c}")
                        vop("vector", "scalar_tensor_tensor",
                            xk[:, c * 128 + 1:(c + 1) * 128],
                            xk[:, c * 128 + 1:(c + 1) * 128], mukc[:, c:c + 1],
                            xop, ALU.mult, ALU.add,
                            reads=[xk_n, "cols", sn], writes=[xk_n],
                            tag=f"D{t}mk{c}")
                        vop("vector", "tensor_scalar", xk[:, c * 128:c * 128 + 1],
                            xk[:, c * 128:c * 128 + 1], mukc[:, c:c + 1], None,
                            ALU.mult, reads=[xk_n, "cols"], writes=[xk_n],
                            tag=f"D{t}mk0{c}")
                    else:
                        xop = src_[:, db + t0 - 1:db + t0 + 127]
                        vop("vector", "tensor_sub", xk[:, c * 128:(c + 1) * 128],
                            xo, xop, reads=[sn], writes=[xk_n], tag=f"D{t}d{c}")
                        vop("vector", "scalar_tensor_tensor",
                            xr[:, c * 128:(c + 1) * 128],
                            xk[:, c * 128:(c + 1) * 128], murc[:, c:c + 1],
                            xop, ALU.mult, ALU.add,
                            reads=[xk_n, "cols", sn], writes=[xr_n],
                            tag=f"D{t}mr{c}")
                        vop("vector", "scalar_tensor_tensor",
                            xk[:, c * 128:(c + 1) * 128],
                            xk[:, c * 128:(c + 1) * 128], mukc[:, c:c + 1],
                            xop, ALU.mult, ALU.add,
                            reads=[xk_n, "cols", sn], writes=[xk_n],
                            tag=f"D{t}mk{c}")

            def stageD2(t):
                rn = f"res{t}"
                t0 = t * 128
                xr, xr_n, xk, xk_n = slots(t)
                proj1024(xr, xr_n, wA[0], "wA0", (pA0, pA1), ("pA0", "pA1"),
                         tag=f"D{t}r")
                for h in range(2):
                    vop("scalar", "activation", sf[0][:, h * 512:(h + 1) * 512],
                        (pA0, pA1)[h][:, :], AF.Sigmoid,
                        reads=[("pA0", "pA1")[h]], writes=["sf0"], tag=f"D{t}sig")
                proj1024(xk, xk_n, wA[1], "wA1", (pB0, pB1), ("pB0", "pB1"),
                         tag=f"D{t}k")
                for h in range(2):
                    vop("scalar", "activation", sf[1][:, h * 512:(h + 1) * 512],
                        (pB0, pB1)[h][:, :], AF.Relu,
                        reads=[("pB0", "pB1")[h]], writes=["sf1"], tag=f"D{t}relu")
                vop("vector", "tensor_mul", sbf[4][:, :], sf[1][:, :], sf[1][:, :],
                    reads=["sf1"], writes=["sbf4"], tag=f"D{t}k2")
                for c in range(NC_CH):
                    tr(pT2[:, c * 128:(c + 1) * 128],
                       sbf[4][:, c * 128:(c + 1) * 128], idbf[:, :],
                       reads=["sbf4", "idbf_s"], writes=["pT2"], tag=f"D{t}ktr")
                vop("scalar", "copy", sbf[5][:, :], pT2[:, :],
                    reads=["pT2"], writes=["sbf5"], tag=f"D{t}kfe")
                proj1024(sbf[5], "sbf5", wA[2], "wA2", (pC0, pC1), ("pC0", "pC1"),
                         tag=f"D{t}v")
                for h in range(2):
                    vop("vector", "tensor_mul", sf[1][:, h * 512:(h + 1) * 512],
                        (pC0, pC1)[h][:, :], sf[0][:, h * 512:(h + 1) * 512],
                        reads=[("pC0", "pC1")[h], "sf0"], writes=["sf1"],
                        tag=f"D{t}rv")
                vop("vector", "tensor_add", res[t][:, :], res[t][:, :], sf[1][:, :],
                    reads=[rn, "sf1"], writes=[rn], tag=f"D{t}res")
                dma("syncq", y_d[s * T_SEQ + t0:s * T_SEQ + t0 + 128, :],
                    res[t][:, :], reads=[rn], tag=f"D{t}out")
                if s == 0:
                    dma("syncq", res[t][:, :],
                        x_d[T_SEQ + t0:T_SEQ + t0 + 128, :],
                        writes=[rn], tag=f"x1_{t}")
                    if t >= 2:
                        stats_tile("A", t - 2)

            stageD1(0)
            for t in range(NT):
                if t + 1 < NT:
                    stageD1(t + 1)
                stageD2(t)
            if s == 0:
                stats_tile("A", NT - 2)
                stats_tile("A", NT - 1)

        for s in range(2):
            if s == 1:
                load_w1024("wA0", wA[0], wk0_d, "wk0b",
                           extra=tuple(f"fm{t}" for t in range(8)))
                load_w1024("wA1", wA[1], wv0_d, "wv0b",
                           extra=tuple(f"fm{t}" for t in range(8, 16)))
                load_w1024("wA2", wA[2], wr0_d, "wr0b")
                load_w1024("wA3", wA[3], wo0_d, "wo0b")
            if "A" in PHASES:
                phase_A(s)
            if "B" in PHASES:
                phase_B(s)
            if "C" in PHASES:
                phase_C(s)
            if "D" in PHASES:
                phase_D(s)
            if "D" not in PHASES and DEBUG_DUMP in (None, "noexp", "nobias", "e0"):
                for t in range(NT):
                    dma("syncq", y_d[s * T_SEQ + t * 128:s * T_SEQ + (t + 1) * 128, :],
                        res[t][:, :], reads=[f"res{t}"], tag=f"dbgout{t}")

        p.finalize(nc)
    return nc


def _blockmask(f, upper):
    m = np.zeros((128, 128), f)
    for i in range(8):
        b = np.triu(np.ones((16, 16), f), 1) if upper else \
            np.tril(np.ones((16, 16), f), -1)
        m[i * 16:(i + 1) * 16, i * 16:(i + 1) * 16] = b
    return m


def _blockstrict(f):
    m = np.zeros((128, 128), f)
    for i in range(8):
        m[i * 16:, :i * 16][:16] = 1.0 if False else m[i * 16:, :i * 16][:16]
    for bi in range(8):
        for bj in range(bi):
            m[bi * 16:(bi + 1) * 16, bj * 16:(bj + 1) * 16] = 1.0
    return m


def _prep_inputs(inputs):
    f = np.float32
    c = np.ascontiguousarray

    def T(a):
        return c(np.asarray(a).T.astype(f, copy=False))

    vecs = np.zeros((19, D), f)
    vecs[0] = inputs["L0_attn_ln_gamma"]
    vecs[1] = inputs["L0_attn_ln_beta"]
    vecs[2] = inputs["L0_attn_time_decay"]
    vecs[3] = inputs["L0_attn_time_first"]
    vecs[4, :NE] = inputs["L0_ffn_router_b"]
    vecs[5:13] = inputs["L0_ffn_expert_b"]
    vecs[13, :128] = np.concatenate([inputs["L1_attn_bk"], inputs["L1_attn_bq"]])
    vecs[14] = inputs["L1_attn_bv"]
    vecs[15] = inputs["L1_attn_gamma"]
    vecs[16] = inputs["L1_attn_beta"]
    vecs[17] = inputs["L1_ffn_mu_r"]
    vecs[18] = inputs["L1_ffn_mu_k"]

    shared = dict(
        wk0t=T(inputs["L0_attn_Wk"]),
        wv0t=T(inputs["L0_attn_Wv"]),
        wr0t=T(inputs["L0_attn_Wr"]),
        wo0t=T(inputs["L0_attn_Wo"]),
        rt0=T(inputs["L0_ffn_router_w"]),
        et0=c(np.asarray(inputs["L0_ffn_expert_w"]).transpose(0, 2, 1)
              .reshape(NE * D, D).astype(f, copy=False)),
        wkq1t=c(np.concatenate([np.asarray(inputs["L1_attn_Wk"]).T,
                                np.asarray(inputs["L1_attn_Wq"]).T],
                               axis=1).astype(f, copy=False)),
        wv1t=T(inputs["L1_attn_Wv"]),
        wr2t=T(inputs["L1_ffn_W_r"]),
        wk2t=T(inputs["L1_ffn_W_k"]),
        wv2t=T(inputs["L1_ffn_W_v"]),
        vecs=vecs,
        idbf=np.eye(128, dtype=BF),
        usbf=np.triu(np.ones((128, 128), BF), 1),
        onescol=np.ones((128, 1), BF),
        onesrow=np.ones((65, 128), BF),
        onesrow_f=np.ones((1, 128), f),
        iota1=np.arange(1, 129, dtype=f).reshape(1, 128),
        mubd=_blockmask(BF, upper=True),
        mlbd=_blockmask(BF, upper=False),
        mlbs=_blockstrict(BF),
        mus_f=np.triu(np.ones((128, 128), BF), 1),
        if32=np.eye(128, dtype=f),
        one11_f=np.ones((1, 1), f),
    )
    x = np.asarray(inputs["L0_attn_x"]).astype(f, copy=False)
    in_maps = []
    for i in range(N_CORES):
        m = dict(shared)
        m["x"] = c(x[2 * i:2 * i + 2].reshape(2 * T_SEQ, D))
        in_maps.append(m)
    return in_maps


def kernel(**inputs):
    in_maps = _prep_inputs(inputs)
    nc = build_program()
    res = run_bass_kernel_spmd(nc, in_maps, core_ids=list(range(N_CORES)))
    out = np.empty((16, T_SEQ, D), np.float32)
    for i in range(N_CORES):
        out[2 * i:2 * i + 2] = res.results[i]["y"].reshape(2, T_SEQ, D)
    return out
